# version 09
# baseline (speedup 1.0000x reference)
import numpy as np
import ml_dtypes
import concourse.bass as bass
import concourse.mybir as mybir
from concourse.bass_utils import run_bass_kernel_spmd
from concourse.bass import IndirectOffsetOnAxis
from contextlib import ExitStack

F32 = mybir.dt.float32
BF16 = mybir.dt.bfloat16
I32 = mybir.dt.int32
AF = mybir.ActivationFunctionType
ALU = mybir.AluOpType
AX = mybir.AxisListType

D = 1024
DI = 2048
DT = 64
KCONV = 4
H3 = 3072
DM = 4096
NE = 8
QSCALE = 1.0 / 127.0
EPS = 1e-5
NEG = -1e9


def cfg_full():
    return {"B": 2, "T": 2048}


class Sem:

    def __init__(self, nc, name):
        self.h = nc.alloc_semaphore(name=name)
        self.n = 0

    def inc(self, instr, k=1):
        instr.then_inc(self.h, k)
        self.n += k
        return self.n


class P:
    pass


def build_program(cfg, dbg=False, nstages=6):
    B, T = cfg["B"], cfg["T"]
    NTOK = B * T
    NTT = NTOK // 128
    assert T % 128 == 0

    nc = bass.Bass("TRN2", target_bir_lowering=False)
    p = P()
    p.nc = nc
    p.B, p.T, p.NTOK, p.NTT = B, T, NTOK, NTT

    ap = lambda t: t.ap()
    p.x_in = ap(nc.dram_tensor("x0", [NTOK, D], F32, kind="ExternalInput"))
    p.w_inproj = ap(nc.dram_tensor("L0_attn_in_proj_w", [2 * DI, D], F32, kind="ExternalInput"))
    p.w_conv = ap(nc.dram_tensor("L0_attn_conv_w", [KCONV, DI], F32, kind="ExternalInput"))
    p.b_conv = ap(nc.dram_tensor("L0_attn_conv_b", [DI], F32, kind="ExternalInput"))
    p.w_xproj = ap(nc.dram_tensor("L0_attn_x_proj_w", [DT, DI], F32, kind="ExternalInput"))
    p.w_dtproj = ap(nc.dram_tensor("L0_attn_dt_proj_w", [DI, DT], F32, kind="ExternalInput"))
    p.b_dtproj = ap(nc.dram_tensor("L0_attn_dt_proj_b", [DI], F32, kind="ExternalInput"))
    p.a_log = ap(nc.dram_tensor("L0_attn_A_log", [DI], F32, kind="ExternalInput"))
    p.d_vec = ap(nc.dram_tensor("L0_attn_D", [DI], F32, kind="ExternalInput"))
    p.w_outproj = ap(nc.dram_tensor("L0_attn_out_proj_w", [D, DI], F32, kind="ExternalInput"))
    p.w_gate_i = ap(nc.dram_tensor("L0_ffn_w_gate", [H3, D], I32, kind="ExternalInput"))
    p.w_up_i = ap(nc.dram_tensor("L0_ffn_w_up", [H3, D], I32, kind="ExternalInput"))
    p.w_down_i = ap(nc.dram_tensor("L0_ffn_w_down", [D, H3], I32, kind="ExternalInput"))
    p.w_qkv = ap(nc.dram_tensor("L1_attn_w_qkv", [D, 3 * D], F32, kind="ExternalInput"))
    p.w_wo = ap(nc.dram_tensor("L1_attn_w_out", [D, D], F32, kind="ExternalInput"))
    p.w_moe_gate = ap(nc.dram_tensor("L1_ffn_W_gate", [D, NE], F32, kind="ExternalInput"))
    p.w_moe_in = ap(nc.dram_tensor("L1_ffn_W_in", [NE, D, DM], F32, kind="ExternalInput"))
    p.b_moe_in = ap(nc.dram_tensor("L1_ffn_b_in", [NE, DM], F32, kind="ExternalInput"))
    p.w_moe_out = ap(nc.dram_tensor("L1_ffn_W_out", [NE, DM, D], F32, kind="ExternalInput"))
    p.b_moe_out = ap(nc.dram_tensor("L1_ffn_b_out", [NE, D], F32, kind="ExternalInput"))
    p.y_out = ap(nc.dram_tensor("y", [NTOK, D], F32, kind="ExternalOutput"))

    if dbg:
        dr = lambda name, shape, dt=BF16: nc.dram_tensor(name, shape, dt, kind="ExternalOutput").ap()
    else:
        dr = lambda name, shape, dt=BF16: nc.dram_tensor(name, shape, dt).ap()
    p.resid = [dr("resid_a", [NTOK, D], F32), dr("resid_b", [NTOK, D], F32)]
    p.l0_gu = dr("l0_gu", [H3 // 128, 128, 2 * D])
    p.l0_dT = dr("l0_dT", [H3, D])
    p.inprojT = dr("inprojT", [D, 2 * DI])
    p.outprojT = dr("outprojT", [DI, D])
    p.xprojT = dr("xprojT", [DI, DT])
    p.dtprojT = dr("dtprojT", [DT, DI])
    p.qkv_bf = dr("qkv_bf", [D, 3 * D])
    p.wo_bf = dr("wo_bf", [D, D])
    p.wgate_bf = dr("wgate_bf", [D, NE])
    p.win_bf = dr("win_bf", [NE, D, DM])
    p.wout_bf = dr("wout_bf", [NE, DM, D])
    p.xz_scr = dr("xz_scr", [B, 2 * DI, T], F32)
    p.xiact_f = dr("xiact_f", [B, DI, T], F32)
    p.xiact_b = dr("xiact_b", [B, DI, T])
    p.qT_scr = dr("qT_scr", [B, D, T])
    p.kT_scr = dr("kT_scr", [B, D, T])
    p.xlnT_scr = dr("xlnT_scr", [D, NTOK])
    p.xtm_scr = dr("xtm_scr", [NTOK + 1, 1032])
    p.moe_out = dr("moe_out", [NTOK + 1, D], F32)
    NRC_ = max(1, NTOK // 1024)
    p.idx_scr = dr("idx_scr", [NRC_, 4224], I32)
    p.idx_fill = dr("idx_fill", [4224], I32)
    if dbg:
        p.dbg_call = [dr(f"dbg_call{i}", [128, NTT * NE], F32) for i in range(2)]
        p.dbg_xln = dr("dbg_xln", [128, 8 * 512])
        p.dbg_h = dr("dbg_h", [128, 24 * 512])
        p.dbg_stat = dr("dbg_stat", [128, 20], F32)
        p.dbg_xch = dr("dbg_xch", [128, 4 * 1024], F32)
    p.dbg = dbg

    p.ts = Sem(nc, "ts")
    p.vs = Sem(nc, "vs")
    p.ss = Sem(nc, "ss")
    p.gs = Sem(nc, "gs")
    p.ld = [Sem(nc, f"ld{i}") for i in range(8)]
    p.st = [Sem(nc, f"st{i}") for i in range(4)]
    p.gld = [Sem(nc, f"gld{i}") for i in range(2)]
    p.gcast = Sem(nc, "gcast")
    p.gga = [Sem(nc, f"gga{i}") for i in range(2)]
    p.gidx = Sem(nc, "gidx")
    p.gc2 = Sem(nc, "gc2")
    p.gc3 = Sem(nc, "gc3")
    p.gsc = [Sem(nc, f"gsc{i}") for i in range(2)]

    p.stack = ExitStack()
    en = p.stack.enter_context
    p.ident = en(nc.sbuf_tensor("ident", [128, 128], BF16))
    p.masks = en(nc.sbuf_tensor("masks", [128, 4 * 512], BF16))
    p.ones_bf = en(nc.sbuf_tensor("ones_bf", [128, 64], BF16))
    p.ones_f1 = en(nc.sbuf_tensor("ones_f1", [1, 8], F32))
    p.bcols = en(nc.sbuf_tensor("bcols", [128, 128], F32))
    p.bin_cols = en(nc.sbuf_tensor("bin_cols", [128, 256], F32))
    p.bout_bf = en(nc.sbuf_tensor("bout_bf", [1, NE * D], BF16))
    p.wg_hi = en(nc.sbuf_tensor("wg_hi", [128, (D // 128) * NE], BF16))
    p.wg_lo = en(nc.sbuf_tensor("wg_lo", [128, (D // 128) * NE], BF16))
    p.ones_row = en(nc.sbuf_tensor("ones_row", [1, 128], BF16))
    p.tri = en(nc.sbuf_tensor("tri", [128, 128], BF16))
    p.ones_f32r = en(nc.sbuf_tensor("ones_f32r", [1, 128], F32))
    p.ebase = en(nc.sbuf_tensor("ebase", [1, NE], F32))
    p.tokid = en(nc.sbuf_tensor("tokid", [128, 64], I32))

    p.mb_dtmm = {}
    p.mo_ga_tot = {}
    p.mo_ini_tot = {}
    p.mo_idx_tot = {}
    p.mo_sc_tot = {}
    p.at_qst = {}
    p.at_khld = {}
    p.mb_strip_ld = {}
    p.mb_kt_done = {}
    build_prepass(p)
    stages = [
        lambda: build_ffn0(p, 0, 1, tag=0),
        lambda: build_mamba(p, 1, 0),
        lambda: build_ffn0(p, 0, 1, tag=1),
        lambda: build_moe(p, 1, 0, tag=0),
        lambda: build_attn(p, 0, 1),
        lambda: build_moe(p, 1, 0, final=True, tag=1),
    ]
    for f in stages[:nstages]:
        f()
    build_drain(p)
    p.stack.close()
    return nc


def build_drain(p):
    nc = p.nc
    with nc.Block() as block:
        @block.sync
        def _(sync):
            for s in p.ld + p.st + p.gld + p.gga + p.gsc + [p.gcast, p.gidx,
                                                            p.gc2, p.gc3]:
                sync.wait_ge(s.h, s.n)


def build_prepass(p):
    nc = p.nc
    ld, st = p.ld, p.st
    ts, vs, ss, gs = p.ts, p.vs, p.ss, p.gs
    gld = p.gld
    gcast = p.gcast

    NHT = H3 // 128
    ND = D // 128
    NDI = DI // 128

    with ExitStack() as ctx:
        en = ctx.enter_context
        wrow = [en(nc.sbuf_tensor(f"pp_wrow{i}", [128, H3], BF16)) for i in range(2)]
        strip = [en(nc.sbuf_tensor(f"pp_strip{i}", [128, 2 * D], BF16)) for i in range(2)]
        dstrips = en(nc.sbuf_tensor("pp_dstrips", [128, (NHT // 2) * D], BF16))
        istrips = en(nc.sbuf_tensor("pp_istrips", [128, ND * DI], BF16))
        ostrips = en(nc.sbuf_tensor("pp_ostrips", [128, NDI * D], BF16))
        xstrip = en(nc.sbuf_tensor("pp_xstrip", [128, NDI * DT], BF16))
        dtstrip = en(nc.sbuf_tensor("pp_dtstrip", [64, DI], BF16))
        rowbuf = en(nc.sbuf_tensor("pp_rowbuf", [1, DM // 2], F32))
        bout_rows = en(nc.sbuf_tensor("pp_bout_rows", [1, NE * D], F32))
        ptr = [en(nc.psum_tensor(f"pp_ptr{i}", [128, 128], BF16)) for i in range(2)]
        pb = en(nc.psum_tensor("pp_pb", [128, 512], F32))

        with nc.Block() as block:
            @block.gpsimd
            def _(g):
                g.dma_start(p.wgate_bf, p.w_moe_gate).then_inc(gcast.h, 16); gcast.n += 16
                p.cast_gate_tot = gcast.n
                g.dma_start(p.win_bf, p.w_moe_in).then_inc(p.gc2.h, 16); p.gc2.n += 16
                g.dma_start(p.wout_bf, p.w_moe_out).then_inc(p.gc2.h, 16); p.gc2.n += 16
                p.cast_moe_tot = p.gc2.n
                g.dma_start(p.qkv_bf, p.w_qkv).then_inc(p.gc3.h, 16); p.gc3.n += 16
                g.dma_start(p.wo_bf, p.w_wo).then_inc(p.gc3.h, 16); p.gc3.n += 16
                p.cast_attn_tot = p.gc3.n
                g.memset(p.ident[:], 0.0).then_inc(gs.h, 1); gs.n += 1
                g.memset(p.ones_bf[:], 1.0).then_inc(gs.h, 1); gs.n += 1
                g.memset(p.ones_row[:], 1.0).then_inc(gs.h, 1); gs.n += 1
                g.memset(p.ones_f1[:], 1.0).then_inc(gs.h, 1); gs.n += 1
                g.memset(p.masks[:], 0.0).then_inc(gs.h, 1); gs.n += 1
                g.memset(p.tri[:], 1.0).then_inc(gs.h, 1); gs.n += 1
                g.memset(p.ones_f32r[:], 1.0).then_inc(gs.h, 1); gs.n += 1
                g.iota(p.ebase[0:1, :], [[min(1024, p.NTOK) // 2, NE]], base=-1,
                       channel_multiplier=0,
                       allow_small_or_imprecise_dtypes=True).then_inc(gs.h, 1); gs.n += 1
                g.iota(p.tokid[:, 0:p.NTT], [[128, p.NTT]], base=0,
                       channel_multiplier=1).then_inc(gs.h, 1); gs.n += 1
                g.wait_ge(gs.h, gs.n)
                g.affine_select(out=p.ident[:], in_=p.ident[:], compare_op=ALU.not_equal,
                                fill=1.0, base=0, pattern=[[-1, 128]], channel_multiplier=1).then_inc(gs.h, 1)
                gs.n += 1
                g.affine_select(out=p.tri[:], in_=p.tri[:], compare_op=ALU.is_ge, fill=0.0,
                                base=0, pattern=[[1, 128]], channel_multiplier=-1).then_inc(gs.h, 1)
                gs.n += 1
                for mp in range(4):
                    g.affine_select(out=p.masks[:, mp * 512:(mp + 1) * 512],
                                    in_=p.masks[:, mp * 512:(mp + 1) * 512],
                                    compare_op=ALU.is_ge, fill=NEG, base=-mp * 128,
                                    pattern=[[1, 512]], channel_multiplier=-1).then_inc(gs.h, 1)
                    gs.n += 1

            @block.sync
            def _(sync):
                sync.dma_start(p.resid[0], p.x_in).then_inc(st[0].h, 16); st[0].n += 16
                sync.dma_start(bout_rows[0:1, :],
                               p.b_moe_out.rearrange("e d -> (e d)")[None, :]).then_inc(ld[1].h, 16)
                ld[1].n += 16

        rows = ([("bin", e * 2 + hh) for e in range(NE) for hh in range(2)]
                + [("vec", i) for i in range(4)] + [("cw", k) for k in range(KCONV)])
        vec_src = [p.b_conv, p.b_dtproj, p.a_log, p.d_vec]
        ts0, ld0 = ts.n, ld[0].n
        with nc.Block() as block:
            @block.sync
            def _(sync):
                for i, (kind, idx) in enumerate(rows):
                    if i >= 1:
                        sync.wait_ge(ts.h, ts0 + i)
                    if kind == "bin":
                        e2, hh = divmod(idx, 2)
                        sync.dma_start(rowbuf[0:1, 0:DM // 2],
                                       p.b_moe_in[e2, hh * (DM // 2):(hh + 1) * (DM // 2)][None, :]
                                       ).then_inc(ld[0].h, 16)
                    elif kind == "vec":
                        sync.dma_start(rowbuf[0:1, 0:DI], vec_src[idx][None, :]).then_inc(ld[0].h, 16)
                    else:
                        sync.dma_start(rowbuf[0:1, 0:DI], p.w_conv[idx, :][None, :]).then_inc(ld[0].h, 16)
                    ld[0].n += 16

            @block.tensor
            def _(t):
                for i, (kind, idx) in enumerate(rows):
                    t.wait_ge(ld[0].h, ld0 + 16 * (i + 1))
                    ncols = 16
                    for c in range(ncols):
                        if kind == "bin":
                            col = idx * 16 + c
                        elif kind == "vec":
                            col = 256 + idx * 16 + c
                        else:
                            col = 320 + idx * 16 + c
                        mm = t.matmul(pb[:, col:col + 1], rowbuf[0:1, c * 128:(c + 1) * 128],
                                      p.ones_f1[0:1, 0:1], start=True, stop=True)
                    mm.then_inc(ts.h, 1); ts.n += 1

            @block.vector
            def _(v):
                v.wait_ge(ts.h, ts.n)
                v.tensor_copy(p.bin_cols[:], pb[:, 0:256]).then_inc(vs.h, 1); vs.n += 1
                v.tensor_copy(p.bcols[:], pb[:, 256:384]).then_inc(vs.h, 1); vs.n += 1
                v.wait_ge(ss.h, ss.n + 1)
                v.tensor_scalar(p.bcols[:, 32:48], p.bcols[:, 32:48], 10.0, 1e-6,
                                op0=ALU.min, op1=ALU.max).then_inc(vs.h, 1); vs.n += 1
                v.wait_ge(vs.h, vs.n)
                v.tensor_scalar(p.bcols[:, 32:48], p.bcols[:, 32:48], -1.0, None,
                                op0=ALU.mult).then_inc(vs.h, 1); vs.n += 1

            vsnap = vs.n - 2

            @block.scalar
            def _(sc):
                sc.wait_ge(vs.h, vsnap)
                sc.activation(p.bcols[:, 32:48], p.bcols[:, 32:48], AF.Exp).then_inc(ss.h, 1); ss.n += 1

        with ExitStack() as ctx2:
            wgf = ctx2.enter_context(nc.sbuf_tensor("pp_wgf", [128, (D // 128) * NE], F32))
            fillb = ctx2.enter_context(nc.sbuf_tensor("pp_fillb", [128, 33], I32))
            zrow = ctx2.enter_context(nc.sbuf_tensor("pp_zrow", [1, 1032], BF16))
            with nc.Block() as block:
                @block.gpsimd
                def _(g):
                    g.iota(fillb[:, :], [[0, 33]], base=p.NTOK,
                           channel_multiplier=0).then_inc(gs.h, 1); gs.n += 1
                    g.memset(zrow[0:1, :], 0.0).then_inc(gs.h, 1); gs.n += 1
                    g.wait_ge(gs.h, gs.n)
                    g.dma_start(p.idx_fill.rearrange("(q cc) -> q cc", cc=33),
                                fillb[:, :]).then_inc(p.gld[0].h, 16)
                    p.gld[0].n += 16
                    g.dma_start(p.xtm_scr[p.NTOK:p.NTOK + 1, :], zrow[0:1, :]).then_inc(p.gld[0].h, 16)
                    p.gld[0].n += 16
                @block.sync
                def _(sync):
                    sync.dma_start(wgf[:, :].rearrange("q (dt e) -> q dt e", dt=D // 128),
                                   p.w_moe_gate.rearrange("(dt q) e -> q dt e", q=128)
                                   ).then_inc(ld[2].h, 16)
                    ld[2].n += 16
                    sync.wait_ge(gcast.h, p.cast_gate_tot)
                    sync.dma_start(p.wg_hi[:, :].rearrange("q (dt e) -> q dt e", dt=D // 128),
                                   p.wgate_bf.rearrange("(dt q) e -> q dt e", q=128)
                                   ).then_inc(ld[3].h, 16)
                    ld[3].n += 16

                @block.vector
                def _(v):
                    sop(v, vs, lambda: v.tensor_tensor(p.wg_lo[:, :], wgf[:, :], p.wg_hi[:, :],
                                                       op=ALU.subtract),
                        waits=[(ld[2], ld[2].n), (ld[3], ld[3].n)])
                    sop(v, vs, lambda: v.tensor_copy(p.bout_bf[0:1, :], bout_rows[0:1, :]),
                        waits=[(ld[1], ld[1].n)])
            drain_dmas(p)

        units = []
        for ht in range(NHT):
            units.append(("gu", ht, 2, 2 * ND))
        for half in range(2):
            for dt_i in range(ND):
                units.append(("wd", half * ND + dt_i, 1, NHT // 2))
        for half in range(2):
            for ft in range(2 * ND):
                units.append(("ip", half * 2 * ND + ft, 1, ND))
        for dt_i in range(ND):
            units.append(("op", dt_i, 1, NDI))
        units.append(("xp", 0, 1, NDI))
        for kt in range(NDI):
            units.append(("dp", kt, 1, 1))

        ts_base, vs_base = ts.n, vs.n
        gld_tot = [gld[0].n, gld[1].n]
        unit_gld = []
        unit_tcum = []
        tc = 0
        for i, (kind, idx, nl, nt) in enumerate(units):
            slot = i % 2
            gld_tot[slot] += 16 * nl
            unit_gld.append((slot, gld_tot[slot]))
            tc += nt
            unit_tcum.append(tc)
        NIP = 2 * ND

        def stores_after(i):
            kind, idx, _, _ = units[i]
            out = []
            if kind == "gu":
                out.append((p.l0_gu[idx, :, :], strip[idx % 2][:, :]))
            elif kind == "wd" and idx % ND == ND - 1:
                half = idx // ND
                out.append((p.l0_dT[half * (NHT // 2) * 128:(half + 1) * (NHT // 2) * 128, :]
                            .rearrange("(ht q) d -> q ht d", q=128),
                            dstrips[:, :].rearrange("q (ht d) -> q ht d", ht=NHT // 2)))
            elif kind == "ip" and idx == NIP - 1:
                out.append((p.inprojT[:, 0:DI].rearrange("(dt q) f -> q dt f", q=128),
                            istrips[:, :].rearrange("q (dt f) -> q dt f", dt=ND)))
            elif kind == "ip" and idx == 2 * NIP - 1:
                out.append((p.inprojT[:, DI:2 * DI].rearrange("(dt q) f -> q dt f", q=128),
                            istrips[:, :].rearrange("q (dt f) -> q dt f", dt=ND)))
            elif kind == "op" and idx == ND - 1:
                out.append((p.outprojT.rearrange("(kt q) d -> q kt d", q=128),
                            ostrips[:, :].rearrange("q (kt d) -> q kt d", kt=NDI)))
            elif kind == "xp":
                out.append((p.xprojT.rearrange("(kt q) r -> q kt r", q=128),
                            xstrip[:, :].rearrange("q (kt r) -> q kt r", kt=NDI)))
            elif kind == "dp" and idx == NDI - 1:
                out.append((p.dtprojT[:, :], dtstrip[0:64, :]))
            return out

        st_plan = {}
        st_tot = {j: p.st[j].n for j in range(4)}
        sidx = 0
        for i in range(len(units)):
            for _ in stores_after(i):
                sl = sidx % 4
                st_tot[sl] += 16
                st_plan.setdefault(i, []).append((sl, st_tot[sl]))
                sidx += 1

        with nc.Block() as block:
            @block.gpsimd
            def _(g):
                for i, (kind, idx, nl, nt) in enumerate(units):
                    slot, tot = unit_gld[i]
                    buf = wrow[slot]
                    if i >= 2:
                        g.wait_ge(ts.h, ts_base + unit_tcum[i - 2])
                    if kind == "gu":
                        g.dma_start(buf[:, 0:D], p.w_gate_i[idx * 128:(idx + 1) * 128, :]).then_inc(gld[slot].h, 16)
                        g.dma_start(buf[:, D:2 * D], p.w_up_i[idx * 128:(idx + 1) * 128, :]).then_inc(gld[slot].h, 16)
                    elif kind == "wd":
                        g.dma_start(buf[:, 0:H3], p.w_down_i[(idx % ND) * 128:(idx % ND + 1) * 128, :]).then_inc(gld[slot].h, 16)
                    elif kind == "ip":
                        g.dma_start(buf[:, 0:D], p.w_inproj[idx * 128:(idx + 1) * 128, :]).then_inc(gld[slot].h, 16)
                    elif kind == "op":
                        g.dma_start(buf[:, 0:DI], p.w_outproj[idx * 128:(idx + 1) * 128, :]).then_inc(gld[slot].h, 16)
                    elif kind == "xp":
                        g.dma_start(buf[0:64, 0:DI], p.w_xproj[:, :]).then_inc(gld[slot].h, 16)
                    elif kind == "dp":
                        g.dma_start(buf[:, 0:DT], p.w_dtproj[idx * 128:(idx + 1) * 128, :]).then_inc(gld[slot].h, 16)
                    gld[slot].n += 16

            @block.tensor
            def _(t):
                j = 0

                def tr(in_ap, out_ap):
                    nonlocal j
                    if j >= 2:
                        t.wait_ge(vs.h, vs_base + j - 1)
                    t.transpose(out_ap, in_ap, p.ident[0:in_ap.shape[0], 0:in_ap.shape[0]]).then_inc(ts.h, 1)
                    ts.n += 1
                    j += 1

                for i, (kind, idx, nl, nt) in enumerate(units):
                    slot, tot = unit_gld[i]
                    buf = wrow[slot]
                    t.wait_ge(gld[slot].h, tot)
                    if kind == "gu":
                        for gu in range(2):
                            for dt_i in range(ND):
                                tr(buf[:, gu * D + dt_i * 128: gu * D + (dt_i + 1) * 128], ptr[j % 2][:, :])
                    elif kind == "wd":
                        half, dt2 = divmod(idx, ND)
                        for ht in range(NHT // 2):
                            tr(buf[:, (half * (NHT // 2) + ht) * 128:(half * (NHT // 2) + ht + 1) * 128],
                               ptr[j % 2][:, :])
                    elif kind == "ip":
                        for dt_i in range(ND):
                            tr(buf[:, dt_i * 128:(dt_i + 1) * 128], ptr[j % 2][:, :])
                    elif kind == "op":
                        for kt in range(NDI):
                            tr(buf[:, kt * 128:(kt + 1) * 128], ptr[j % 2][:, :])
                    elif kind == "xp":
                        for kt in range(NDI):
                            tr(buf[0:64, kt * 128:(kt + 1) * 128], ptr[j % 2][0:128, 0:64])
                    elif kind == "dp":
                        tr(buf[:, 0:DT], ptr[j % 2][0:64, 0:128])

            @block.vector
            def _(v):
                j = 0

                def ev(dst_ap, src_ap, i):
                    nonlocal j
                    v.wait_ge(ts.h, ts_base + j + 1)
                    v.tensor_copy(dst_ap, src_ap).then_inc(vs.h, 1); vs.n += 1
                    j += 1

                for i, (kind, idx, nl, nt) in enumerate(units):
                    if kind == "gu":
                        if idx >= 2 and (idx - 2) in st_plan:
                            for sl, tot in st_plan[idx - 2]:
                                v.wait_ge(p.st[sl].h, tot)
                        for gu in range(2):
                            for dt_i in range(ND):
                                ev(strip[idx % 2][:, gu * D + dt_i * 128: gu * D + (dt_i + 1) * 128],
                                   ptr[j % 2][:, :], i)
                    elif kind == "wd":
                        half, dt2 = divmod(idx, ND)
                        if dt2 == 0 and half == 1:
                            for ui, sts in st_plan.items():
                                if units[ui][0] == "wd" and units[ui][1] == ND - 1:
                                    for slx, tot in sts:
                                        v.wait_ge(p.st[slx].h, tot)
                        for ht in range(NHT // 2):
                            ev(dstrips[:, ht * D + dt2 * 128: ht * D + (dt2 + 1) * 128], ptr[j % 2][:, :], i)
                    elif kind == "ip":
                        lidx = idx % NIP
                        if lidx == 0 and idx >= NIP:
                            for ui, sts in st_plan.items():
                                if units[ui][0] == "ip" and units[ui][1] == NIP - 1:
                                    for sl, tot in sts:
                                        v.wait_ge(p.st[sl].h, tot)
                        for dt_i in range(ND):
                            ev(istrips[:, dt_i * DI + lidx * 128: dt_i * DI + (lidx + 1) * 128],
                               ptr[j % 2][:, :], i)
                    elif kind == "op":
                        for kt in range(NDI):
                            ev(ostrips[:, kt * D + idx * 128: kt * D + (idx + 1) * 128], ptr[j % 2][:, :], i)
                    elif kind == "xp":
                        for kt in range(NDI):
                            ev(xstrip[:, kt * DT:(kt + 1) * DT], ptr[j % 2][0:128, 0:64], i)
                    elif kind == "dp":
                        ev(dtstrip[0:64, idx * 128:(idx + 1) * 128], ptr[j % 2][0:64, 0:128], i)

            @block.sync
            def _(sync):
                for i in range(len(units)):
                    if i in st_plan:
                        sync.wait_ge(vs.h, vs_base + unit_tcum[i])
                        for (sl, tot), (dram_ap, sb_ap) in zip(st_plan[i], stores_after(i)):
                            sync.dma_start(dram_ap, sb_ap).then_inc(p.st[sl].h, 16)
                            p.st[sl].n += 16
                sync.wait_ge(gcast.h, gcast.n)
                for s in p.st:
                    sync.wait_ge(s.h, s.n)
                for s in (p.ld[0], p.ld[1]):
                    sync.wait_ge(s.h, s.n)


def drain_dmas(p):
    nc = p.nc
    with nc.Block() as block:
        @block.sync
        def _(sync):
            for s in p.ld + p.st + p.gld + p.gga + p.gsc + [p.gidx]:
                sync.wait_ge(s.h, s.n)


def sop(eng, sem, emit_fn, waits=()):
    for (s, n) in waits:
        eng.wait_ge(s.h, n)
    eng.wait_ge(sem.h, sem.n)
    instr = emit_fn()
    sem.inc(instr)
    return sem.n


def emit_ln(p, block, R, c, guard=False):
    nc = p.nc
    ts, vs, ss = p.ts, p.vs, p.ss
    ntt = R["ntt"]
    ND = D // 128
    Tc = ntt * 128
    s = c % 2
    xch, xlnT, xtm, junk, stat, ptT = (R[k] for k in ("xch", "xlnT", "xtm", "junk", "stat", "ptT"))
    SUM, SQ, MC, VAR, RSTD = (stat[:, i * ntt:(i + 1) * ntt] for i in range(5))
    xsem, xtot = R["xld"](c)
    ts0, vs0, ss0 = ts.n, vs.n, ss.n
    TS_TR0 = ts0
    SS_TR0 = ss0 + ntt + 2
    VS_N0 = vs0 + ntt + 4

    @block.scalar
    def _(sc):
        for tt in range(ntt):
            w = [(xsem, xtot)]
            if guard and tt == 0:
                w.append((vs, vs0))
            sop(sc, ss, lambda tt=tt: sc.activation(
                junk[tt % 2][:, :], xch[s][:, tt * D:(tt + 1) * D], AF.Square,
                accum_out=SQ[:, tt:tt + 1]), waits=w)
        sop(sc, ss, lambda: sc.activation(RSTD[:, :], VAR[:, :], AF.Ln),
            waits=[(vs, vs0 + ntt + 4)])
        sop(sc, ss, lambda: sc.activation(RSTD[:, :], RSTD[:, :], AF.Exp, scale=-0.5))
        for j in range(8 * ntt):
            tt, k = divmod(j, ND)
            sop(sc, ss, lambda j=j, tt=tt, k=k: sc.copy(
                xlnT[s][:, k * Tc + tt * 128: k * Tc + (tt + 1) * 128], ptT[j % 2][:, :]),
                waits=[(ts, ts0 + j + 1)])

    @block.vector
    def _(v):
        for tt in range(ntt):
            w = [(xsem, xtot)]
            if guard and tt == 0:
                w.append((ss, ss0))
            sop(v, vs, lambda tt=tt: v.tensor_reduce(
                SUM[:, tt:tt + 1], xch[s][:, tt * D:(tt + 1) * D], axis=AX.X, op=ALU.add), waits=w)
        sop(v, vs, lambda: v.tensor_scalar(MC[:, :], SUM[:, :], 1.0 / D, None, op0=ALU.mult))
        sop(v, vs, lambda: v.tensor_tensor(VAR[:, :], MC[:, :], MC[:, :], op=ALU.mult))
        sop(v, vs, lambda: v.scalar_tensor_tensor(
            VAR[:, :], SQ[:, :], 1.0 / D, VAR[:, :], op0=ALU.mult, op1=ALU.subtract),
            waits=[(ss, ss0 + ntt)])
        sop(v, vs, lambda: v.tensor_scalar(VAR[:, :], VAR[:, :], EPS, None, op0=ALU.add))
        for tt in range(ntt):
            w = [(ss, ss0 + ntt + 2)]
            if tt >= 2:
                w.append((ts, ts0 + 8 * (tt - 1)))
            elif guard:
                w.append((ts, ts0))
            sop(v, vs, lambda tt=tt: v.tensor_scalar(
                xtm[tt % 2][:, :], xch[s][:, tt * D:(tt + 1) * D],
                MC[:, tt:tt + 1], RSTD[:, tt:tt + 1], op0=ALU.subtract, op1=ALU.mult),
                waits=w)

    @block.tensor
    def _(t):
        for j in range(8 * ntt):
            tt, k = divmod(j, ND)
            t.wait_ge(vs.h, VS_N0 + tt + 1)
            if j >= 2:
                t.wait_ge(ss.h, SS_TR0 + (j - 2) + 1)
            t.transpose(ptT[j % 2][:, :], xtm[tt % 2][:, k * 128:(k + 1) * 128],
                        p.ident[:, :]).then_inc(ts.h, 1)
            ts.n += 1

    assert ts.n == ts0 + 8 * ntt
    assert ss.n == SS_TR0 + 8 * ntt
    assert vs.n == VS_N0 + ntt
    return ts0, vs0, ss0


def build_ffn0(p, isrc, idst, tag=0):
    nc = p.nc
    NTOK, NTT = p.NTOK, p.NTT
    Tc = min(512, NTOK)
    ntt = Tc // 128
    NCH = NTOK // Tc
    NHT = H3 // 128
    ND = D // 128
    rs, rd = p.resid[isrc], p.resid[idst]
    ts, vs, ss = p.ts, p.vs, p.ss
    ld, st = p.ld, p.st

    with ExitStack() as ctx:
        en = ctx.enter_context
        xch = [en(nc.sbuf_tensor(f"f0{tag}_xch{i}", [128, ntt * D], F32)) for i in range(2)]
        wdT = en(nc.sbuf_tensor(f"f0{tag}_wdT", [128, NHT * D], BF16))
        slab = [en(nc.sbuf_tensor(f"f0{tag}_slab{i}", [128, 2 * D], BF16)) for i in range(4)]
        xlnT = [en(nc.sbuf_tensor(f"f0{tag}_xlnT{i}", [128, ND * Tc], BF16)) for i in range(2)]
        h_sb = en(nc.sbuf_tensor(f"f0{tag}_h", [128, NHT * Tc], BF16))
        xtm = [en(nc.sbuf_tensor(f"f0{tag}_xtm{i}", [128, D], BF16)) for i in range(2)]
        junk = [en(nc.sbuf_tensor(f"f0{tag}_junk", [128, D], F32))] * 2
        sg = [en(nc.sbuf_tensor(f"f0{tag}_sg{i}", [128, Tc], BF16)) for i in range(2)]
        su = [en(nc.sbuf_tensor(f"f0{tag}_su{i}", [128, Tc], BF16)) for i in range(2)]
        stat = en(nc.sbuf_tensor(f"f0{tag}_stat", [128, 5 * ntt], F32))
        ptT = [en(nc.psum_tensor(f"f0{tag}_ptT{i}", [128, 128], BF16)) for i in range(2)]
        ptA = [en(nc.psum_tensor(f"f0{tag}_ptA{i}", [128, 512], F32)) for i in range(2)]
        ptB = [en(nc.psum_tensor(f"f0{tag}_ptB{i}", [128, D], F32)) for i in range(2)]

        p.f0_xch_ld = getattr(p, "f0_xch_ld", {})
        _lc = [ld[0].n, ld[1].n]
        for _c in range(NCH):
            _lc[_c % 2] += 16
            p.f0_xch_ld[(tag, _c)] = _lc[_c % 2]
        p.f0_ts_after_A = {}
        p.f0_slab_tot = {}
        _cnt = [ld[2 + i].n for i in range(4)]
        for _c in range(NCH):
            for _ht in range(NHT):
                _sl = _ht % 4
                _cnt[_sl] += 16
                p.f0_slab_tot[(_c, _ht)] = _cnt[_sl]

        R = {"xch": xch, "xlnT": xlnT, "xtm": xtm, "junk": junk, "stat": stat, "ptT": ptT,
             "ntt": ntt, "xld": lambda c: (ld[c % 2], p.f0_xch_ld[(tag, c)])}

        def slab_load(sync, c, ht, in_block):
            sl = ht % 4
            if in_block:
                sync.wait_ge(ts.h, p.f0_ts_after_A[(c, ht - 4)])
            elif c > 0:
                sync.wait_ge(ts.h, p.f0_ts_after_A[(c - 1, NHT - 4 + sl)])
            sync.dma_start(slab[sl][:, :], p.l0_gu[ht, :, :]).then_inc(ld[2 + sl].h, 16)
            ld[2 + sl].n += 16
            assert p.f0_slab_tot[(c, ht)] == ld[2 + sl].n

        def emit_loads(sync, c):
            s = c % 2
            sync.wait_ge(st[s].h, st[s].n)
            sync.dma_start(xch[s][:, :].rearrange("q (tt d) -> q tt d", tt=ntt),
                           rs[c * Tc:(c + 1) * Tc, :].rearrange("(tt q) d -> q tt d", q=128)
                           ).then_inc(ld[s].h, 16)
            ld[s].n += 16
            assert p.f0_xch_ld[(tag, c)] == ld[s].n
            for ht in range(4):
                slab_load(sync, c, ht, in_block=False)

        with nc.Block() as block:
            @block.sync
            def _(sync):
                sync.dma_start(wdT[:, :].rearrange("q (ht d) -> q ht d", ht=NHT),
                               p.l0_dT.rearrange("(ht q) d -> q ht d", q=128)).then_inc(ld[6].h, 16)
                ld[6].n += 16
                emit_loads(sync, 0)

        for c in range(NCH):
            s = c % 2
            with nc.Block() as block:
                ln0_ss_end = None
                if c == 0:
                    emit_ln(p, block, R, 0)
                    ln0_ss_end = ss.n

                ts0, vs0, ss0 = ts.n, vs.n, ss.n
                TS_A0 = ts0
                TS_B0 = TS_A0 + 2 * NHT
                SS_A0 = ss0
                VS_A0 = vs0
                VS_B0 = VS_A0 + 2 * NHT
                for ht in range(NHT):
                    p.f0_ts_after_A[(c, ht)] = TS_A0 + 2 * ht + 2

                @block.scalar
                def _(sc):
                    for ht in range(NHT):
                        w = [(ts, TS_A0 + 2 * ht + 1)]
                        if ht >= 2:
                            w.append((vs, VS_A0 + 2 * (ht - 2) + 2))
                        sop(sc, ss, lambda ht=ht: sc.activation(
                            sg[ht % 2][:, 0:Tc], ptA[0][:, 0:Tc], AF.Silu, scale=QSCALE), waits=w)

                @block.vector
                def _(v):
                    for ht in range(NHT):
                        sop(v, vs, lambda ht=ht: v.tensor_copy(su[ht % 2][:, 0:Tc], ptA[1][:, 0:Tc]),
                            waits=[(ts, TS_A0 + 2 * ht + 2)])
                        sop(v, vs, lambda ht=ht: v.tensor_tensor(
                            h_sb[:, ht * Tc:(ht + 1) * Tc], sg[ht % 2][:, 0:Tc], su[ht % 2][:, 0:Tc],
                            op=ALU.mult),
                            waits=[(ss, SS_A0 + ht + 1)])
                    for tb in range(ntt):
                        sop(v, vs, lambda tb=tb: v.scalar_tensor_tensor(
                            xch[s][:, tb * D:(tb + 1) * D], ptB[tb % 2][:, :],
                            0.5 * QSCALE * QSCALE, xch[s][:, tb * D:(tb + 1) * D],
                            op0=ALU.mult, op1=ALU.add),
                            waits=[(ts, TS_B0 + tb + 1)])

                @block.tensor
                def _(t):
                    if ln0_ss_end is not None:
                        t.wait_ge(ss.h, ln0_ss_end)
                    for ht in range(NHT):
                        sl = ht % 4
                        t.wait_ge(ld[2 + sl].h, p.f0_slab_tot[(c, ht)])
                        for gu in range(2):
                            if ht >= 1:
                                if gu == 0:
                                    t.wait_ge(ss.h, SS_A0 + (ht - 1) + 1)
                                else:
                                    t.wait_ge(vs.h, VS_A0 + 2 * (ht - 1) + 1)
                            for k in range(ND):
                                mm = t.matmul(ptA[gu][:, 0:Tc],
                                              slab[sl][:, gu * D + k * 128: gu * D + (k + 1) * 128],
                                              xlnT[s][:, k * Tc:(k + 1) * Tc],
                                              start=(k == 0), stop=(k == ND - 1))
                            mm.then_inc(ts.h, 1); ts.n += 1
                    for tb in range(ntt):
                        t.wait_ge(vs.h, VS_A0 + 2 * NHT)
                        if tb >= 2:
                            t.wait_ge(vs.h, VS_B0 + (tb - 2) + 1)
                        t.wait_ge(ld[6].h, ld[6].n)
                        for ht in range(NHT):
                            for dh in range(2):
                                mm = t.matmul(ptB[tb % 2][:, dh * 512:(dh + 1) * 512],
                                              h_sb[:, ht * Tc + tb * 128: ht * Tc + (tb + 1) * 128],
                                              wdT[:, ht * D + dh * 512: ht * D + dh * 512 + 512],
                                              start=(ht == 0), stop=(ht == NHT - 1))
                        mm.then_inc(ts.h, 1); ts.n += 1

                assert ts.n == TS_B0 + ntt
                assert ss.n == SS_A0 + NHT
                assert vs.n == VS_B0 + ntt

                if c + 1 < NCH:
                    emit_ln(p, block, R, c + 1, guard=(c == 0))

                @block.sync
                def _(sync):
                    for ht in range(4, NHT):
                        slab_load(sync, c, ht, in_block=True)
                    sync.wait_ge(vs.h, VS_B0 + ntt)
                    sync.dma_start(rd[c * Tc:(c + 1) * Tc, :].rearrange("(tt q) d -> q tt d", q=128),
                                   xch[s][:, :].rearrange("q (tt d) -> q tt d", tt=ntt)
                                   ).then_inc(st[s].h, 16)
                    st[s].n += 16
                    if c + 1 < NCH:
                        emit_loads(sync, c + 1)
        drain_dmas(p)



def build_mamba(p, isrc, idst):
    nc = p.nc
    B, T, NTOK = p.B, p.T, p.NTOK
    Tc = min(512, NTOK)
    ntt = Tc // 128
    NCH = NTOK // Tc
    ND = D // 128
    NDI = DI // 128
    NF = 2 * DI // 128
    NTB = T // 128
    rs, rd = p.resid[isrc], p.resid[idst]
    ts, vs, ss = p.ts, p.vs, p.ss
    ld, st = p.ld, p.st

    CW = lambda k, kt: p.bcols[:, 64 + k * 16 + kt: 64 + k * 16 + kt + 1]
    CB = lambda kt: p.bcols[:, kt:kt + 1]
    DTB = lambda kt: p.bcols[:, 16 + kt:16 + kt + 1]
    ACOL = lambda kt: p.bcols[:, 32 + kt:32 + kt + 1]
    DCOL = lambda kt: p.bcols[:, 48 + kt:48 + kt + 1]

    with ExitStack() as stage:

        with ExitStack() as ctx:
            en = ctx.enter_context
            iproj = en(nc.sbuf_tensor("mb_iproj", [128, ND * 2 * DI], BF16))
            xch = [en(nc.sbuf_tensor(f"mb_xch{i}", [128, ntt * D], F32)) for i in range(2)]
            xlnT = [en(nc.sbuf_tensor(f"mb_xlnT{i}", [128, ND * Tc], BF16)) for i in range(2)]
            xtm = [en(nc.sbuf_tensor(f"mb_xtm{i}", [128, D], BF16)) for i in range(2)]
            junk = [en(nc.sbuf_tensor("mb_junk", [128, D], F32))] * 2
            stat = en(nc.sbuf_tensor("mb_stat", [128, 5 * ntt], F32))
            xz_sb = en(nc.sbuf_tensor("mb_xz", [128, (NF // 2) * Tc], F32))
            ptT = [en(nc.psum_tensor(f"mb_ptT{i}", [128, 128], BF16)) for i in range(2)]
            ptM = [en(nc.psum_tensor(f"mb_ptM{i}", [128, 512], F32)) for i in range(2)]

            p.mb_xz_st = {}
            _s2 = p.st[2].n
            for _c in range(NCH):
                for _hf in range(2):
                    _s2 += 16
                    p.mb_xz_st[(_c, _hf)] = _s2
            p.mb_xch_ld = {}
            _lc = [ld[0].n, ld[1].n]
            for _c in range(NCH):
                _lc[_c % 2] += 16
                p.mb_xch_ld[_c] = _lc[_c % 2]

            R = {"xch": xch, "xlnT": xlnT, "xtm": xtm, "junk": junk, "stat": stat, "ptT": ptT,
                 "ntt": ntt, "xld": lambda c: (ld[c % 2], p.mb_xch_ld[c])}

            def m1_load(sync, c):
                s2 = c % 2
                sync.wait_ge(st[s2].h, st[s2].n)
                sync.dma_start(xch[s2][:, :].rearrange("q (tt d) -> q tt d", tt=ntt),
                               rs[c * Tc:(c + 1) * Tc, :].rearrange("(tt q) d -> q tt d", q=128)
                               ).then_inc(ld[s2].h, 16)
                ld[s2].n += 16
                assert p.mb_xch_ld[c] == ld[s2].n

            with nc.Block() as block:
                @block.sync
                def _(sync):
                    sync.dma_start(iproj[:, :].rearrange("q (dt f) -> q dt f", dt=ND),
                                   p.inprojT.rearrange("(dt q) f -> q dt f", q=128)).then_inc(ld[6].h, 16)
                    ld[6].n += 16
                    m1_load(sync, 0)

            for c in range(NCH):
                s = c % 2
                b = (c * Tc) // T
                t0 = c * Tc - b * T
                with nc.Block() as block:
                    ln0_ss_end = None
                    if c == 0:
                        emit_ln(p, block, R, 0)
                        ln0_ss_end = ss.n

                    ts0, vs0, ss0 = ts.n, vs.n, ss.n

                    @block.vector
                    def _(v):
                        for ft in range(NF):
                            w = [(ts, ts0 + ft + 1)]
                            if ft == 0 and c > 0:
                                w.append((p.st[2], p.mb_xz_st[(c - 1, 1)]))
                            if ft == NF // 2:
                                w.append((p.st[2], p.mb_xz_st[(c, 0)]))
                            sop(v, vs, lambda ft=ft: v.tensor_copy(
                                xz_sb[:, (ft % (NF // 2)) * Tc:((ft % (NF // 2)) + 1) * Tc],
                                ptM[ft % 2][:, 0:Tc]), waits=w)

                    @block.tensor
                    def _(t):
                        t.wait_ge(ld[6].h, ld[6].n)
                        if ln0_ss_end is not None:
                            t.wait_ge(ss.h, ln0_ss_end)
                        for ft in range(NF):
                            if ft >= 2:
                                t.wait_ge(vs.h, vs0 + (ft - 2) + 1)
                            for k in range(ND):
                                mm = t.matmul(ptM[ft % 2][:, 0:Tc],
                                              iproj[:, k * 2 * DI + ft * 128: k * 2 * DI + (ft + 1) * 128],
                                              xlnT[s][:, k * Tc:(k + 1) * Tc],
                                              start=(k == 0), stop=(k == ND - 1))
                            mm.then_inc(ts.h, 1); ts.n += 1

                    assert ts.n == ts0 + NF and vs.n == vs0 + NF

                    if c + 1 < NCH:
                        emit_ln(p, block, R, c + 1, guard=(c == 0))

                    @block.sync
                    def _(sync):
                        for hf in range(2):
                            sync.wait_ge(vs.h, vs0 + (NF // 2) * (hf + 1))
                            sync.dma_start(
                                p.xz_scr[b, hf * DI:(hf + 1) * DI, t0:t0 + Tc]
                                .rearrange("(ft q) t -> q ft t", q=128),
                                xz_sb[:, :].rearrange("q (ft t) -> q ft t", ft=NF // 2)
                                ).then_inc(st[2].h, 16)
                            st[2].n += 16
                            assert p.mb_xz_st[(c, hf)] == st[2].n
                        if c + 1 < NCH:
                            m1_load(sync, c + 1)
            drain_dmas(p)

        with ExitStack() as ctx:
            en = ctx.enter_context
            xpad = [en(nc.sbuf_tensor(f"mb_xpad{i}", [128, T + 3], F32)) for i in range(2)]
            acc = [en(nc.sbuf_tensor(f"mb_acc{i}", [128, T], F32)) for i in range(2)]
            xaf = [en(nc.sbuf_tensor(f"mb_xaf{i}", [128, T], F32)) for i in range(2)]
            xab = [en(nc.sbuf_tensor(f"mb_xab{i}", [128, T], BF16)) for i in range(2)]
            units = [(b, kt) for b in range(B) for kt in range(NDI)]
            vs0, ss0 = vs.n, ss.n
            ld0 = [ld[0].n, ld[1].n]
            st0 = [p.st[0].n, p.st[1].n]

            with nc.Block() as block:
                @block.gpsimd
                def _(g):
                    g.memset(xpad[0][:, 0:3], 0.0).then_inc(p.gs.h, 1); p.gs.n += 1
                    g.memset(xpad[1][:, 0:3], 0.0).then_inc(p.gs.h, 1); p.gs.n += 1

                @block.sync
                def _(sync):
                    def store(u2):
                        b2, kt2 = units[u2]
                        sl2 = u2 % 2
                        sync.wait_ge(vs.h, vs0 + 5 * u2 + 5)
                        sync.dma_start(p.xiact_f[b2, kt2 * 128:(kt2 + 1) * 128, :],
                                       xaf[sl2][:, :]).then_inc(p.st[sl2].h, 16)
                        sync.dma_start(p.xiact_b[b2, kt2 * 128:(kt2 + 1) * 128, :],
                                       xab[sl2][:, :]).then_inc(p.st[sl2].h, 16)
                        p.st[sl2].n += 32
                    for u, (b, kt) in enumerate(units):
                        sl = u % 2
                        if u >= 2:
                            sync.wait_ge(vs.h, vs0 + 5 * (u - 2) + 4)
                        sync.dma_start(xpad[sl][:, 3:3 + T],
                                       p.xz_scr[b, kt * 128:(kt + 1) * 128, :]).then_inc(ld[sl].h, 16)
                        ld[sl].n += 16
                        if u >= 1:
                            store(u - 1)
                    store(len(units) - 1)

                @block.vector
                def _(v):
                    for u, (b, kt) in enumerate(units):
                        sl = u % 2
                        w = [(ld[sl], ld0[sl] + 16 * (u // 2 + 1)), (p.gs, p.gs.n)]
                        sop(v, vs, lambda sl=sl, kt=kt: v.tensor_scalar(
                            acc[sl][:, :], xpad[sl][:, 0:T], CW(0, kt), None, op0=ALU.mult), waits=w)
                        for k in range(1, KCONV):
                            sop(v, vs, lambda sl=sl, kt=kt, k=k: v.scalar_tensor_tensor(
                                acc[sl][:, :], xpad[sl][:, k:k + T], CW(k, kt), acc[sl][:, :],
                                op0=ALU.mult, op1=ALU.add))
                        sop(v, vs, lambda sl=sl: v.tensor_copy(xab[sl][:, :], xaf[sl][:, :]),
                            waits=[(ss, ss0 + u + 1)])

                @block.scalar
                def _(sc):
                    for u, (b, kt) in enumerate(units):
                        sl = u % 2
                        w = [(vs, vs0 + 5 * u + 4)]
                        if u >= 2:
                            w.append((p.st[sl], st0[sl] + 32 * (u // 2)))
                        sop(sc, ss, lambda sl=sl, kt=kt: sc.activation(
                            xaf[sl][:, :], acc[sl][:, :], AF.Silu, bias=CB(kt)), waits=w)

            drain_dmas(p)

        with ExitStack() as ctx:
            en = ctx.enter_context
            y_sb = en(nc.sbuf_tensor("mb_y", [128, NDI * T], BF16))
            xpr = en(nc.sbuf_tensor("mb_xpr", [128, NDI * DT], BF16))
            dpr = en(nc.sbuf_tensor("mb_dpr", [64, DI], BF16))
            oproj = en(nc.sbuf_tensor("mb_oproj", [128, NDI * D], BF16))
            strip = [en(nc.sbuf_tensor(f"mb_strip{i}", [128, T], BF16)) for i in range(2)]
            dtlow = en(nc.sbuf_tensor("mb_dtlow", [64, T], BF16))
            vband = en(nc.sbuf_tensor("mb_vband", [128, T], F32))
            t1 = en(nc.sbuf_tensor("mb_t1", [128, T], F32))
            t2 = en(nc.sbuf_tensor("mb_t2", [128, T], F32))
            uf = [en(nc.sbuf_tensor("mb_uf", [128, T], F32))] * 2
            zf = [en(nc.sbuf_tensor("mb_zf", [128, T], F32))] * 2
            xt = [en(nc.sbuf_tensor(f"mb_xt{i}", [128, D], F32)) for i in range(2)]
            W5 = min(512, T)
            NCF = T // W5

            with nc.Block() as block:
                @block.sync
                def _(sync):
                    sync.dma_start(xpr[:, :].rearrange("q (kt r) -> q kt r", kt=NDI),
                                   p.xprojT.rearrange("(kt q) r -> q kt r", q=128)).then_inc(ld[6].h, 16)
                    ld[6].n += 16
                    sync.dma_start(dpr[0:64, :], p.dtprojT[:, :]).then_inc(ld[6].h, 16)
                    ld[6].n += 16
                    sync.dma_start(oproj[:, :].rearrange("q (kt d) -> q kt d", kt=NDI),
                                   p.outprojT.rearrange("(kt q) d -> q kt d", q=128)).then_inc(ld[6].h, 16)
                    ld[6].n += 16

            pstk = [ExitStack()]

            def psums(phase):
                pstk[0].close()
                pstk[0] = ExitStack()
                if phase == "dt":
                    return pstk[0].enter_context(nc.psum_tensor(f"mb_pDL{b}", [64, T], F32))
                if phase == "delta":
                    return [pstk[0].enter_context(nc.psum_tensor(f"mb_pD{b}{i}", [128, 512], F32))
                            for i in range(2)]
                return [pstk[0].enter_context(nc.psum_tensor(f"mb_pO{b}{i}", [128, D], F32))
                        for i in range(2)]

            for b in range(B):
                pDL = psums("dt")
                ts0 = ts.n
                with nc.Block() as block:
                    @block.sync
                    def _(sync):
                        for kt in range(NDI):
                            sl = kt % 2
                            if kt >= 2:
                                sync.wait_ge(ts.h, ts0 + (kt - 2) + 1)
                            sync.dma_start(strip[sl][:, :],
                                           p.xiact_b[b, kt * 128:(kt + 1) * 128, :]).then_inc(ld[sl].h, 16)
                            ld[sl].n += 16
                            p.mb_strip_ld[(b, kt)] = ld[sl].n

                    @block.tensor
                    def _(t):
                        t.wait_ge(ld[6].h, ld[6].n)
                        for kt in range(NDI):
                            sl = kt % 2
                            t.wait_ge(ld[sl].h, p.mb_strip_ld[(b, kt)])
                            for ncf in range(NCF):
                                mm = t.matmul(pDL[0:64, ncf * W5:(ncf + 1) * W5],
                                              xpr[:, kt * DT:(kt + 1) * DT],
                                              strip[sl][:, ncf * W5:(ncf + 1) * W5],
                                              start=(kt == 0), stop=(kt == NDI - 1))
                            mm.then_inc(ts.h, 1); ts.n += 1

                    @block.vector
                    def _(v):
                        sop(v, vs, lambda: v.tensor_copy(dtlow[0:64, :], pDL[0:64, :]),
                            waits=[(ts, ts0 + NDI)])

                pD = psums("delta")
                for kt in range(NDI):
                    sl = kt % 2
                    ts0, vs0, ss0 = ts.n, vs.n, ss.n
                    uld, zld = ld[sl].n + 16, ld[7].n + 16
                    with nc.Block() as block:
                        @block.sync
                        def _(sync):
                            sync.dma_start(uf[sl][:, :],
                                           p.xiact_f[b, kt * 128:(kt + 1) * 128, :]).then_inc(ld[sl].h, 16)
                            ld[sl].n += 16
                            sync.dma_start(zf[sl][:, :],
                                           p.xz_scr[b, DI + kt * 128:DI + (kt + 1) * 128, :]
                                           ).then_inc(ld[7].h, 16)
                            ld[7].n += 16

                        @block.tensor
                        def _(t):
                            for ncf in range(NCF):
                                if ncf >= 2:
                                    t.wait_ge(vs.h, vs0 + (ncf - 2) + 1)
                                mm = t.matmul(pD[ncf % 2][:, 0:W5],
                                              dpr[0:64, kt * 128:(kt + 1) * 128],
                                              dtlow[0:64, ncf * W5:(ncf + 1) * W5],
                                              start=True, stop=True)
                                mm.then_inc(ts.h, 1); ts.n += 1

                        @block.vector
                        def _(v):
                            for ncf in range(NCF):
                                sop(v, vs, lambda ncf=ncf: v.tensor_scalar(
                                    vband[:, ncf * W5:(ncf + 1) * W5], pD[ncf % 2][:, 0:W5],
                                    DTB(kt), None, op0=ALU.add),
                                    waits=[(ts, ts0 + ncf + 1)])
                            sop(v, vs, lambda: v.tensor_scalar(vband[:, :], vband[:, :], 0.0, None,
                                                               op0=ALU.max),
                                waits=[(ss, ss0 + 1)])
                            sop(v, vs, lambda: v.tensor_tensor(vband[:, :], vband[:, :], t1[:, :],
                                                               op=ALU.add),
                                waits=[(ss, ss0 + 3)])
                            sop(v, vs, lambda: v.tensor_scalar(vband[:, :], vband[:, :], 10.0, 1e-6,
                                                               op0=ALU.min, op1=ALU.max))
                            sop(v, vs, lambda: v.tensor_scalar(t1[:, :], t1[:, :], 1.0, 1e-6,
                                                               op0=ALU.min, op1=ALU.max),
                                waits=[(ss, ss0 + 4)])
                            sop(v, vs, lambda: v.tensor_tensor_scan(
                                t2[:, :], t1[:, :], uf[sl][:, :], 0.0, op0=ALU.mult, op1=ALU.add),
                                waits=[(ld[sl], uld)])
                            sop(v, vs, lambda: v.scalar_tensor_tensor(
                                t2[:, :], uf[sl][:, :], DCOL(kt), t2[:, :], op0=ALU.mult, op1=ALU.add))
                            sop(v, vs, lambda: v.tensor_scalar(t2[:, :], t2[:, :], 1e4, -1e4,
                                                               op0=ALU.min, op1=ALU.max))
                            sop(v, vs, lambda: v.tensor_tensor(
                                y_sb[:, kt * T:(kt + 1) * T], t2[:, :], t1[:, :], op=ALU.mult),
                                waits=[(ss, ss0 + 5)])

                        @block.scalar
                        def _(sc):
                            sop(sc, ss, lambda: sc.activation(t1[:, :], vband[:, :], AF.Abs),
                                waits=[(vs, vs0 + NCF)])
                            sop(sc, ss, lambda: sc.activation(t1[:, :], t1[:, :], AF.Exp, scale=-1.0))
                            sop(sc, ss, lambda: sc.activation(t1[:, :], t1[:, :], AF.Ln, bias=1.0))
                            sop(sc, ss, lambda: sc.activation(t1[:, :], vband[:, :], AF.Exp,
                                                              scale=ACOL(kt)),
                                waits=[(vs, vs0 + NCF + 3)])
                            sop(sc, ss, lambda: sc.activation(t1[:, :], zf[sl][:, :], AF.Silu),
                                waits=[(vs, vs0 + NCF + 5), (ld[7], zld)])

                pO = psums("out")
                ts0, vs0 = ts.n, vs.n
                ld0s = [ld[0].n, ld[1].n]
                st0s = [st[0].n, st[1].n]
                with nc.Block() as block:
                    @block.tensor
                    def _(t):
                        for tt in range(NTB):
                            sl = tt % 2
                            if tt >= 2:
                                t.wait_ge(vs.h, vs0 + (tt - 2) + 1)
                            for kt in range(NDI):
                                for dh in range(2):
                                    mm = t.matmul(pO[tt % 2][:, dh * 512:(dh + 1) * 512],
                                                  y_sb[:, kt * T + tt * 128: kt * T + (tt + 1) * 128],
                                                  oproj[:, kt * D + dh * 512: kt * D + dh * 512 + 512],
                                                  start=(kt == 0), stop=(kt == NDI - 1))
                            mm.then_inc(ts.h, 1); ts.n += 1

                    @block.vector
                    def _(v):
                        for tt in range(NTB):
                            sl = tt % 2
                            sop(v, vs, lambda tt=tt, sl=sl: v.scalar_tensor_tensor(
                                xt[sl][:, :], pO[tt % 2][:, :], 1.0, xt[sl][:, :],
                                op0=ALU.mult, op1=ALU.add),
                                waits=[(ts, ts0 + tt + 1),
                                       (ld[sl], ld0s[sl] + 16 * (tt // 2 + 1))])

                    @block.sync
                    def _(sync):
                        for tt in range(NTB):
                            sl = tt % 2
                            if tt >= 2:
                                sync.wait_ge(vs.h, vs0 + (tt - 2) + 1)
                                sync.wait_ge(st[sl].h, st0s[sl] + 16 * (tt // 2))
                            sync.dma_start(
                                xt[sl][:, :],
                                rs[b * T + tt * 128: b * T + (tt + 1) * 128, :]).then_inc(ld[sl].h, 16)
                            ld[sl].n += 16
                            if tt >= 1:
                                psl = (tt - 1) % 2
                                sync.wait_ge(vs.h, vs0 + tt)
                                sync.dma_start(rd[b * T + (tt - 1) * 128: b * T + tt * 128, :],
                                               xt[psl][:, :]).then_inc(st[psl].h, 16)
                                st[psl].n += 16
                        sync.wait_ge(vs.h, vs0 + NTB)
                        lsl = (NTB - 1) % 2
                        sync.dma_start(rd[b * T + (NTB - 1) * 128: b * T + NTB * 128, :],
                                       xt[lsl][:, :]).then_inc(st[lsl].h, 16)
                        st[lsl].n += 16
            pstk[0].close()
            drain_dmas(p)


def build_moe(p, isrc, idst, final=False, tag=0):
    nc = p.nc
    NTOK, NTT = p.NTOK, p.NTT
    ND = D // 128
    NMT = DM // 128
    RC = min(1024, NTOK)
    NRC = NTOK // RC
    CAP = RC // 2
    NTB = CAP // 128
    TPC = RC // 128
    JUNK = float(NE * CAP)
    rs = p.resid[isrc]
    rd = p.y_out if final else p.resid[idst]
    ts, vs, ss, gs = p.ts, p.vs, p.ss, p.gs
    ld, st = p.ld, p.st
    gld = p.gld

    with ExitStack() as stage:
        sen = stage.enter_context
        c_all = sen(nc.sbuf_tensor(f"mo{tag}_call", [128, NTT * NE], F32))

        with ExitStack() as ctx:
            en = ctx.enter_context
            xt = [en(nc.sbuf_tensor(f"mo{tag}_xt{i}", [128, D], F32)) for i in range(2)]
            xlf = en(nc.sbuf_tensor(f"mo{tag}_xlf", [128, D], F32))
            xtm = [en(nc.sbuf_tensor(f"mo{tag}_xtm{i}", [128, 1032], BF16)) for i in range(2)]
            xlo = [en(nc.sbuf_tensor(f"mo{tag}_xlo{i}", [128, D], BF16)) for i in range(2)]
            xtbh = [en(nc.sbuf_tensor(f"mo{tag}_xtbh{i}", [128, D], BF16)) for i in range(2)]
            xtbl = [en(nc.sbuf_tensor(f"mo{tag}_xtbl{i}", [128, D], BF16)) for i in range(2)]
            junk = [en(nc.sbuf_tensor(f"mo{tag}_junk", [128, D], F32))] * 2
            stat = en(nc.sbuf_tensor(f"mo{tag}_stat", [128, 8], F32))
            tp2 = en(nc.sbuf_tensor(f"mo{tag}_tp2", [128, 16], F32))
            lsb = en(nc.sbuf_tensor(f"mo{tag}_lsb", [128, NE], F32))
            geb = en(nc.sbuf_tensor(f"mo{tag}_geb", [128, NE], F32))
            exb = en(nc.sbuf_tensor(f"mo{tag}_exb", [128, NE], F32))
            mskf = en(nc.sbuf_tensor(f"mo{tag}_mskf", [128, NE], F32))
            mskb = en(nc.sbuf_tensor(f"mo{tag}_mskb", [128, NE], BF16))
            preb = en(nc.sbuf_tensor(f"mo{tag}_preb", [128, NE], F32))
            rowb = en(nc.sbuf_tensor(f"mo{tag}_rowb", [1, NE], F32))
            offb = en(nc.sbuf_tensor(f"mo{tag}_offb", [1, NE], F32))
            posa = en(nc.sbuf_tensor(f"mo{tag}_posa", [128, NE], F32))
            posb = en(nc.sbuf_tensor(f"mo{tag}_posb", [128, NE], F32))
            pos12 = en(nc.sbuf_tensor(f"mo{tag}_pos12", [128, 4], F32))
            posi = en(nc.sbuf_tensor(f"mo{tag}_posi", [128, 4], I32))
            ptT = [en(nc.psum_tensor(f"mo{tag}_ptT{i}", [128, 128], BF16)) for i in range(2)]
            pl = [en(nc.psum_tensor(f"mo{tag}_pl{i}", [128, NE], F32)) for i in range(2)]
            psP = en(nc.psum_tensor(f"mo{tag}_psP", [128, NE], F32))
            psT = en(nc.psum_tensor(f"mo{tag}_psT", [1, NE], F32))
            psB = en(nc.psum_tensor(f"mo{tag}_psB", [128, NE], F32))
            SUM = stat[:, 0:1]; SQ = stat[:, 1:2]; MC = stat[:, 2:3]; VAR = stat[:, 3:4]; RSTD = stat[:, 4:5]
            M1 = tp2[:, 0:1]; M2 = tp2[:, 1:2]; DMM = tp2[:, 2:3]; EDM = tp2[:, 3:4]
            RDEN = tp2[:, 4:5]; NEGM1 = tp2[:, 5:6]
            PMIN = tp2[:, 6:7]; PEQ = tp2[:, 7:8]; PMIN2 = tp2[:, 8:9]

            NS = 21
            NT = 20
            NVu = [20 + 16 + (1 if (u % TPC == 0) else 0) for u in range(NTT)]
            ts_base = [0]; vs_base = [0]; ss_base = [0]
            for u in range(NTT):
                ts_base.append(ts_base[-1] + NT)
                vs_base.append(vs_base[-1] + NVu[u])
                ss_base.append(ss_base[-1] + NS)
            ts0, vs0, ss0 = ts.n, vs.n, ss.n
            ld0 = [ld[0].n, ld[1].n]
            st2_0 = st[2].n
            gf0 = gld[0].n
            gsc0 = gld[1].n
            gx0 = p.gidx.n

            def TSB(u): return ts0 + ts_base[u]
            def VSB(u): return vs0 + vs_base[u]
            def SSB(u): return ss0 + ss_base[u]

            with nc.Block() as block:
                @block.sync
                def _(sync):
                    for u in range(NTT):
                        sl = u % 2
                        if u >= 2:
                            sync.wait_ge(vs.h, VSB(u - 2) + 6)
                        sync.dma_start(xt[sl][:, :],
                                       rs[u * 128:(u + 1) * 128, :]).then_inc(ld[sl].h, 16)
                        ld[sl].n += 16
                        if u >= 1:
                            sync.wait_ge(vs.h, VSB(u - 1) + 21)
                            sync.dma_start(p.xtm_scr[(u - 1) * 128:u * 128, :],
                                           xtm[(u - 1) % 2][:, :]).then_inc(st[2].h, 16)
                            st[2].n += 16
                    sync.wait_ge(vs.h, VSB(NTT - 1) + 21)
                    sync.dma_start(p.xtm_scr[(NTT - 1) * 128:NTT * 128, :],
                                   xtm[(NTT - 1) % 2][:, :]).then_inc(st[2].h, 16)
                    st[2].n += 16

                @block.vector
                def _(v):
                    for u in range(NTT):
                        sl = u % 2
                        xld = ld0[sl] + 16 * (u // 2 + 1)
                        sop(v, vs, lambda sl=sl: v.tensor_reduce(
                            SUM, xt[sl][:, :], axis=AX.X, op=ALU.add), waits=[(ld[sl], xld)])
                        sop(v, vs, lambda: v.tensor_scalar(MC, SUM, 1.0 / D, None, op0=ALU.mult))
                        sop(v, vs, lambda: v.tensor_tensor(VAR, MC, MC, op=ALU.mult))
                        sop(v, vs, lambda u=u: v.scalar_tensor_tensor(
                            VAR, SQ, 1.0 / D, VAR, op0=ALU.mult, op1=ALU.subtract),
                            waits=[(ss, SSB(u) + 1)])
                        sop(v, vs, lambda: v.tensor_scalar(VAR, VAR, EPS, None, op0=ALU.add))
                        sop(v, vs, lambda sl=sl, u=u: v.tensor_scalar(
                            xlf[:, :], xt[sl][:, :], MC, RSTD, op0=ALU.subtract, op1=ALU.mult),
                            waits=[(ss, SSB(u) + 3)])
                        w7 = []
                        if u >= 2:
                            w7 = [(ts, TSB(u - 2) + 16), (st[2], st2_0 + 16 * (u - 1))]
                        sop(v, vs, lambda u=u, sl=sl: v.tensor_copy(xtm[sl][:, 0:D], xlf[:, :]),
                            waits=w7)
                        sop(v, vs, lambda sl=sl: v.tensor_tensor(
                            xlo[sl][:, :], xlf[:, :], xtm[sl][:, 0:D], op=ALU.subtract))
                        sop(v, vs, lambda u=u: v.tensor_copy(lsb[:, :], pl[u % 2][:, :]),
                            waits=[(ts, TSB(u) + 17)])
                        sop(v, vs, lambda: v.tensor_reduce(M1, lsb[:, :], axis=AX.X, op=ALU.max))
                        sop(v, vs, lambda: v.tensor_scalar(geb[:, :], lsb[:, :], M1, None,
                                                           op0=ALU.is_equal))
                        sop(v, vs, lambda: v.scalar_tensor_tensor(
                            geb[:, :], geb[:, :], -1e30, lsb[:, :], op0=ALU.mult, op1=ALU.add))
                        sop(v, vs, lambda: v.tensor_reduce(M2, geb[:, :], axis=AX.X, op=ALU.max))
                        sop(v, vs, lambda: v.tensor_tensor(DMM, M2, M1, op=ALU.subtract))
                        sop(v, vs, lambda u=u: v.tensor_scalar(EDM, EDM, 1.0, None, op0=ALU.add),
                            waits=[(ss, SSB(u) + 20)])
                        sop(v, vs, lambda: v.reciprocal(RDEN, EDM))
                        sop(v, vs, lambda: v.tensor_scalar(RDEN, RDEN, 0.5, None, op0=ALU.mult))
                        sop(v, vs, lambda: v.tensor_scalar(NEGM1, M1, -1.0, None, op0=ALU.mult))
                        sop(v, vs, lambda: v.tensor_scalar(geb[:, :], lsb[:, :], M2, None,
                                                           op0=ALU.is_ge))
                        sop(v, vs, lambda u=u: v.scalar_tensor_tensor(
                            c_all[:, u * NE:(u + 1) * NE], exb[:, :], RDEN, geb[:, :],
                            op0=ALU.mult, op1=ALU.mult),
                            waits=[(ss, SSB(u) + 21)])
                        sop(v, vs, lambda u=u, sl=sl: v.tensor_copy(
                            xtm[sl][:, D:D + NE], c_all[:, u * NE:(u + 1) * NE]))
                        sop(v, vs, lambda u=u: v.tensor_scalar(
                            mskf[:, :], c_all[:, u * NE:(u + 1) * NE], 0.0, None, op0=ALU.is_gt))
                        sop(v, vs, lambda: v.tensor_copy(mskb[:, :], mskf[:, :]))
                        sop(v, vs, lambda u=u: v.tensor_copy(preb[:, :], psP[:, :]),
                            waits=[(ts, TSB(u) + 18)])
                        if u % TPC == 0:
                            rcb = float((u // TPC) * 4224)
                            sop(v, vs, lambda rcb=rcb: v.tensor_scalar(
                                offb[0:1, :], p.ebase[0:1, :], 0.0, rcb,
                                op0=ALU.mult, op1=ALU.add))
                        sop(v, vs, lambda: v.tensor_tensor(rowb[0:1, :], offb[0:1, :],
                                                           p.ebase[0:1, :], op=ALU.add))
                        sop(v, vs, lambda u=u: v.tensor_tensor(posa[:, :], preb[:, :], psB[:, :],
                                                               op=ALU.add),
                            waits=[(ts, TSB(u) + 20)])
                        sop(v, vs, lambda: v.tensor_tensor(posa[:, :], posa[:, :], mskf[:, :],
                                                           op=ALU.mult))
                        jk = float((u // TPC) * 4224 + 4096)
                        sop(v, vs, lambda jk=jk: v.tensor_scalar(posb[:, :], mskf[:, :], -jk, jk,
                                                                 op0=ALU.mult, op1=ALU.add))
                        sop(v, vs, lambda: v.tensor_tensor(posa[:, :], posa[:, :], posb[:, :],
                                                           op=ALU.add))
                        sop(v, vs, lambda: v.tensor_reduce(PMIN, posa[:, :], axis=AX.X,
                                                           op=ALU.min))
                        sop(v, vs, lambda: v.tensor_scalar(posb[:, :], posa[:, :], PMIN, None,
                                                           op0=ALU.is_equal))
                        sop(v, vs, lambda: v.scalar_tensor_tensor(
                            posb[:, :], posb[:, :], 1.0e7, posa[:, :],
                            op0=ALU.mult, op1=ALU.add))
                        sop(v, vs, lambda: v.tensor_reduce(PMIN2, posb[:, :], axis=AX.X,
                                                           op=ALU.min))
                        wps = [(p.gidx, gx0 + 16 * ((u - 2) // TPC + 1) + 32 * (u - 1))] \
                            if u >= 2 else []
                        sl2 = (u % 2) * 2
                        sop(v, vs, lambda sl2=sl2: v.tensor_copy(posi[:, sl2:sl2 + 1], PMIN),
                            waits=wps)
                        sop(v, vs, lambda sl2=sl2: v.tensor_copy(posi[:, sl2 + 1:sl2 + 2],
                                                                 PMIN2))
                        sop(v, vs, lambda u=u: v.tensor_tensor(offb[0:1, :], offb[0:1, :],
                                                               psT[0:1, :], op=ALU.add),
                            waits=[(ts, TSB(u) + 19)])

                @block.scalar
                def _(sc):
                    for u in range(NTT):
                        sl = u % 2
                        xld = ld0[sl] + 16 * (u // 2 + 1)
                        sop(sc, ss, lambda sl=sl: sc.activation(
                            junk[sl][:, :], xt[sl][:, :], AF.Square, accum_out=SQ),
                            waits=[(ld[sl], xld), (vs, VSB(u - 1) + 6) if u >= 1 else (vs, 0)])
                        sop(sc, ss, lambda u=u: sc.activation(RSTD, VAR, AF.Ln),
                            waits=[(vs, VSB(u) + 5)])
                        sop(sc, ss, lambda: sc.activation(RSTD, RSTD, AF.Exp, scale=-0.5))
                        for j in range(16):
                            hi = j < 8
                            k = j % 8
                            dstb = xtbh if hi else xtbl
                            w = [(ts, TSB(u) + j + 1)]
                            if j == 0 and u >= 2:
                                w.append((ts, TSB(u - 2) + 17))
                            jj = NT * u + j
                            sop(sc, ss, lambda jj=jj, k=k, sl=sl, dstb=dstb: sc.copy(
                                dstb[sl][:, k * 128:(k + 1) * 128], ptT[jj % 2][:, :]), waits=w)
                        sop(sc, ss, lambda u=u: sc.activation(EDM, DMM, AF.Exp),
                            waits=[(vs, VSB(u) + 14)])
                        sop(sc, ss, lambda u=u: sc.activation(exb[:, :], lsb[:, :], AF.Exp,
                                                              bias=NEGM1),
                            waits=[(vs, VSB(u) + 18)])

                @block.tensor
                def _(t):
                    for u in range(NTT):
                        sl = u % 2
                        for j in range(16):
                            hi = j < 8
                            k = j % 8
                            srcb = xtm if hi else xlo
                            t.wait_ge(vs.h, VSB(u) + (7 if hi else 8))
                            jj = NT * u + j
                            if jj >= 2:
                                prev_u, prev_j = divmod(jj - 2, NT)
                                if prev_j < 16:
                                    t.wait_ge(ss.h, SSB(prev_u) + 3 + prev_j + 1)
                            t.transpose(ptT[jj % 2][:, :], srcb[sl][:, k * 128:(k + 1) * 128],
                                        p.ident[:, :]).then_inc(ts.h, 1)
                            ts.n += 1
                        t.wait_ge(ss.h, SSB(u) + 19)
                        if u >= 2:
                            t.wait_ge(vs.h, VSB(u - 2) + 9)
                        nmm = 0
                        for (xa, wa) in ((xtbh, p.wg_hi), (xtbh, p.wg_lo), (xtbl, p.wg_hi),
                                         (xtbl, p.wg_lo)):
                            for k in range(ND):
                                mm = t.matmul(pl[u % 2][:, :], xa[sl][:, k * 128:(k + 1) * 128],
                                              wa[:, k * NE:(k + 1) * NE],
                                              start=(nmm == 0), stop=(nmm == 4 * ND - 1))
                                nmm += 1
                        mm.then_inc(ts.h, 1); ts.n += 1
                        t.wait_ge(vs.h, VSB(u) + 23)
                        if u >= 1:
                            t.wait_ge(vs.h, VSB(u - 1) + 24)
                        mm = t.matmul(psP[:, :], p.tri[:, :], mskb[:, :], start=True, stop=True)
                        mm.then_inc(ts.h, 1); ts.n += 1
                        if u >= 1:
                            t.wait_ge(vs.h, VSB(u))
                        mm = t.matmul(psT[0:1, :], p.ones_bf[:, 0:1], mskb[:, :],
                                      start=True, stop=True)
                        mm.then_inc(ts.h, 1); ts.n += 1
                        t.wait_ge(vs.h, VSB(u) + 25 + (1 if u % TPC == 0 else 0))
                        if u >= 1:
                            t.wait_ge(vs.h, VSB(u - 1) + 26 + (1 if (u - 1) % TPC == 0 else 0))
                        mm = t.matmul(psB[:, :], p.ones_f32r[0:1, :], rowb[0:1, :],
                                      start=True, stop=True)
                        mm.then_inc(ts.h, 1); ts.n += 1

                @block.gpsimd
                def _(g):
                    gx = p.gidx
                    for u in range(NTT):
                        rc = u // TPC
                        if u % TPC == 0:
                            g.wait_ge(gx.h, gx.n)
                            g.dma_start(p.idx_scr[rc].rearrange("(q cc) -> q cc", cc=33),
                                        p.idx_fill.rearrange("(q cc) -> q cc", cc=33)
                                        ).then_inc(gx.h, 16)
                            gx.n += 16
                        g.wait_ge(vs.h, VSB(u) + 35 + (1 if u % TPC == 0 else 0))
                        for col in range(2):
                            g.wait_ge(gx.h, gx.n)
                            g.indirect_dma_start(
                                p.idx_scr.rearrange("a b -> (a b)")[:, None],
                                IndirectOffsetOnAxis(posi[:, (u % 2) * 2 + col:
                                                          (u % 2) * 2 + col + 1], 0),
                                p.tokid[:, u:u + 1], None).then_inc(gx.h, 16)
                            gx.n += 16

            assert ts.n == ts0 + ts_base[NTT]
            assert vs.n == vs0 + vs_base[NTT], (vs.n, vs0 + vs_base[NTT])
            assert ss.n == ss0 + ss_base[NTT]
            drain_dmas(p)

        with ExitStack() as ctx:
            en = ctx.enter_context
            idxs = en(nc.sbuf_tensor(f"mo{tag}_idxs", [128, 2 * NE * NTB], I32))
            xg = [en(nc.sbuf_tensor(f"mo{tag}_xg{i}", [128, NTB * 1032], BF16)) for i in range(2)]
            xgT = [en(nc.sbuf_tensor(f"mo{tag}_xgT", [128, ND * CAP], BF16))] * 2
            h_sb = en(nc.sbuf_tensor(f"mo{tag}_h", [128, NMT * CAP], BF16))
            y_sb = [en(nc.sbuf_tensor(f"mo{tag}_y", [128, NTB * D], F32))] * 2
            wq = [en(nc.sbuf_tensor(f"mo{tag}_wq{i}", [128, ND * D], BF16)) for i in range(2)]
            wo_sb = en(nc.sbuf_tensor(f"mo{tag}_wo", [128, NMT * D], BF16))
            ccol = en(nc.sbuf_tensor(f"mo{tag}_ccol", [128, 8], F32))
            ptT = [en(nc.psum_tensor(f"mo{tag}_qtT{i}", [128, 128], BF16)) for i in range(2)]
            pA = [en(nc.psum_tensor(f"mo{tag}_pA{i}", [128, 512], F32)) for i in range(2)]
            pB = [en(nc.psum_tensor(f"mo{tag}_pB{i}", [128, D], F32)) for i in range(2)]

            qtot = {}
            _q = [ld[2].n, ld[3].n]
            for _c in range(NRC):
                for _e in range(NE):
                    for _mq in range(4):
                        qg = _e * 4 + _mq
                        _q[qg % 2] += 16
                        qtot[(_c, _e, _mq)] = _q[qg % 2]
            idx0 = ld[5].n
            ini0 = st[3].n

            for c in range(NRC):
                ts0, vs0, ss0 = ts.n, vs.n, ss.n
                wotot = {}
                _w = ld[4].n
                for _e in range(NE):
                    _w += 16
                    wotot[_e] = _w
                NTR = NTB * 8
                EST = NTR + 32 + NTB
                SSE = NTR + 32

                def ts_tr(e, j):
                    return ts0 + EST * e + j + 1

                def ts_A(e, g):
                    return ts0 + EST * e + NTR + g + 1

                def ts_B(e, tb):
                    return ts0 + EST * e + NTR + 32 + tb + 1

                def ss_tr(e, j):
                    return ss0 + SSE * e + j + 1

                def ss_gelu(e, g):
                    return ss0 + SSE * e + NTR + g + 1

                def vs_y(e, tb):
                    return vs0 + 2 * NTB * e + 2 * tb + 2


                with nc.Block() as block:
                    @block.sync
                    def _(sync):
                        sync.wait_ge(p.gidx.h, p.gidx.n)
                        if c >= 1:
                            for _s in p.gsc:
                                sync.wait_ge(_s.h, _s.n)
                            sync.dma_start(rd[(c - 1) * RC:c * RC, :],
                                           p.moe_out[(c - 1) * RC:c * RC, :]).then_inc(st[3].h, 16)
                            st[3].n += 16
                        sync.dma_start(p.moe_out[c * RC:(c + 1) * RC, :],
                                       rs[c * RC:(c + 1) * RC, :]).then_inc(st[3].h, 16)
                        st[3].n += 16
                        p.mo_ini_tot[(tag, c)] = st[3].n
                        if c == 0:
                            for col in range(NE * NTB):
                                sync.dma_start(idxs[:, col:col + 1],
                                               p.idx_scr[0][col * 128:(col + 1) * 128][:, None]
                                               ).then_inc(ld[5].h, 16)
                                ld[5].n += 16
                        p.mo_idx_tot[(tag, c)] = ld[5].n
                        if c == 0:
                            sync.wait_ge(p.gc2.h, p.cast_moe_tot)
                        for e in range(NE):
                            for mq in range(4):
                                qg = e * 4 + mq
                                if qg >= 2:
                                    pe, pq = divmod(qg - 2, 4)
                                    sync.wait_ge(ts.h, ts_A(pe, pq * 8 + 7))
                                sync.dma_start(
                                    wq[qg % 2][:, :].rearrange("q (dt m) -> q dt m", dt=ND),
                                    p.win_bf[e][:, mq * D:(mq + 1) * D].rearrange(
                                        "(dt q) m -> q dt m", q=128)).then_inc(ld[(qg % 2) + 2].h, 16)
                                ld[(qg % 2) + 2].n += 16
                            if e >= 1:
                                sync.wait_ge(ts.h, ts_B(e - 1, NTB - 1))
                            sync.dma_start(
                                wo_sb[:, :].rearrange("q (mt d) -> q mt d", mt=NMT),
                                p.wout_bf[e].rearrange("(mt q) d -> q mt d", q=128)
                                ).then_inc(ld[4].h, 16)
                            ld[4].n += 16
                        if c + 1 < NRC:
                            for _s in p.gga:
                                sync.wait_ge(_s.h, _s.n)
                            so = ((c + 1) % 2) * NE * NTB
                            for col in range(NE * NTB):
                                sync.dma_start(idxs[:, so + col:so + col + 1],
                                               p.idx_scr[c + 1][col * 128:(col + 1) * 128][:, None]
                                               ).then_inc(ld[5].h, 16)
                                ld[5].n += 16
                            p.mo_idx_tot[(tag, c + 1)] = ld[5].n

                    @block.gpsimd
                    def _(g):
                        ssem = p.gsc[0]
                        if c == 0:
                            for _s in p.gga + p.gsc:
                                g.wait_ge(_s.h, _s.n)
                        g.wait_ge(ld[5].h, p.mo_idx_tot[(tag, c)])
                        g.wait_ge(st[3].h, p.mo_ini_tot[(tag, c)])
                        iso = (c % 2) * NE * NTB
                        for e in range(NE + 2):
                            if e >= 2:
                                se = e - 2
                                E = c * NE + se
                                for tb in range(NTB):
                                    g.wait_ge(vs.h, vs_y(se, tb))
                                    g.wait_ge(ssem.h, ssem.n)
                                    g.indirect_dma_start(
                                        p.moe_out[:, :],
                                        IndirectOffsetOnAxis(
                                            idxs[:, iso + se * NTB + tb: iso + se * NTB + tb + 1], 0),
                                        y_sb[se % 2][:, tb * D:(tb + 1) * D], None,
                                        compute_op=ALU.add).then_inc(ssem.h, 16)
                                    ssem.n += 16
                                p.mo_sc_tot[(tag, c, se)] = (0, ssem.n)
                            if e < NE:
                                E = c * NE + e
                                gsem = p.gga[E % 2]
                                if e >= 2:
                                    g.wait_ge(ts.h, ts_tr(e - 2, NTR - 1))
                                    g.wait_ge(vs.h, vs_y(e - 2, NTB - 1))
                                for tb in range(NTB):
                                    g.indirect_dma_start(
                                        xg[e % 2][:, tb * 1032:(tb + 1) * 1032], None,
                                        p.xtm_scr[:, :],
                                        IndirectOffsetOnAxis(
                                            idxs[:, iso + e * NTB + tb: iso + e * NTB + tb + 1], 0)
                                        ).then_inc(gsem.h, 16)
                                    gsem.n += 16
                                p.mo_ga_tot[(tag, c, e)] = (E % 2, gsem.n)

                    @block.tensor
                    def _(t):
                        for e in range(NE):
                            gsl, gtot = p.mo_ga_tot[(tag, c, e)]
                            t.wait_ge(p.gga[gsl].h, gtot)
                            for j in range(NTR):
                                tb, k = divmod(j, 8)
                                jj = EST * e + j
                                if jj >= 2:
                                    pe, pj = divmod(jj - 2, EST)
                                    if pj < NTR:
                                        t.wait_ge(ss.h, ss_tr(pe, pj))
                                t.transpose(ptT[jj % 2][:, :],
                                            xg[e % 2][:, tb * 1032 + k * 128: tb * 1032 + (k + 1) * 128],
                                            p.ident[:, :]).then_inc(ts.h, 1)
                                ts.n += 1
                            for mq in range(4):
                                t.wait_ge(ld[(e * 4 + mq) % 2 + 2].h, qtot[(c, e, mq)])
                                for mtl in range(8):
                                    g2 = mq * 8 + mtl
                                    if e * 32 + g2 >= 2:
                                        pe, pg = (e, g2 - 2) if g2 >= 2 else (e - 1, 30 + g2)
                                        t.wait_ge(ss.h, ss_gelu(pe, pg))
                                    if g2 == 0:
                                        t.wait_ge(ss.h, ss_tr(e, NTR - 1))
                                    for k in range(ND):
                                        mm = t.matmul(pA[g2 % 2][:, 0:CAP],
                                                      wq[(e * 4 + mq) % 2][:, k * D + mtl * 128:
                                                                           k * D + (mtl + 1) * 128],
                                                      xgT[e % 2][:, k * CAP:(k + 1) * CAP],
                                                      start=(k == 0), stop=(k == ND - 1))
                                    mm.then_inc(ts.h, 1); ts.n += 1
                            t.wait_ge(ld[4].h, wotot[e])
                            t.wait_ge(ss.h, ss_gelu(e, 31))
                            for tb in range(NTB):
                                if tb >= 2:
                                    t.wait_ge(vs.h, vs_y(e, tb - 2))
                                elif e >= 1:
                                    t.wait_ge(vs.h, vs_y(e - 1, NTB - 2 + tb) if NTB >= 2
                                              else vs_y(e - 1, tb))
                                for dh in range(2):
                                    mm = t.matmul(pB[tb % 2][:, dh * 512:(dh + 1) * 512],
                                                  p.ones_row[0:1, :],
                                                  p.bout_bf[0:1, e * D + dh * 512: e * D + dh * 512 + 512],
                                                  start=True, stop=False)
                                    for mt in range(NMT):
                                        mm = t.matmul(pB[tb % 2][:, dh * 512:(dh + 1) * 512],
                                                      h_sb[:, mt * CAP + tb * 128: mt * CAP + (tb + 1) * 128],
                                                      wo_sb[:, mt * D + dh * 512: mt * D + dh * 512 + 512],
                                                      start=False, stop=(mt == NMT - 1))
                                mm.then_inc(ts.h, 1); ts.n += 1

                    @block.scalar
                    def _(sc):
                        for e in range(NE):
                            for j in range(NTR):
                                tb, k = divmod(j, 8)
                                jj = EST * e + j
                                w = [(ts, ts_tr(e, j))]
                                if j == 0 and e >= 1:
                                    w.append((ts, ts_A(e - 1, 31)))
                                sop(sc, ss, lambda jj=jj, tb=tb, k=k, e=e: sc.copy(
                                    xgT[e % 2][:, k * CAP + tb * 128: k * CAP + (tb + 1) * 128],
                                    ptT[jj % 2][:, :]), waits=w)
                            for g2 in range(32):
                                sop(sc, ss, lambda e=e, g2=g2: sc.activation(
                                    h_sb[:, g2 * CAP:(g2 + 1) * CAP], pA[g2 % 2][:, 0:CAP], AF.Gelu,
                                    bias=p.bin_cols[:, e * 32 + g2: e * 32 + g2 + 1]),
                                    waits=[(ts, ts_A(e, g2))])

                    @block.vector
                    def _(v):
                        for e in range(NE):
                            gsl, gtot = p.mo_ga_tot[(tag, c, e)]
                            for tb in range(NTB):
                                sop(v, vs, lambda e=e, tb=tb: v.tensor_copy(
                                    ccol[:, tb:tb + 1],
                                    xg[e % 2][:, tb * 1032 + D + e: tb * 1032 + D + e + 1]),
                                    waits=[(p.gga[gsl], gtot)] if tb == 0 else [])
                                w = [(ts, ts_B(e, tb))]
                                E = c * NE + e
                                if E >= 1:
                                    pc, pe = divmod(E - 1, NE)
                                    ssl, stot = p.mo_sc_tot[(tag, pc, pe)]
                                    w.append((p.gsc[ssl], stot))
                                sop(v, vs, lambda e=e, tb=tb: v.tensor_scalar(
                                    y_sb[e % 2][:, tb * D:(tb + 1) * D], pB[tb % 2][:, :],
                                    ccol[:, tb:tb + 1], None,
                                    op0=ALU.mult), waits=w)

                assert ts.n == ts0 + EST * NE
                assert ss.n == ss0 + SSE * NE
                assert vs.n == vs0 + 2 * NTB * NE

            with nc.Block() as block:
                @block.sync
                def _(sync):
                    for _s in p.gsc:
                        sync.wait_ge(_s.h, _s.n)
                    sync.dma_start(rd[(NRC - 1) * RC:NRC * RC, :],
                                   p.moe_out[(NRC - 1) * RC:NRC * RC, :]).then_inc(st[3].h, 16)
                    st[3].n += 16
            drain_dmas(p)


def build_attn(p, isrc, idst):
    nc = p.nc
    B, T, NTOK = p.B, p.T, p.NTOK
    Tc = min(512, NTOK)
    ntt = Tc // 128
    ND = D // 128
    NTB = T // 128
    NH = 16
    HD = 64
    Wq = min(512, T)
    NQC = T // Wq
    nwt = Wq // 128
    rs, rd = p.resid[isrc], p.resid[idst]
    ts, vs, ss = p.ts, p.vs, p.ss
    ld, st = p.ld, p.st

    with ExitStack() as stage:
        sen = stage.enter_context
        v_sb = sen(nc.sbuf_tensor("at_v", [128, NTB * D], BF16))
        attnT = sen(nc.sbuf_tensor("at_attnT", [128, 8 * T], BF16))

        with ExitStack() as ctx:
            en = ctx.enter_context
            xch = [en(nc.sbuf_tensor(f"at_xch{i}", [128, ntt * D], F32)) for i in range(2)]
            xlnT = [en(nc.sbuf_tensor(f"at_xlnT{i}", [128, ND * Tc], BF16)) for i in range(2)]
            xtm = [en(nc.sbuf_tensor(f"at_xtm{i}", [128, D], BF16)) for i in range(2)]
            junk = [en(nc.sbuf_tensor("at_junk", [128, D], F32))] * 2
            stat = en(nc.sbuf_tensor("at_stat", [128, 5 * ntt], F32))
            qbuf = en(nc.sbuf_tensor("at_qbuf", [128, ND * Tc], BF16))
            kbuf = en(nc.sbuf_tensor("at_kbuf", [128, ND * Tc], BF16))
            wqk = en(nc.sbuf_tensor("at_wqk", [128, ND * 2 * D], BF16))
            wv = en(nc.sbuf_tensor("at_wv", [128, ND * D], BF16))
            ptT = [en(nc.psum_tensor(f"at_ptT{i}", [128, 128], BF16)) for i in range(2)]
            pQK = [en(nc.psum_tensor(f"at_pQK{i}", [128, 512], F32)) for i in range(2)]
            pV = [en(nc.psum_tensor(f"at_pV{i}", [128, D], F32)) for i in range(2)]
            NCHG = NTOK // Tc

            p.at_xch_ld = {}
            _lc = [ld[0].n, ld[1].n]
            for _c in range(NCHG):
                _lc[_c % 2] += 16
                p.at_xch_ld[_c] = _lc[_c % 2]

            R = {"xch": xch, "xlnT": xlnT, "xtm": xtm, "junk": junk, "stat": stat, "ptT": ptT,
                 "ntt": ntt, "xld": lambda c: (ld[c % 2], p.at_xch_ld[c])}

            def a1_load(sync, c):
                s2 = c % 2
                sync.wait_ge(st[s2].h, st[s2].n)
                sync.dma_start(xch[s2][:, :].rearrange("q (tt d) -> q tt d", tt=ntt),
                               rs[c * Tc:(c + 1) * Tc, :].rearrange("(tt q) d -> q tt d", q=128)
                               ).then_inc(ld[s2].h, 16)
                ld[s2].n += 16
                assert p.at_xch_ld[c] == ld[s2].n

            with nc.Block() as block:
                @block.sync
                def _(sync):
                    sync.wait_ge(p.gc3.h, p.cast_attn_tot)
                    sync.dma_start(wqk[:, :].rearrange("q (dt f) -> q dt f", dt=ND),
                                   p.qkv_bf[:, 0:2 * D].rearrange("(dt q) f -> q dt f", q=128)
                                   ).then_inc(ld[6].h, 16)
                    ld[6].n += 16
                    sync.dma_start(wv[:, :].rearrange("q (dt f) -> q dt f", dt=ND),
                                   p.qkv_bf[:, 2 * D:3 * D].rearrange("(dt q) f -> q dt f", q=128)
                                   ).then_inc(ld[6].h, 16)
                    ld[6].n += 16
                    a1_load(sync, 0)

            for c_glob in range(NCHG):
                b = (c_glob * Tc) // T
                t0 = c_glob * Tc - b * T
                s = c_glob % 2
                with nc.Block() as block:
                    ln0_ss_end = None
                    if c_glob == 0:
                        emit_ln(p, block, R, 0)
                        ln0_ss_end = ss.n

                    ts0, vs0, ss0 = ts.n, vs.n, ss.n
                    TS_QK0 = ts0
                    TS_V0 = TS_QK0 + 2 * ND
                    VS_E0 = vs0

                    @block.vector
                    def _(v):
                        for fq in range(ND):
                            sop(v, vs, lambda fq=fq: v.tensor_copy(
                                qbuf[:, fq * Tc:(fq + 1) * Tc], pQK[0][:, 0:Tc]),
                                waits=[(ts, TS_QK0 + 2 * fq + 1),
                                       (st[3], p.at_qst.get(c_glob - 1, 0))])
                            sop(v, vs, lambda fq=fq: v.tensor_copy(
                                kbuf[:, fq * Tc:(fq + 1) * Tc], pQK[1][:, 0:Tc]),
                                waits=[(ts, TS_QK0 + 2 * fq + 2)])
                        for tb in range(ntt):
                            sop(v, vs, lambda tb=tb: v.tensor_copy(
                                v_sb[:, (t0 // 128 + tb) * D:(t0 // 128 + tb + 1) * D],
                                pV[tb % 2][:, :]),
                                waits=[(ts, TS_V0 + tb + 1)])

                    @block.tensor
                    def _(t):
                        t.wait_ge(ld[6].h, ld[6].n)
                        if ln0_ss_end is not None:
                            t.wait_ge(ss.h, ln0_ss_end)
                        for fq in range(ND):
                            for qk in range(2):
                                g = 2 * fq + qk
                                if g >= 2:
                                    t.wait_ge(vs.h, VS_E0 + g - 1)
                                for k in range(ND):
                                    mm = t.matmul(pQK[g % 2][:, 0:Tc],
                                                  wqk[:, k * 2 * D + qk * D + fq * 128:
                                                      k * 2 * D + qk * D + (fq + 1) * 128],
                                                  xlnT[s][:, k * Tc:(k + 1) * Tc],
                                                  start=(k == 0), stop=(k == ND - 1))
                                mm.then_inc(ts.h, 1); ts.n += 1
                        for tb in range(ntt):
                            if tb >= 2:
                                t.wait_ge(vs.h, VS_E0 + 2 * ND + (tb - 2) + 1)
                            for k in range(ND):
                                for dh in range(2):
                                    mm = t.matmul(pV[tb % 2][:, dh * 512:(dh + 1) * 512],
                                                  xlnT[s][:, k * Tc + tb * 128: k * Tc + (tb + 1) * 128],
                                                  wv[:, k * D + dh * 512: k * D + dh * 512 + 512],
                                                  start=(k == 0), stop=(k == ND - 1))
                            mm.then_inc(ts.h, 1); ts.n += 1

                    assert ts.n == TS_V0 + ntt
                    assert vs.n == VS_E0 + 2 * ND + ntt

                    if c_glob + 1 < NCHG:
                        emit_ln(p, block, R, c_glob + 1, guard=(c_glob == 0))

                    @block.sync
                    def _(sync):
                        sync.wait_ge(vs.h, VS_E0 + 2 * ND)
                        sync.dma_start(
                            p.qT_scr[b][:, t0:t0 + Tc].rearrange("(ft q) t -> q ft t", q=128),
                            qbuf[:, :].rearrange("q (ft t) -> q ft t", ft=ND)).then_inc(st[3].h, 16)
                        st[3].n += 16
                        sync.dma_start(
                            p.kT_scr[b][:, t0:t0 + Tc].rearrange("(ft q) t -> q ft t", q=128),
                            kbuf[:, :].rearrange("q (ft t) -> q ft t", ft=ND)).then_inc(st[3].h, 16)
                        st[3].n += 16
                        if c_glob + 1 < NCHG:
                            a1_load(sync, c_glob + 1)
                p.at_qst[c_glob] = st[3].n
            drain_dmas(p)

        for b in range(B):
            with ExitStack() as ctx:
                en = ctx.enter_context
                qh = [en(nc.sbuf_tensor(f"at_qh{b}{i}", [128, Wq], BF16)) for i in range(2)]
                kh = [en(nc.sbuf_tensor(f"at_kh{b}{i}", [128, T], BF16)) for i in range(2)]
                esb = [en(nc.sbuf_tensor(f"at_esb{b}{i}", [128, 2 * Wq], BF16)) for i in range(2)]
                rsb = en(nc.sbuf_tensor(f"at_rsb{b}", [64, Wq], F32))
                pS = [en(nc.psum_tensor(f"at_pS{b}{i}", [128, 2 * Wq], F32)) for i in range(2)]
                pSum = en(nc.psum_tensor(f"at_pSum{b}", [64, Wq], F32))
                pPV = en(nc.psum_tensor(f"at_pPV{b}", [64, Wq], F32))

                _k45 = [ld[4].n, ld[5].n]
                for _h in range(NH):
                    _sl = 4 + (_h % 2) if _h >= 1 else 4
                    _k45[_sl - 4] += 16
                    p.at_khld[(b, _h)] = (_sl, _k45[_sl - 4])
                for h in range(NH):
                    ts0, vs0, ss0 = ts.n, vs.n, ss.n
                    ld2_0 = [ld[2].n, ld[3].n]
                    pb = (h % 2) * 64
                    fg = h // 2
                    nk_of = [(qc + 1) * nwt for qc in range(NQC)]
                    ts_pre = [0]
                    vs_pre = [0]
                    pg_pre = [0]
                    for qc in range(NQC):
                        ts_pre.append(ts_pre[-1] + 2 * nk_of[qc])
                        vs_pre.append(vs_pre[-1] + nwt + 2)
                        pg_pre.append(pg_pre[-1] + nk_of[qc] // 2)
                    pair_last = {}
                    for qc in range(NQC):
                        for pr in range(nk_of[qc] // 2):
                            pair_last[pg_pre[qc] + pr] = (qc, 2 * pr + 1)
                    tpos_S = {}
                    tpos_PV = {}

                    with nc.Block() as block:
                        @block.sync
                        def _(sync):
                            if h == 0:
                                sync.wait_ge(st[3].h, st[3].n)
                                sync.dma_start(kh[0][pb:pb + 64, :],
                                               p.kT_scr[b][h * 64:(h + 1) * 64, :]).then_inc(ld[4].h, 16)
                                ld[4].n += 16
                            if h + 1 < NH:
                                pb2 = ((h + 1) % 2) * 64
                                sm = ld[4 + (h + 1) % 2]
                                sync.dma_start(kh[(h + 1) % 2][pb2:pb2 + 64, :],
                                               p.kT_scr[b][(h + 1) * 64:(h + 2) * 64, :]
                                               ).then_inc(sm.h, 16)
                                sm.n += 16
                            for qc in range(NQC):
                                sl = qc % 2
                                if qc >= 2:
                                    sync.wait_ge(ts.h, ts0 + ts_pre[qc - 1])
                                sync.dma_start(
                                    qh[sl][pb:pb + 64, :],
                                    p.qT_scr[b][h * 64:(h + 1) * 64, qc * Wq:(qc + 1) * Wq]
                                    ).then_inc(ld[2 + sl].h, 16)
                                ld[2 + sl].n += 16

                        @block.tensor
                        def _(t):
                            for qc in range(NQC):
                                sl = qc % 2
                                t.wait_ge(ld[2 + sl].h, ld2_0[sl] + 16 * (qc // 2 + 1))
                                nk = nk_of[qc]
                                npr = nk // 2
                                if qc >= 1:
                                    t.wait_ge(vs.h, vs0 + vs_pre[qc])
                                if qc == 0:
                                    khs, khn = p.at_khld[(b, h)]
                                    t.wait_ge(ld[khs].h, khn)
                                for pr in range(npr + 1):
                                    if pr < npr:
                                        pg = pg_pre[qc] + pr
                                        if pg >= 2:
                                            t.wait_ge(ss.h, ss0 + pg - 1)
                                        for half in range(2):
                                            kt = 2 * pr + half
                                            mm = t.matmul(pS[pg % 2][:, half * Wq:(half + 1) * Wq],
                                                          kh[h % 2][pb:pb + 64, kt * 128:(kt + 1) * 128],
                                                          qh[sl][pb:pb + 64, :],
                                                          start=True, stop=True)
                                            mm.then_inc(ts.h, 1); ts.n += 1
                                            tpos_S[(qc, kt)] = ts.n
                                    if pr >= 1:
                                        ppr = pr - 1
                                        pg2 = pg_pre[qc] + ppr
                                        t.wait_ge(ss.h, ss0 + pg2 + 1)
                                        for half in range(2):
                                            kt = 2 * ppr + half
                                            t.matmul(pSum[0:64, :], p.ones_bf[:, 0:64],
                                                     esb[pg2 % 2][:, half * Wq:(half + 1) * Wq],
                                                     start=(kt == 0), stop=(kt == nk - 1))
                                            mm = t.matmul(pPV[0:64, :],
                                                          v_sb[:, kt * D + h * 64: kt * D + h * 64 + 64],
                                                          esb[pg2 % 2][:, half * Wq:(half + 1) * Wq],
                                                          start=(kt == 0), stop=(kt == nk - 1))
                                            mm.then_inc(ts.h, 1); ts.n += 1
                                            tpos_PV[(qc, kt)] = ts.n

                        @block.scalar
                        def _(sc):
                            for qc in range(NQC):
                                for pr in range(nk_of[qc] // 2):
                                    pg = pg_pre[qc] + pr
                                    w = [(ts, tpos_S[(qc, 2 * pr + 1)])]
                                    for half in range(2):
                                        kt = 2 * pr + half
                                        mp = kt - qc * nwt
                                        if mp >= 0:
                                            w.append((vs, vs0 + vs_pre[qc] + mp + 1))
                                    if pg >= 2:
                                        qc2, kt2 = pair_last[pg - 2]
                                        w.append((ts, tpos_PV[(qc2, kt2)]))
                                    sop(sc, ss, lambda pg=pg: sc.activation(
                                        esb[pg % 2][:, :], pS[pg % 2][:, :], AF.Exp, scale=0.125),
                                        waits=w)

                        @block.vector
                        def _(v):
                            for qc in range(NQC):
                                for mp in range(nwt):
                                    kt = qc * nwt + mp
                                    pg = pg_pre[qc] + kt // 2
                                    sop(v, vs, lambda pg=pg, kt=kt, mp=mp: v.tensor_tensor(
                                        pS[pg % 2][:, (kt % 2) * Wq:(kt % 2 + 1) * Wq],
                                        pS[pg % 2][:, (kt % 2) * Wq:(kt % 2 + 1) * Wq],
                                        p.masks[:, mp * 512: mp * 512 + Wq], op=ALU.add),
                                        waits=[(ts, tpos_S[(qc, kt)])])
                                sop(v, vs, lambda: v.reciprocal(rsb[0:64, :], pSum[0:64, :]),
                                    waits=[(ts, ts0 + ts_pre[qc + 1])])
                                sop(v, vs, lambda h=h, qc=qc: v.tensor_tensor(
                                    attnT[pb:pb + 64, fg * T + qc * Wq: fg * T + (qc + 1) * Wq],
                                    pPV[0:64, :], rsb[0:64, :], op=ALU.mult))

                    assert ts.n == ts0 + ts_pre[NQC]
                    assert ss.n == ss0 + pg_pre[NQC]
                    assert vs.n == vs0 + vs_pre[NQC]

            with ExitStack() as ctx:
                en = ctx.enter_context
                xt = [en(nc.sbuf_tensor(f"at_xt{b}{i}", [128, D], F32)) for i in range(2)]
                wo2 = en(nc.sbuf_tensor(f"at_wo2{b}", [128, ND * D], BF16))
                pO = [en(nc.psum_tensor(f"at_pO{b}{i}", [128, D], F32)) for i in range(2)]
                ts0, vs0 = ts.n, vs.n
                ld0s = [ld[0].n, ld[1].n]
                st0s = [st[0].n, st[1].n]
                with nc.Block() as block:
                    @block.sync
                    def _(sync):
                        sync.dma_start(wo2[:, :].rearrange("q (dt f) -> q dt f", dt=ND),
                                       p.wo_bf.rearrange("(dt q) f -> q dt f", q=128)
                                       ).then_inc(ld[6].h, 16)
                        ld[6].n += 16

                    @block.tensor
                    def _(t):
                        t.wait_ge(ld[6].h, ld[6].n)
                        for tt in range(NTB):
                            if tt >= 2:
                                t.wait_ge(vs.h, vs0 + (tt - 2) + 1)
                            for ft in range(ND):
                                for dh in range(2):
                                    mm = t.matmul(pO[tt % 2][:, dh * 512:(dh + 1) * 512],
                                                  attnT[:, ft * T + tt * 128: ft * T + (tt + 1) * 128],
                                                  wo2[:, ft * D + dh * 512: ft * D + dh * 512 + 512],
                                                  start=(ft == 0), stop=(ft == ND - 1))
                            mm.then_inc(ts.h, 1); ts.n += 1

                    @block.vector
                    def _(v):
                        for tt in range(NTB):
                            sl = tt % 2
                            sop(v, vs, lambda tt=tt, sl=sl: v.scalar_tensor_tensor(
                                xt[sl][:, :], pO[tt % 2][:, :], 1.0, xt[sl][:, :],
                                op0=ALU.mult, op1=ALU.add),
                                waits=[(ts, ts0 + tt + 1),
                                       (ld[sl], ld0s[sl] + 16 * (tt // 2 + 1))])

                    @block.sync
                    def _(sync):
                        for tt in range(NTB):
                            sl = tt % 2
                            if tt >= 2:
                                sync.wait_ge(vs.h, vs0 + (tt - 2) + 1)
                                sync.wait_ge(st[sl].h, st0s[sl] + 16 * (tt // 2))
                            sync.dma_start(
                                xt[sl][:, :],
                                rs[b * T + tt * 128: b * T + (tt + 1) * 128, :]).then_inc(ld[sl].h, 16)
                            ld[sl].n += 16
                            if tt >= 1:
                                psl = (tt - 1) % 2
                                sync.wait_ge(vs.h, vs0 + tt)
                                sync.dma_start(rd[b * T + (tt - 1) * 128: b * T + tt * 128, :],
                                               xt[psl][:, :]).then_inc(st[psl].h, 16)
                                st[psl].n += 16
                        sync.wait_ge(vs.h, vs0 + NTB)
                        lsl = (NTB - 1) % 2
                        sync.dma_start(rd[b * T + (NTB - 1) * 128: b * T + NTB * 128, :],
                                       xt[lsl][:, :]).then_inc(st[lsl].h, 16)
                        st[lsl].n += 16
        drain_dmas(p)



def kernel(**inputs):
    cfg = cfg_full()
    nc = build_program(cfg)
    x = np.ascontiguousarray(np.asarray(inputs["L0_attn_x"]).reshape(16 * 2048, 1024))
    conv_w = np.ascontiguousarray(np.asarray(inputs["L0_attn_conv_w"]).reshape(DI, KCONV).T)
    names = {
        "L0_attn_in_proj_w": inputs["L0_attn_in_proj_w"],
        "L0_attn_conv_w": conv_w,
        "L0_attn_conv_b": inputs["L0_attn_conv_b"],
        "L0_attn_x_proj_w": inputs["L0_attn_x_proj_w"],
        "L0_attn_dt_proj_w": inputs["L0_attn_dt_proj_w"],
        "L0_attn_dt_proj_b": inputs["L0_attn_dt_proj_b"],
        "L0_attn_A_log": inputs["L0_attn_A_log"],
        "L0_attn_D": inputs["L0_attn_D"],
        "L0_attn_out_proj_w": inputs["L0_attn_out_proj_w"],
        "L0_ffn_w_gate": inputs["L0_ffn_w_gate"],
        "L0_ffn_w_up": inputs["L0_ffn_w_up"],
        "L0_ffn_w_down": inputs["L0_ffn_w_down"],
        "L1_attn_w_qkv": inputs["L1_attn_w_qkv"],
        "L1_attn_w_out": inputs["L1_attn_w_out"],
        "L1_ffn_W_gate": inputs["L1_ffn_W_gate"],
        "L1_ffn_W_in": inputs["L1_ffn_W_in"],
        "L1_ffn_b_in": inputs["L1_ffn_b_in"],
        "L1_ffn_W_out": inputs["L1_ffn_W_out"],
        "L1_ffn_b_out": inputs["L1_ffn_b_out"],
    }
    names = {k: np.ascontiguousarray(np.asarray(v)) for k, v in names.items()}
    in_maps = []
    ntok = cfg["B"] * cfg["T"]
    for c in range(8):
        m = dict(names)
        m["x0"] = x[c * ntok:(c + 1) * ntok]
        in_maps.append(m)
    res = run_bass_kernel_spmd(nc, in_maps, core_ids=list(range(8)))
    out = np.concatenate([np.asarray(r["y"]) for r in res.results], axis=0)
    return out.reshape(16, 2048, 1024).astype(np.float32)


# version 11
# speedup vs baseline: 1.0303x; 1.0303x over previous
import numpy as np
import ml_dtypes
import concourse.bass as bass
import concourse.mybir as mybir
from concourse.bass_utils import run_bass_kernel_spmd
from concourse.bass import IndirectOffsetOnAxis
from contextlib import ExitStack

F32 = mybir.dt.float32
BF16 = mybir.dt.bfloat16
I32 = mybir.dt.int32
AF = mybir.ActivationFunctionType
ALU = mybir.AluOpType
AX = mybir.AxisListType

D = 1024
DI = 2048
DT = 64
KCONV = 4
H3 = 3072
DM = 4096
NE = 8
QSCALE = 1.0 / 127.0
EPS = 1e-5
NEG = -1e9


def cfg_full():
    return {"B": 2, "T": 2048}


class Sem:

    def __init__(self, nc, name):
        self.h = nc.alloc_semaphore(name=name)
        self.n = 0

    def inc(self, instr, k=1):
        instr.then_inc(self.h, k)
        self.n += k
        return self.n


class P:
    pass


def build_program(cfg, dbg=False, nstages=6):
    B, T = cfg["B"], cfg["T"]
    NTOK = B * T
    NTT = NTOK // 128
    assert T % 128 == 0

    nc = bass.Bass("TRN2", target_bir_lowering=False)
    p = P()
    p.nc = nc
    p.B, p.T, p.NTOK, p.NTT = B, T, NTOK, NTT

    ap = lambda t: t.ap()
    p.x_in = ap(nc.dram_tensor("x0", [NTOK, D], F32, kind="ExternalInput"))
    p.w_inproj = ap(nc.dram_tensor("L0_attn_in_proj_w", [2 * DI, D], F32, kind="ExternalInput"))
    p.w_conv = ap(nc.dram_tensor("L0_attn_conv_w", [KCONV, DI], F32, kind="ExternalInput"))
    p.b_conv = ap(nc.dram_tensor("L0_attn_conv_b", [DI], F32, kind="ExternalInput"))
    p.w_xproj = ap(nc.dram_tensor("L0_attn_x_proj_w", [DT, DI], F32, kind="ExternalInput"))
    p.w_dtproj = ap(nc.dram_tensor("L0_attn_dt_proj_w", [DI, DT], F32, kind="ExternalInput"))
    p.b_dtproj = ap(nc.dram_tensor("L0_attn_dt_proj_b", [DI], F32, kind="ExternalInput"))
    p.a_log = ap(nc.dram_tensor("L0_attn_A_log", [DI], F32, kind="ExternalInput"))
    p.d_vec = ap(nc.dram_tensor("L0_attn_D", [DI], F32, kind="ExternalInput"))
    p.w_outproj = ap(nc.dram_tensor("L0_attn_out_proj_w", [D, DI], F32, kind="ExternalInput"))
    p.w_gate_i = ap(nc.dram_tensor("L0_ffn_w_gate", [H3, D], I32, kind="ExternalInput"))
    p.w_up_i = ap(nc.dram_tensor("L0_ffn_w_up", [H3, D], I32, kind="ExternalInput"))
    p.w_down_i = ap(nc.dram_tensor("L0_ffn_w_down", [D, H3], I32, kind="ExternalInput"))
    p.w_qkv = ap(nc.dram_tensor("L1_attn_w_qkv", [D, 3 * D], F32, kind="ExternalInput"))
    p.w_wo = ap(nc.dram_tensor("L1_attn_w_out", [D, D], F32, kind="ExternalInput"))
    p.w_moe_gate = ap(nc.dram_tensor("L1_ffn_W_gate", [D, NE], F32, kind="ExternalInput"))
    p.w_moe_in = ap(nc.dram_tensor("L1_ffn_W_in", [NE, D, DM], F32, kind="ExternalInput"))
    p.b_moe_in = ap(nc.dram_tensor("L1_ffn_b_in", [NE, DM], F32, kind="ExternalInput"))
    p.w_moe_out = ap(nc.dram_tensor("L1_ffn_W_out", [NE, DM, D], F32, kind="ExternalInput"))
    p.b_moe_out = ap(nc.dram_tensor("L1_ffn_b_out", [NE, D], F32, kind="ExternalInput"))
    p.y_out = ap(nc.dram_tensor("y", [NTOK, D], F32, kind="ExternalOutput"))

    if dbg:
        dr = lambda name, shape, dt=BF16: nc.dram_tensor(name, shape, dt, kind="ExternalOutput").ap()
    else:
        dr = lambda name, shape, dt=BF16: nc.dram_tensor(name, shape, dt).ap()
    p.resid = [dr("resid_a", [NTOK, D], F32), dr("resid_b", [NTOK, D], F32)]
    p.l0_gu = dr("l0_gu", [H3 // 128, 128, 2 * D])
    p.l0_dT = dr("l0_dT", [H3, D])
    p.inprojT = dr("inprojT", [D, 2 * DI])
    p.outprojT = dr("outprojT", [DI, D])
    p.xprojT = dr("xprojT", [DI, DT])
    p.dtprojT = dr("dtprojT", [DT, DI])
    p.qkv_bf = dr("qkv_bf", [D, 3 * D])
    p.wo_bf = dr("wo_bf", [D, D])
    p.wgate_bf = dr("wgate_bf", [D, NE])
    p.win_bf = dr("win_bf", [NE, D, DM])
    p.wout_bf = dr("wout_bf", [NE, DM, D])
    p.xz_scr = dr("xz_scr", [B, 2 * DI, T], F32)
    p.xiact_f = dr("xiact_f", [B, DI, T], F32)
    p.xiact_b = dr("xiact_b", [B, DI, T])
    p.qT_scr = dr("qT_scr", [B, D, T])
    p.kT_scr = dr("kT_scr", [B, D, T])
    p.xlnT_scr = dr("xlnT_scr", [D, NTOK])
    p.xtm_scr = dr("xtm_scr", [NTOK + 1, 1032])
    p.moe_out = dr("moe_out", [NTOK + 1, D], F32)
    NRC_ = max(1, NTOK // 1024)
    p.idx_scr = dr("idx_scr", [NRC_, 4224], I32)
    p.idx_fill = dr("idx_fill", [4224], I32)
    if dbg:
        p.dbg_call = [dr(f"dbg_call{i}", [128, NTT * NE], F32) for i in range(2)]
        p.dbg_xln = dr("dbg_xln", [128, 8 * 512])
        p.dbg_h = dr("dbg_h", [128, 24 * 512])
        p.dbg_stat = dr("dbg_stat", [128, 20], F32)
        p.dbg_xch = dr("dbg_xch", [128, 4 * 1024], F32)
    p.dbg = dbg

    p.ts = Sem(nc, "ts")
    p.vs = Sem(nc, "vs")
    p.ss = Sem(nc, "ss")
    p.gs = Sem(nc, "gs")
    p.ld = [Sem(nc, f"ld{i}") for i in range(8)]
    p.st = [Sem(nc, f"st{i}") for i in range(4)]
    p.gld = [Sem(nc, f"gld{i}") for i in range(2)]
    p.gcast = Sem(nc, "gcast")
    p.gga = [Sem(nc, f"gga{i}") for i in range(2)]
    p.gidx = Sem(nc, "gidx")
    p.gc2 = Sem(nc, "gc2")
    p.gc3 = Sem(nc, "gc3")
    p.gsc = [Sem(nc, f"gsc{i}") for i in range(2)]

    p.stack = ExitStack()
    en = p.stack.enter_context
    p.ident = en(nc.sbuf_tensor("ident", [128, 128], BF16))
    p.masks = en(nc.sbuf_tensor("masks", [128, 4 * 512], BF16))
    p.ones_bf = en(nc.sbuf_tensor("ones_bf", [128, 64], BF16))
    p.ones_f1 = en(nc.sbuf_tensor("ones_f1", [1, 8], F32))
    p.bcols = en(nc.sbuf_tensor("bcols", [128, 128], F32))
    p.bin_cols = en(nc.sbuf_tensor("bin_cols", [128, 256], F32))
    p.bout_bf = en(nc.sbuf_tensor("bout_bf", [1, NE * D], BF16))
    p.wg_hi = en(nc.sbuf_tensor("wg_hi", [128, (D // 128) * NE], BF16))
    p.wg_lo = en(nc.sbuf_tensor("wg_lo", [128, (D // 128) * NE], BF16))
    p.ones_row = en(nc.sbuf_tensor("ones_row", [1, 128], BF16))
    p.tri = en(nc.sbuf_tensor("tri", [128, 128], BF16))
    p.ones_f32r = en(nc.sbuf_tensor("ones_f32r", [1, 128], F32))
    p.ebase = en(nc.sbuf_tensor("ebase", [1, NE], F32))
    p.tokid = en(nc.sbuf_tensor("tokid", [128, 64], I32))

    p.mb_dtmm = {}
    p.mo_ga_tot = {}
    p.mo_ini_tot = {}
    p.mo_idx_tot = {}
    p.mo_sc_tot = {}
    p.at_qst = {}
    p.at_khld = {}
    p.mb_strip_ld = {}
    p.mb_kt_done = {}
    build_prepass(p)
    stages = [
        lambda: build_ffn0(p, 0, 1, tag=0),
        lambda: build_mamba(p, 1, 0),
        lambda: build_ffn0(p, 0, 1, tag=1),
        lambda: build_moe(p, 1, 0, tag=0),
        lambda: build_attn(p, 0, 1),
        lambda: build_moe(p, 1, 0, final=True, tag=1),
    ]
    for f in stages[:nstages]:
        f()
    build_drain(p)
    p.stack.close()
    return nc


def build_drain(p):
    nc = p.nc
    with nc.Block() as block:
        @block.sync
        def _(sync):
            for s in p.ld + p.st + p.gld + p.gga + p.gsc + [p.gcast, p.gidx,
                                                            p.gc2, p.gc3]:
                sync.wait_ge(s.h, s.n)


def build_prepass(p):
    nc = p.nc
    ld, st = p.ld, p.st
    ts, vs, ss, gs = p.ts, p.vs, p.ss, p.gs
    gld = p.gld
    gcast = p.gcast

    NHT = H3 // 128
    ND = D // 128
    NDI = DI // 128

    with ExitStack() as ctx:
        en = ctx.enter_context
        wrow = [en(nc.sbuf_tensor(f"pp_wrow{i}", [128, H3], BF16)) for i in range(2)]
        strip = [en(nc.sbuf_tensor(f"pp_strip{i}", [128, 2 * D], BF16)) for i in range(2)]
        dstrips = en(nc.sbuf_tensor("pp_dstrips", [128, (NHT // 2) * D], BF16))
        istrips = en(nc.sbuf_tensor("pp_istrips", [128, ND * DI], BF16))
        ostrips = en(nc.sbuf_tensor("pp_ostrips", [128, NDI * D], BF16))
        xstrip = en(nc.sbuf_tensor("pp_xstrip", [128, NDI * DT], BF16))
        dtstrip = en(nc.sbuf_tensor("pp_dtstrip", [64, DI], BF16))
        rowbuf = en(nc.sbuf_tensor("pp_rowbuf", [1, DM // 2], F32))
        bout_rows = en(nc.sbuf_tensor("pp_bout_rows", [1, NE * D], F32))
        ptr = [en(nc.psum_tensor(f"pp_ptr{i}", [128, 128], BF16)) for i in range(2)]
        pb = en(nc.psum_tensor("pp_pb", [128, 512], F32))

        with nc.Block() as block:
            @block.gpsimd
            def _(g):
                g.dma_start(p.wgate_bf, p.w_moe_gate).then_inc(gcast.h, 16); gcast.n += 16
                p.cast_gate_tot = gcast.n
                g.dma_start(p.win_bf, p.w_moe_in).then_inc(p.gc2.h, 16); p.gc2.n += 16
                g.dma_start(p.wout_bf, p.w_moe_out).then_inc(p.gc2.h, 16); p.gc2.n += 16
                p.cast_moe_tot = p.gc2.n
                g.dma_start(p.qkv_bf, p.w_qkv).then_inc(p.gc3.h, 16); p.gc3.n += 16
                g.dma_start(p.wo_bf, p.w_wo).then_inc(p.gc3.h, 16); p.gc3.n += 16
                p.cast_attn_tot = p.gc3.n
                g.memset(p.ident[:], 0.0).then_inc(gs.h, 1); gs.n += 1
                g.memset(p.ones_bf[:], 1.0).then_inc(gs.h, 1); gs.n += 1
                g.memset(p.ones_row[:], 1.0).then_inc(gs.h, 1); gs.n += 1
                g.memset(p.ones_f1[:], 1.0).then_inc(gs.h, 1); gs.n += 1
                g.memset(p.masks[:], 0.0).then_inc(gs.h, 1); gs.n += 1
                g.memset(p.tri[:], 1.0).then_inc(gs.h, 1); gs.n += 1
                g.memset(p.ones_f32r[:], 1.0).then_inc(gs.h, 1); gs.n += 1
                g.iota(p.ebase[0:1, :], [[min(1024, p.NTOK) // 2, NE]], base=-1,
                       channel_multiplier=0,
                       allow_small_or_imprecise_dtypes=True).then_inc(gs.h, 1); gs.n += 1
                g.iota(p.tokid[:, 0:p.NTT], [[128, p.NTT]], base=0,
                       channel_multiplier=1).then_inc(gs.h, 1); gs.n += 1
                g.wait_ge(gs.h, gs.n)
                g.affine_select(out=p.ident[:], in_=p.ident[:], compare_op=ALU.not_equal,
                                fill=1.0, base=0, pattern=[[-1, 128]], channel_multiplier=1).then_inc(gs.h, 1)
                gs.n += 1
                g.affine_select(out=p.tri[:], in_=p.tri[:], compare_op=ALU.is_ge, fill=0.0,
                                base=0, pattern=[[1, 128]], channel_multiplier=-1).then_inc(gs.h, 1)
                gs.n += 1
                for mp in range(4):
                    g.affine_select(out=p.masks[:, mp * 512:(mp + 1) * 512],
                                    in_=p.masks[:, mp * 512:(mp + 1) * 512],
                                    compare_op=ALU.is_ge, fill=NEG, base=-mp * 128,
                                    pattern=[[1, 512]], channel_multiplier=-1).then_inc(gs.h, 1)
                    gs.n += 1

            @block.sync
            def _(sync):
                sync.dma_start(p.resid[0], p.x_in).then_inc(st[0].h, 16); st[0].n += 16
                sync.dma_start(bout_rows[0:1, :],
                               p.b_moe_out.rearrange("e d -> (e d)")[None, :]).then_inc(ld[1].h, 16)
                ld[1].n += 16

        rows = ([("bin", e * 2 + hh) for e in range(NE) for hh in range(2)]
                + [("vec", i) for i in range(4)] + [("cw", k) for k in range(KCONV)])
        vec_src = [p.b_conv, p.b_dtproj, p.a_log, p.d_vec]
        ts0, ld0 = ts.n, ld[0].n
        with nc.Block() as block:
            @block.sync
            def _(sync):
                for i, (kind, idx) in enumerate(rows):
                    if i >= 1:
                        sync.wait_ge(ts.h, ts0 + i)
                    if kind == "bin":
                        e2, hh = divmod(idx, 2)
                        sync.dma_start(rowbuf[0:1, 0:DM // 2],
                                       p.b_moe_in[e2, hh * (DM // 2):(hh + 1) * (DM // 2)][None, :]
                                       ).then_inc(ld[0].h, 16)
                    elif kind == "vec":
                        sync.dma_start(rowbuf[0:1, 0:DI], vec_src[idx][None, :]).then_inc(ld[0].h, 16)
                    else:
                        sync.dma_start(rowbuf[0:1, 0:DI], p.w_conv[idx, :][None, :]).then_inc(ld[0].h, 16)
                    ld[0].n += 16

            @block.tensor
            def _(t):
                for i, (kind, idx) in enumerate(rows):
                    t.wait_ge(ld[0].h, ld0 + 16 * (i + 1))
                    ncols = 16
                    for c in range(ncols):
                        if kind == "bin":
                            col = idx * 16 + c
                        elif kind == "vec":
                            col = 256 + idx * 16 + c
                        else:
                            col = 320 + idx * 16 + c
                        mm = t.matmul(pb[:, col:col + 1], rowbuf[0:1, c * 128:(c + 1) * 128],
                                      p.ones_f1[0:1, 0:1], start=True, stop=True)
                    mm.then_inc(ts.h, 1); ts.n += 1

            @block.vector
            def _(v):
                v.wait_ge(ts.h, ts.n)
                v.tensor_copy(p.bin_cols[:], pb[:, 0:256]).then_inc(vs.h, 1); vs.n += 1
                v.tensor_copy(p.bcols[:], pb[:, 256:384]).then_inc(vs.h, 1); vs.n += 1
                v.wait_ge(ss.h, ss.n + 1)
                v.tensor_scalar(p.bcols[:, 32:48], p.bcols[:, 32:48], 10.0, 1e-6,
                                op0=ALU.min, op1=ALU.max).then_inc(vs.h, 1); vs.n += 1
                v.wait_ge(vs.h, vs.n)
                v.tensor_scalar(p.bcols[:, 32:48], p.bcols[:, 32:48], -1.0, None,
                                op0=ALU.mult).then_inc(vs.h, 1); vs.n += 1

            vsnap = vs.n - 2

            @block.scalar
            def _(sc):
                sc.wait_ge(vs.h, vsnap)
                sc.activation(p.bcols[:, 32:48], p.bcols[:, 32:48], AF.Exp).then_inc(ss.h, 1); ss.n += 1

        with ExitStack() as ctx2:
            wgf = ctx2.enter_context(nc.sbuf_tensor("pp_wgf", [128, (D // 128) * NE], F32))
            fillb = ctx2.enter_context(nc.sbuf_tensor("pp_fillb", [128, 33], I32))
            zrow = ctx2.enter_context(nc.sbuf_tensor("pp_zrow", [1, 1032], BF16))
            with nc.Block() as block:
                @block.gpsimd
                def _(g):
                    g.iota(fillb[:, :], [[0, 33]], base=p.NTOK,
                           channel_multiplier=0).then_inc(gs.h, 1); gs.n += 1
                    g.memset(zrow[0:1, :], 0.0).then_inc(gs.h, 1); gs.n += 1
                    g.wait_ge(gs.h, gs.n)
                    g.dma_start(p.idx_fill.rearrange("(q cc) -> q cc", cc=33),
                                fillb[:, :]).then_inc(p.gld[0].h, 16)
                    p.gld[0].n += 16
                    g.dma_start(p.xtm_scr[p.NTOK:p.NTOK + 1, :], zrow[0:1, :]).then_inc(p.gld[0].h, 16)
                    p.gld[0].n += 16
                @block.sync
                def _(sync):
                    sync.dma_start(wgf[:, :].rearrange("q (dt e) -> q dt e", dt=D // 128),
                                   p.w_moe_gate.rearrange("(dt q) e -> q dt e", q=128)
                                   ).then_inc(ld[2].h, 16)
                    ld[2].n += 16
                    sync.wait_ge(gcast.h, p.cast_gate_tot)
                    sync.dma_start(p.wg_hi[:, :].rearrange("q (dt e) -> q dt e", dt=D // 128),
                                   p.wgate_bf.rearrange("(dt q) e -> q dt e", q=128)
                                   ).then_inc(ld[3].h, 16)
                    ld[3].n += 16

                @block.vector
                def _(v):
                    sop(v, vs, lambda: v.tensor_tensor(p.wg_lo[:, :], wgf[:, :], p.wg_hi[:, :],
                                                       op=ALU.subtract),
                        waits=[(ld[2], ld[2].n), (ld[3], ld[3].n)])
                    sop(v, vs, lambda: v.tensor_copy(p.bout_bf[0:1, :], bout_rows[0:1, :]),
                        waits=[(ld[1], ld[1].n)])
            drain_dmas(p)

        units = []
        for ht in range(NHT):
            units.append(("gu", ht, 2, 2 * ND))
        for half in range(2):
            for dt_i in range(ND):
                units.append(("wd", half * ND + dt_i, 1, NHT // 2))
        for half in range(2):
            for ft in range(2 * ND):
                units.append(("ip", half * 2 * ND + ft, 1, ND))
        for dt_i in range(ND):
            units.append(("op", dt_i, 1, NDI))
        units.append(("xp", 0, 1, NDI))
        for kt in range(NDI):
            units.append(("dp", kt, 1, 1))

        ts_base, vs_base = ts.n, vs.n
        gld_tot = [gld[0].n, gld[1].n]
        unit_gld = []
        unit_tcum = []
        tc = 0
        for i, (kind, idx, nl, nt) in enumerate(units):
            slot = i % 2
            gld_tot[slot] += 16 * nl
            unit_gld.append((slot, gld_tot[slot]))
            tc += nt
            unit_tcum.append(tc)
        NIP = 2 * ND

        def stores_after(i):
            kind, idx, _, _ = units[i]
            out = []
            if kind == "gu":
                out.append((p.l0_gu[idx, :, :], strip[idx % 2][:, :]))
            elif kind == "wd" and idx % ND == ND - 1:
                half = idx // ND
                out.append((p.l0_dT[half * (NHT // 2) * 128:(half + 1) * (NHT // 2) * 128, :]
                            .rearrange("(ht q) d -> q ht d", q=128),
                            dstrips[:, :].rearrange("q (ht d) -> q ht d", ht=NHT // 2)))
            elif kind == "ip" and idx == NIP - 1:
                out.append((p.inprojT[:, 0:DI].rearrange("(dt q) f -> q dt f", q=128),
                            istrips[:, :].rearrange("q (dt f) -> q dt f", dt=ND)))
            elif kind == "ip" and idx == 2 * NIP - 1:
                out.append((p.inprojT[:, DI:2 * DI].rearrange("(dt q) f -> q dt f", q=128),
                            istrips[:, :].rearrange("q (dt f) -> q dt f", dt=ND)))
            elif kind == "op" and idx == ND - 1:
                out.append((p.outprojT.rearrange("(kt q) d -> q kt d", q=128),
                            ostrips[:, :].rearrange("q (kt d) -> q kt d", kt=NDI)))
            elif kind == "xp":
                out.append((p.xprojT.rearrange("(kt q) r -> q kt r", q=128),
                            xstrip[:, :].rearrange("q (kt r) -> q kt r", kt=NDI)))
            elif kind == "dp" and idx == NDI - 1:
                out.append((p.dtprojT[:, :], dtstrip[0:64, :]))
            return out

        st_plan = {}
        st_tot = {j: p.st[j].n for j in range(4)}
        sidx = 0
        for i in range(len(units)):
            for _ in stores_after(i):
                sl = sidx % 4
                st_tot[sl] += 16
                st_plan.setdefault(i, []).append((sl, st_tot[sl]))
                sidx += 1

        with nc.Block() as block:
            @block.gpsimd
            def _(g):
                for i, (kind, idx, nl, nt) in enumerate(units):
                    slot, tot = unit_gld[i]
                    buf = wrow[slot]
                    if i >= 2:
                        g.wait_ge(ts.h, ts_base + unit_tcum[i - 2])
                    if kind == "gu":
                        g.dma_start(buf[:, 0:D], p.w_gate_i[idx * 128:(idx + 1) * 128, :]).then_inc(gld[slot].h, 16)
                        g.dma_start(buf[:, D:2 * D], p.w_up_i[idx * 128:(idx + 1) * 128, :]).then_inc(gld[slot].h, 16)
                    elif kind == "wd":
                        g.dma_start(buf[:, 0:H3], p.w_down_i[(idx % ND) * 128:(idx % ND + 1) * 128, :]).then_inc(gld[slot].h, 16)
                    elif kind == "ip":
                        g.dma_start(buf[:, 0:D], p.w_inproj[idx * 128:(idx + 1) * 128, :]).then_inc(gld[slot].h, 16)
                    elif kind == "op":
                        g.dma_start(buf[:, 0:DI], p.w_outproj[idx * 128:(idx + 1) * 128, :]).then_inc(gld[slot].h, 16)
                    elif kind == "xp":
                        g.dma_start(buf[0:64, 0:DI], p.w_xproj[:, :]).then_inc(gld[slot].h, 16)
                    elif kind == "dp":
                        g.dma_start(buf[:, 0:DT], p.w_dtproj[idx * 128:(idx + 1) * 128, :]).then_inc(gld[slot].h, 16)
                    gld[slot].n += 16

            @block.tensor
            def _(t):
                j = 0

                def tr(in_ap, out_ap):
                    nonlocal j
                    if j >= 2:
                        t.wait_ge(vs.h, vs_base + j - 1)
                    t.transpose(out_ap, in_ap, p.ident[0:in_ap.shape[0], 0:in_ap.shape[0]]).then_inc(ts.h, 1)
                    ts.n += 1
                    j += 1

                for i, (kind, idx, nl, nt) in enumerate(units):
                    slot, tot = unit_gld[i]
                    buf = wrow[slot]
                    t.wait_ge(gld[slot].h, tot)
                    if kind == "gu":
                        for gu in range(2):
                            for dt_i in range(ND):
                                tr(buf[:, gu * D + dt_i * 128: gu * D + (dt_i + 1) * 128], ptr[j % 2][:, :])
                    elif kind == "wd":
                        half, dt2 = divmod(idx, ND)
                        for ht in range(NHT // 2):
                            tr(buf[:, (half * (NHT // 2) + ht) * 128:(half * (NHT // 2) + ht + 1) * 128],
                               ptr[j % 2][:, :])
                    elif kind == "ip":
                        for dt_i in range(ND):
                            tr(buf[:, dt_i * 128:(dt_i + 1) * 128], ptr[j % 2][:, :])
                    elif kind == "op":
                        for kt in range(NDI):
                            tr(buf[:, kt * 128:(kt + 1) * 128], ptr[j % 2][:, :])
                    elif kind == "xp":
                        for kt in range(NDI):
                            tr(buf[0:64, kt * 128:(kt + 1) * 128], ptr[j % 2][0:128, 0:64])
                    elif kind == "dp":
                        tr(buf[:, 0:DT], ptr[j % 2][0:64, 0:128])

            @block.vector
            def _(v):
                j = 0

                def ev(dst_ap, src_ap, i):
                    nonlocal j
                    v.wait_ge(ts.h, ts_base + j + 1)
                    v.tensor_copy(dst_ap, src_ap).then_inc(vs.h, 1); vs.n += 1
                    j += 1

                for i, (kind, idx, nl, nt) in enumerate(units):
                    if kind == "gu":
                        if idx >= 2 and (idx - 2) in st_plan:
                            for sl, tot in st_plan[idx - 2]:
                                v.wait_ge(p.st[sl].h, tot)
                        for gu in range(2):
                            for dt_i in range(ND):
                                ev(strip[idx % 2][:, gu * D + dt_i * 128: gu * D + (dt_i + 1) * 128],
                                   ptr[j % 2][:, :], i)
                    elif kind == "wd":
                        half, dt2 = divmod(idx, ND)
                        if dt2 == 0 and half == 1:
                            for ui, sts in st_plan.items():
                                if units[ui][0] == "wd" and units[ui][1] == ND - 1:
                                    for slx, tot in sts:
                                        v.wait_ge(p.st[slx].h, tot)
                        for ht in range(NHT // 2):
                            ev(dstrips[:, ht * D + dt2 * 128: ht * D + (dt2 + 1) * 128], ptr[j % 2][:, :], i)
                    elif kind == "ip":
                        lidx = idx % NIP
                        if lidx == 0 and idx >= NIP:
                            for ui, sts in st_plan.items():
                                if units[ui][0] == "ip" and units[ui][1] == NIP - 1:
                                    for sl, tot in sts:
                                        v.wait_ge(p.st[sl].h, tot)
                        for dt_i in range(ND):
                            ev(istrips[:, dt_i * DI + lidx * 128: dt_i * DI + (lidx + 1) * 128],
                               ptr[j % 2][:, :], i)
                    elif kind == "op":
                        for kt in range(NDI):
                            ev(ostrips[:, kt * D + idx * 128: kt * D + (idx + 1) * 128], ptr[j % 2][:, :], i)
                    elif kind == "xp":
                        for kt in range(NDI):
                            ev(xstrip[:, kt * DT:(kt + 1) * DT], ptr[j % 2][0:128, 0:64], i)
                    elif kind == "dp":
                        ev(dtstrip[0:64, idx * 128:(idx + 1) * 128], ptr[j % 2][0:64, 0:128], i)

            @block.sync
            def _(sync):
                for i in range(len(units)):
                    if i in st_plan:
                        sync.wait_ge(vs.h, vs_base + unit_tcum[i])
                        for (sl, tot), (dram_ap, sb_ap) in zip(st_plan[i], stores_after(i)):
                            sync.dma_start(dram_ap, sb_ap).then_inc(p.st[sl].h, 16)
                            p.st[sl].n += 16
                sync.wait_ge(gcast.h, gcast.n)
                for s in p.st:
                    sync.wait_ge(s.h, s.n)
                for s in (p.ld[0], p.ld[1]):
                    sync.wait_ge(s.h, s.n)


def drain_dmas(p):
    nc = p.nc
    with nc.Block() as block:
        @block.sync
        def _(sync):
            for s in p.ld + p.st + p.gld + p.gga + p.gsc + [p.gidx]:
                sync.wait_ge(s.h, s.n)


def sop(eng, sem, emit_fn, waits=()):
    for (s, n) in waits:
        eng.wait_ge(s.h, n)
    eng.wait_ge(sem.h, sem.n)
    instr = emit_fn()
    sem.inc(instr)
    return sem.n


def emit_ln(p, block, R, c, guard=False):
    nc = p.nc
    ts, vs, ss = p.ts, p.vs, p.ss
    ntt = R["ntt"]
    ND = D // 128
    Tc = ntt * 128
    s = c % 2
    xch, xlnT, xtm, junk, stat, ptT = (R[k] for k in ("xch", "xlnT", "xtm", "junk", "stat", "ptT"))
    SUM, SQ, MC, VAR, RSTD = (stat[:, i * ntt:(i + 1) * ntt] for i in range(5))
    xsem, xtot = R["xld"](c)
    ts0, vs0, ss0 = ts.n, vs.n, ss.n
    TS_TR0 = ts0
    SS_TR0 = ss0 + ntt + 2
    VS_N0 = vs0 + ntt + 4

    @block.scalar
    def _(sc):
        for tt in range(ntt):
            w = [(xsem, xtot)]
            if guard and tt == 0:
                w.append((vs, vs0))
            sop(sc, ss, lambda tt=tt: sc.activation(
                junk[tt % 2][:, :], xch[s][:, tt * D:(tt + 1) * D], AF.Square,
                accum_out=SQ[:, tt:tt + 1]), waits=w)
        sop(sc, ss, lambda: sc.activation(RSTD[:, :], VAR[:, :], AF.Ln),
            waits=[(vs, vs0 + ntt + 4)])
        sop(sc, ss, lambda: sc.activation(RSTD[:, :], RSTD[:, :], AF.Exp, scale=-0.5))
        for j in range(8 * ntt):
            tt, k = divmod(j, ND)
            sop(sc, ss, lambda j=j, tt=tt, k=k: sc.copy(
                xlnT[s][:, k * Tc + tt * 128: k * Tc + (tt + 1) * 128], ptT[j % 2][:, :]),
                waits=[(ts, ts0 + j + 1)])

    @block.vector
    def _(v):
        for tt in range(ntt):
            w = [(xsem, xtot)]
            if guard and tt == 0:
                w.append((ss, ss0))
            sop(v, vs, lambda tt=tt: v.tensor_reduce(
                SUM[:, tt:tt + 1], xch[s][:, tt * D:(tt + 1) * D], axis=AX.X, op=ALU.add), waits=w)
        sop(v, vs, lambda: v.tensor_scalar(MC[:, :], SUM[:, :], 1.0 / D, None, op0=ALU.mult))
        sop(v, vs, lambda: v.tensor_tensor(VAR[:, :], MC[:, :], MC[:, :], op=ALU.mult))
        sop(v, vs, lambda: v.scalar_tensor_tensor(
            VAR[:, :], SQ[:, :], 1.0 / D, VAR[:, :], op0=ALU.mult, op1=ALU.subtract),
            waits=[(ss, ss0 + ntt)])
        sop(v, vs, lambda: v.tensor_scalar(VAR[:, :], VAR[:, :], EPS, None, op0=ALU.add))
        for tt in range(ntt):
            w = [(ss, ss0 + ntt + 2)]
            if tt >= 2:
                w.append((ts, ts0 + 8 * (tt - 1)))
            elif guard:
                w.append((ts, ts0))
            sop(v, vs, lambda tt=tt: v.tensor_scalar(
                xtm[tt % 2][:, :], xch[s][:, tt * D:(tt + 1) * D],
                MC[:, tt:tt + 1], RSTD[:, tt:tt + 1], op0=ALU.subtract, op1=ALU.mult),
                waits=w)

    @block.tensor
    def _(t):
        for j in range(8 * ntt):
            tt, k = divmod(j, ND)
            t.wait_ge(vs.h, VS_N0 + tt + 1)
            if j >= 2:
                t.wait_ge(ss.h, SS_TR0 + (j - 2) + 1)
            t.transpose(ptT[j % 2][:, :], xtm[tt % 2][:, k * 128:(k + 1) * 128],
                        p.ident[:, :]).then_inc(ts.h, 1)
            ts.n += 1

    assert ts.n == ts0 + 8 * ntt
    assert ss.n == SS_TR0 + 8 * ntt
    assert vs.n == VS_N0 + ntt
    return ts0, vs0, ss0


def build_ffn0(p, isrc, idst, tag=0):
    nc = p.nc
    NTOK, NTT = p.NTOK, p.NTT
    Tc = min(512, NTOK)
    ntt = Tc // 128
    NCH = NTOK // Tc
    NHT = H3 // 128
    ND = D // 128
    rs, rd = p.resid[isrc], p.resid[idst]
    ts, vs, ss = p.ts, p.vs, p.ss
    ld, st = p.ld, p.st

    with ExitStack() as ctx:
        en = ctx.enter_context
        xch = [en(nc.sbuf_tensor(f"f0{tag}_xch{i}", [128, ntt * D], F32)) for i in range(2)]
        wdT = en(nc.sbuf_tensor(f"f0{tag}_wdT", [128, NHT * D], BF16))
        slab = [en(nc.sbuf_tensor(f"f0{tag}_slab{i}", [128, 2 * D], BF16)) for i in range(4)]
        xlnT = [en(nc.sbuf_tensor(f"f0{tag}_xlnT{i}", [128, ND * Tc], BF16)) for i in range(2)]
        h_sb = en(nc.sbuf_tensor(f"f0{tag}_h", [128, NHT * Tc], BF16))
        xtm = [en(nc.sbuf_tensor(f"f0{tag}_xtm{i}", [128, D], BF16)) for i in range(2)]
        junk = [en(nc.sbuf_tensor(f"f0{tag}_junk", [128, D], F32))] * 2
        sg = [en(nc.sbuf_tensor(f"f0{tag}_sg{i}", [128, Tc], BF16)) for i in range(2)]
        su = [en(nc.sbuf_tensor(f"f0{tag}_su{i}", [128, Tc], BF16)) for i in range(2)]
        stat = en(nc.sbuf_tensor(f"f0{tag}_stat", [128, 5 * ntt], F32))
        ptT = [en(nc.psum_tensor(f"f0{tag}_ptT{i}", [128, 128], BF16)) for i in range(2)]
        ptA = [en(nc.psum_tensor(f"f0{tag}_ptA{i}", [128, 512], F32)) for i in range(2)]
        ptB = [en(nc.psum_tensor(f"f0{tag}_ptB{i}", [128, D], F32)) for i in range(2)]

        p.f0_xch_ld = getattr(p, "f0_xch_ld", {})
        _lc = [ld[0].n, ld[1].n]
        for _c in range(NCH):
            _lc[_c % 2] += 16
            p.f0_xch_ld[(tag, _c)] = _lc[_c % 2]
        p.f0_ts_after_A = {}
        p.f0_slab_tot = {}
        _cnt = [ld[2 + i].n for i in range(4)]
        for _c in range(NCH):
            for _ht in range(NHT):
                _sl = _ht % 4
                _cnt[_sl] += 16
                p.f0_slab_tot[(_c, _ht)] = _cnt[_sl]

        R = {"xch": xch, "xlnT": xlnT, "xtm": xtm, "junk": junk, "stat": stat, "ptT": ptT,
             "ntt": ntt, "xld": lambda c: (ld[c % 2], p.f0_xch_ld[(tag, c)])}

        def slab_load(sync, c, ht, in_block):
            sl = ht % 4
            if in_block:
                sync.wait_ge(ts.h, p.f0_ts_after_A[(c, ht - 4)])
            elif c > 0:
                sync.wait_ge(ts.h, p.f0_ts_after_A[(c - 1, NHT - 4 + sl)])
            sync.dma_start(slab[sl][:, :], p.l0_gu[ht, :, :]).then_inc(ld[2 + sl].h, 16)
            ld[2 + sl].n += 16
            assert p.f0_slab_tot[(c, ht)] == ld[2 + sl].n

        def emit_loads(sync, c):
            s = c % 2
            sync.wait_ge(st[s].h, st[s].n)
            sync.dma_start(xch[s][:, :].rearrange("q (tt d) -> q tt d", tt=ntt),
                           rs[c * Tc:(c + 1) * Tc, :].rearrange("(tt q) d -> q tt d", q=128)
                           ).then_inc(ld[s].h, 16)
            ld[s].n += 16
            assert p.f0_xch_ld[(tag, c)] == ld[s].n
            for ht in range(4):
                slab_load(sync, c, ht, in_block=False)

        with nc.Block() as block:
            @block.sync
            def _(sync):
                sync.dma_start(wdT[:, :].rearrange("q (ht d) -> q ht d", ht=NHT),
                               p.l0_dT.rearrange("(ht q) d -> q ht d", q=128)).then_inc(ld[6].h, 16)
                ld[6].n += 16
                emit_loads(sync, 0)

        for c in range(NCH):
            s = c % 2
            with nc.Block() as block:
                ln0_ss_end = None
                if c == 0:
                    emit_ln(p, block, R, 0)
                    ln0_ss_end = ss.n

                ts0, vs0, ss0 = ts.n, vs.n, ss.n
                TS_A0 = ts0
                TS_B0 = TS_A0 + 2 * NHT
                SS_A0 = ss0
                VS_A0 = vs0
                VS_B0 = VS_A0 + 2 * NHT
                for ht in range(NHT):
                    p.f0_ts_after_A[(c, ht)] = TS_A0 + 2 * ht + 2

                @block.scalar
                def _(sc):
                    for ht in range(NHT):
                        w = [(ts, TS_A0 + 2 * ht + 1)]
                        if ht >= 2:
                            w.append((vs, VS_A0 + 2 * (ht - 2) + 2))
                        sop(sc, ss, lambda ht=ht: sc.activation(
                            sg[ht % 2][:, 0:Tc], ptA[0][:, 0:Tc], AF.Silu, scale=QSCALE), waits=w)

                @block.vector
                def _(v):
                    for ht in range(NHT):
                        sop(v, vs, lambda ht=ht: v.tensor_copy(su[ht % 2][:, 0:Tc], ptA[1][:, 0:Tc]),
                            waits=[(ts, TS_A0 + 2 * ht + 2)])
                        sop(v, vs, lambda ht=ht: v.tensor_tensor(
                            h_sb[:, ht * Tc:(ht + 1) * Tc], sg[ht % 2][:, 0:Tc], su[ht % 2][:, 0:Tc],
                            op=ALU.mult),
                            waits=[(ss, SS_A0 + ht + 1)])
                    for tb in range(ntt):
                        sop(v, vs, lambda tb=tb: v.scalar_tensor_tensor(
                            xch[s][:, tb * D:(tb + 1) * D], ptB[tb % 2][:, :],
                            0.5 * QSCALE * QSCALE, xch[s][:, tb * D:(tb + 1) * D],
                            op0=ALU.mult, op1=ALU.add),
                            waits=[(ts, TS_B0 + tb + 1)])

                @block.tensor
                def _(t):
                    if ln0_ss_end is not None:
                        t.wait_ge(ss.h, ln0_ss_end)
                    for ht in range(NHT):
                        sl = ht % 4
                        t.wait_ge(ld[2 + sl].h, p.f0_slab_tot[(c, ht)])
                        for gu in range(2):
                            if ht >= 1:
                                if gu == 0:
                                    t.wait_ge(ss.h, SS_A0 + (ht - 1) + 1)
                                else:
                                    t.wait_ge(vs.h, VS_A0 + 2 * (ht - 1) + 1)
                            for k in range(ND):
                                mm = t.matmul(ptA[gu][:, 0:Tc],
                                              slab[sl][:, gu * D + k * 128: gu * D + (k + 1) * 128],
                                              xlnT[s][:, k * Tc:(k + 1) * Tc],
                                              start=(k == 0), stop=(k == ND - 1))
                            mm.then_inc(ts.h, 1); ts.n += 1
                    for tb in range(ntt):
                        t.wait_ge(vs.h, VS_A0 + 2 * NHT)
                        if tb >= 2:
                            t.wait_ge(vs.h, VS_B0 + (tb - 2) + 1)
                        t.wait_ge(ld[6].h, ld[6].n)
                        for ht in range(NHT):
                            for dh in range(2):
                                mm = t.matmul(ptB[tb % 2][:, dh * 512:(dh + 1) * 512],
                                              h_sb[:, ht * Tc + tb * 128: ht * Tc + (tb + 1) * 128],
                                              wdT[:, ht * D + dh * 512: ht * D + dh * 512 + 512],
                                              start=(ht == 0), stop=(ht == NHT - 1))
                        mm.then_inc(ts.h, 1); ts.n += 1

                assert ts.n == TS_B0 + ntt
                assert ss.n == SS_A0 + NHT
                assert vs.n == VS_B0 + ntt

                if c + 1 < NCH:
                    emit_ln(p, block, R, c + 1, guard=(c == 0))

                @block.sync
                def _(sync):
                    for ht in range(4, NHT):
                        slab_load(sync, c, ht, in_block=True)
                    sync.wait_ge(vs.h, VS_B0 + ntt)
                    sync.dma_start(rd[c * Tc:(c + 1) * Tc, :].rearrange("(tt q) d -> q tt d", q=128),
                                   xch[s][:, :].rearrange("q (tt d) -> q tt d", tt=ntt)
                                   ).then_inc(st[s].h, 16)
                    st[s].n += 16
                    if c + 1 < NCH:
                        emit_loads(sync, c + 1)
        drain_dmas(p)



def build_mamba(p, isrc, idst):
    nc = p.nc
    B, T, NTOK = p.B, p.T, p.NTOK
    Tc = min(512, NTOK)
    ntt = Tc // 128
    NCH = NTOK // Tc
    ND = D // 128
    NDI = DI // 128
    NF = 2 * DI // 128
    NTB = T // 128
    rs, rd = p.resid[isrc], p.resid[idst]
    ts, vs, ss = p.ts, p.vs, p.ss
    ld, st = p.ld, p.st

    CW = lambda k, kt: p.bcols[:, 64 + k * 16 + kt: 64 + k * 16 + kt + 1]
    CB = lambda kt: p.bcols[:, kt:kt + 1]
    DTB = lambda kt: p.bcols[:, 16 + kt:16 + kt + 1]
    ACOL = lambda kt: p.bcols[:, 32 + kt:32 + kt + 1]
    DCOL = lambda kt: p.bcols[:, 48 + kt:48 + kt + 1]

    with ExitStack() as stage:

        with ExitStack() as ctx:
            en = ctx.enter_context
            iproj = en(nc.sbuf_tensor("mb_iproj", [128, ND * 2 * DI], BF16))
            xch = [en(nc.sbuf_tensor(f"mb_xch{i}", [128, ntt * D], F32)) for i in range(2)]
            xlnT = [en(nc.sbuf_tensor(f"mb_xlnT{i}", [128, ND * Tc], BF16)) for i in range(2)]
            xtm = [en(nc.sbuf_tensor(f"mb_xtm{i}", [128, D], BF16)) for i in range(2)]
            junk = [en(nc.sbuf_tensor("mb_junk", [128, D], F32))] * 2
            stat = en(nc.sbuf_tensor("mb_stat", [128, 5 * ntt], F32))
            xz_sb = en(nc.sbuf_tensor("mb_xz", [128, (NF // 2) * Tc], F32))
            ptT = [en(nc.psum_tensor(f"mb_ptT{i}", [128, 128], BF16)) for i in range(2)]
            ptM = [en(nc.psum_tensor(f"mb_ptM{i}", [128, 512], F32)) for i in range(2)]

            p.mb_xz_st = {}
            _s2 = p.st[2].n
            for _c in range(NCH):
                for _hf in range(2):
                    _s2 += 16
                    p.mb_xz_st[(_c, _hf)] = _s2
            p.mb_xch_ld = {}
            _lc = [ld[0].n, ld[1].n]
            for _c in range(NCH):
                _lc[_c % 2] += 16
                p.mb_xch_ld[_c] = _lc[_c % 2]

            R = {"xch": xch, "xlnT": xlnT, "xtm": xtm, "junk": junk, "stat": stat, "ptT": ptT,
                 "ntt": ntt, "xld": lambda c: (ld[c % 2], p.mb_xch_ld[c])}

            def m1_load(sync, c):
                s2 = c % 2
                sync.wait_ge(st[s2].h, st[s2].n)
                sync.dma_start(xch[s2][:, :].rearrange("q (tt d) -> q tt d", tt=ntt),
                               rs[c * Tc:(c + 1) * Tc, :].rearrange("(tt q) d -> q tt d", q=128)
                               ).then_inc(ld[s2].h, 16)
                ld[s2].n += 16
                assert p.mb_xch_ld[c] == ld[s2].n

            with nc.Block() as block:
                @block.sync
                def _(sync):
                    sync.dma_start(iproj[:, :].rearrange("q (dt f) -> q dt f", dt=ND),
                                   p.inprojT.rearrange("(dt q) f -> q dt f", q=128)).then_inc(ld[6].h, 16)
                    ld[6].n += 16
                    m1_load(sync, 0)

            for c in range(NCH):
                s = c % 2
                b = (c * Tc) // T
                t0 = c * Tc - b * T
                with nc.Block() as block:
                    ln0_ss_end = None
                    if c == 0:
                        emit_ln(p, block, R, 0)
                        ln0_ss_end = ss.n

                    ts0, vs0, ss0 = ts.n, vs.n, ss.n

                    @block.vector
                    def _(v):
                        for ft in range(NF):
                            w = [(ts, ts0 + ft + 1)]
                            if ft == 0 and c > 0:
                                w.append((p.st[2], p.mb_xz_st[(c - 1, 1)]))
                            if ft == NF // 2:
                                w.append((p.st[2], p.mb_xz_st[(c, 0)]))
                            sop(v, vs, lambda ft=ft: v.tensor_copy(
                                xz_sb[:, (ft % (NF // 2)) * Tc:((ft % (NF // 2)) + 1) * Tc],
                                ptM[ft % 2][:, 0:Tc]), waits=w)

                    @block.tensor
                    def _(t):
                        t.wait_ge(ld[6].h, ld[6].n)
                        if ln0_ss_end is not None:
                            t.wait_ge(ss.h, ln0_ss_end)
                        for ft in range(NF):
                            if ft >= 2:
                                t.wait_ge(vs.h, vs0 + (ft - 2) + 1)
                            for k in range(ND):
                                mm = t.matmul(ptM[ft % 2][:, 0:Tc],
                                              iproj[:, k * 2 * DI + ft * 128: k * 2 * DI + (ft + 1) * 128],
                                              xlnT[s][:, k * Tc:(k + 1) * Tc],
                                              start=(k == 0), stop=(k == ND - 1))
                            mm.then_inc(ts.h, 1); ts.n += 1

                    assert ts.n == ts0 + NF and vs.n == vs0 + NF

                    if c + 1 < NCH:
                        emit_ln(p, block, R, c + 1, guard=(c == 0))

                    @block.sync
                    def _(sync):
                        for hf in range(2):
                            sync.wait_ge(vs.h, vs0 + (NF // 2) * (hf + 1))
                            sync.dma_start(
                                p.xz_scr[b, hf * DI:(hf + 1) * DI, t0:t0 + Tc]
                                .rearrange("(ft q) t -> q ft t", q=128),
                                xz_sb[:, :].rearrange("q (ft t) -> q ft t", ft=NF // 2)
                                ).then_inc(st[2].h, 16)
                            st[2].n += 16
                            assert p.mb_xz_st[(c, hf)] == st[2].n
                        if c + 1 < NCH:
                            m1_load(sync, c + 1)
            drain_dmas(p)

        with ExitStack() as ctx:
            en = ctx.enter_context
            xpad = [en(nc.sbuf_tensor(f"mb_xpad{i}", [128, T + 3], F32)) for i in range(2)]
            acc = [en(nc.sbuf_tensor(f"mb_acc{i}", [128, T], F32)) for i in range(2)]
            xaf = [en(nc.sbuf_tensor(f"mb_xaf{i}", [128, T], F32)) for i in range(2)]
            xab = [en(nc.sbuf_tensor(f"mb_xab{i}", [128, T], BF16)) for i in range(2)]
            units = [(b, kt) for b in range(B) for kt in range(NDI)]
            vs0, ss0 = vs.n, ss.n
            ld0 = [ld[0].n, ld[1].n]
            st0 = [p.st[0].n, p.st[1].n]

            with nc.Block() as block:
                @block.gpsimd
                def _(g):
                    g.memset(xpad[0][:, 0:3], 0.0).then_inc(p.gs.h, 1); p.gs.n += 1
                    g.memset(xpad[1][:, 0:3], 0.0).then_inc(p.gs.h, 1); p.gs.n += 1

                @block.sync
                def _(sync):
                    def store(u2):
                        b2, kt2 = units[u2]
                        sl2 = u2 % 2
                        sync.wait_ge(vs.h, vs0 + 5 * u2 + 5)
                        sync.dma_start(p.xiact_f[b2, kt2 * 128:(kt2 + 1) * 128, :],
                                       xaf[sl2][:, :]).then_inc(p.st[sl2].h, 16)
                        sync.dma_start(p.xiact_b[b2, kt2 * 128:(kt2 + 1) * 128, :],
                                       xab[sl2][:, :]).then_inc(p.st[sl2].h, 16)
                        p.st[sl2].n += 32
                    for u, (b, kt) in enumerate(units):
                        sl = u % 2
                        if u >= 2:
                            sync.wait_ge(vs.h, vs0 + 5 * (u - 2) + 4)
                        sync.dma_start(xpad[sl][:, 3:3 + T],
                                       p.xz_scr[b, kt * 128:(kt + 1) * 128, :]).then_inc(ld[sl].h, 16)
                        ld[sl].n += 16
                        if u >= 1:
                            store(u - 1)
                    store(len(units) - 1)

                @block.vector
                def _(v):
                    for u, (b, kt) in enumerate(units):
                        sl = u % 2
                        w = [(ld[sl], ld0[sl] + 16 * (u // 2 + 1)), (p.gs, p.gs.n)]
                        sop(v, vs, lambda sl=sl, kt=kt: v.tensor_scalar(
                            acc[sl][:, :], xpad[sl][:, 0:T], CW(0, kt), None, op0=ALU.mult), waits=w)
                        for k in range(1, KCONV):
                            sop(v, vs, lambda sl=sl, kt=kt, k=k: v.scalar_tensor_tensor(
                                acc[sl][:, :], xpad[sl][:, k:k + T], CW(k, kt), acc[sl][:, :],
                                op0=ALU.mult, op1=ALU.add))
                        sop(v, vs, lambda sl=sl: v.tensor_copy(xab[sl][:, :], xaf[sl][:, :]),
                            waits=[(ss, ss0 + u + 1)])

                @block.scalar
                def _(sc):
                    for u, (b, kt) in enumerate(units):
                        sl = u % 2
                        w = [(vs, vs0 + 5 * u + 4)]
                        if u >= 2:
                            w.append((p.st[sl], st0[sl] + 32 * (u // 2)))
                        sop(sc, ss, lambda sl=sl, kt=kt: sc.activation(
                            xaf[sl][:, :], acc[sl][:, :], AF.Silu, bias=CB(kt)), waits=w)

            drain_dmas(p)

        with ExitStack() as ctx:
            en = ctx.enter_context
            y_sb = en(nc.sbuf_tensor("mb_y", [128, NDI * T], BF16))
            xpr = en(nc.sbuf_tensor("mb_xpr", [128, NDI * DT], BF16))
            dpr = en(nc.sbuf_tensor("mb_dpr", [64, DI], BF16))
            oproj = en(nc.sbuf_tensor("mb_oproj", [128, NDI * D], BF16))
            strip = [en(nc.sbuf_tensor(f"mb_strip{i}", [128, T], BF16)) for i in range(2)]
            dtlow = en(nc.sbuf_tensor("mb_dtlow", [64, T], BF16))
            vband = en(nc.sbuf_tensor("mb_vband", [128, T], F32))
            t1 = en(nc.sbuf_tensor("mb_t1", [128, T], F32))
            t2 = en(nc.sbuf_tensor("mb_t2", [128, T], F32))
            uf = [en(nc.sbuf_tensor("mb_uf", [128, T], F32))] * 2
            zf = [en(nc.sbuf_tensor("mb_zf", [128, T], F32))] * 2
            xt = [en(nc.sbuf_tensor(f"mb_xt{i}", [128, D], F32)) for i in range(2)]
            W5 = min(512, T)
            NCF = T // W5

            with nc.Block() as block:
                @block.sync
                def _(sync):
                    sync.dma_start(xpr[:, :].rearrange("q (kt r) -> q kt r", kt=NDI),
                                   p.xprojT.rearrange("(kt q) r -> q kt r", q=128)).then_inc(ld[6].h, 16)
                    ld[6].n += 16
                    sync.dma_start(dpr[0:64, :], p.dtprojT[:, :]).then_inc(ld[6].h, 16)
                    ld[6].n += 16
                    sync.dma_start(oproj[:, :].rearrange("q (kt d) -> q kt d", kt=NDI),
                                   p.outprojT.rearrange("(kt q) d -> q kt d", q=128)).then_inc(ld[6].h, 16)
                    ld[6].n += 16

            pstk = [ExitStack()]

            def psums(phase):
                pstk[0].close()
                pstk[0] = ExitStack()
                if phase == "dt":
                    return pstk[0].enter_context(nc.psum_tensor(f"mb_pDL{b}", [64, T], F32))
                if phase == "delta":
                    return [pstk[0].enter_context(nc.psum_tensor(f"mb_pD{b}{i}", [128, 512], F32))
                            for i in range(2)]
                return [pstk[0].enter_context(nc.psum_tensor(f"mb_pO{b}{i}", [128, D], F32))
                        for i in range(2)]

            for b in range(B):
                pDL = psums("dt")
                ts0 = ts.n
                with nc.Block() as block:
                    @block.sync
                    def _(sync):
                        for kt in range(NDI):
                            sl = kt % 2
                            if kt >= 2:
                                sync.wait_ge(ts.h, ts0 + (kt - 2) + 1)
                            sync.dma_start(strip[sl][:, :],
                                           p.xiact_b[b, kt * 128:(kt + 1) * 128, :]).then_inc(ld[sl].h, 16)
                            ld[sl].n += 16
                            p.mb_strip_ld[(b, kt)] = ld[sl].n

                    @block.tensor
                    def _(t):
                        t.wait_ge(ld[6].h, ld[6].n)
                        for kt in range(NDI):
                            sl = kt % 2
                            t.wait_ge(ld[sl].h, p.mb_strip_ld[(b, kt)])
                            for ncf in range(NCF):
                                mm = t.matmul(pDL[0:64, ncf * W5:(ncf + 1) * W5],
                                              xpr[:, kt * DT:(kt + 1) * DT],
                                              strip[sl][:, ncf * W5:(ncf + 1) * W5],
                                              start=(kt == 0), stop=(kt == NDI - 1))
                            mm.then_inc(ts.h, 1); ts.n += 1

                    @block.vector
                    def _(v):
                        sop(v, vs, lambda: v.tensor_copy(dtlow[0:64, :], pDL[0:64, :]),
                            waits=[(ts, ts0 + NDI)])

                pD = psums("delta")
                for kt in range(NDI):
                    sl = kt % 2
                    ts0, vs0, ss0 = ts.n, vs.n, ss.n
                    uld, zld = ld[sl].n + 16, ld[7].n + 16
                    with nc.Block() as block:
                        @block.sync
                        def _(sync):
                            sync.dma_start(uf[sl][:, :],
                                           p.xiact_f[b, kt * 128:(kt + 1) * 128, :]).then_inc(ld[sl].h, 16)
                            ld[sl].n += 16
                            sync.dma_start(zf[sl][:, :],
                                           p.xz_scr[b, DI + kt * 128:DI + (kt + 1) * 128, :]
                                           ).then_inc(ld[7].h, 16)
                            ld[7].n += 16

                        @block.tensor
                        def _(t):
                            for ncf in range(NCF):
                                if ncf >= 2:
                                    t.wait_ge(vs.h, vs0 + (ncf - 2) + 1)
                                mm = t.matmul(pD[ncf % 2][:, 0:W5],
                                              dpr[0:64, kt * 128:(kt + 1) * 128],
                                              dtlow[0:64, ncf * W5:(ncf + 1) * W5],
                                              start=True, stop=True)
                                mm.then_inc(ts.h, 1); ts.n += 1

                        @block.vector
                        def _(v):
                            for ncf in range(NCF):
                                sop(v, vs, lambda ncf=ncf: v.tensor_scalar(
                                    vband[:, ncf * W5:(ncf + 1) * W5], pD[ncf % 2][:, 0:W5],
                                    DTB(kt), None, op0=ALU.add),
                                    waits=[(ts, ts0 + ncf + 1)])
                            sop(v, vs, lambda: v.tensor_scalar(vband[:, :], vband[:, :], 0.0, None,
                                                               op0=ALU.max),
                                waits=[(ss, ss0 + 1)])
                            sop(v, vs, lambda: v.tensor_tensor(vband[:, :], vband[:, :], t1[:, :],
                                                               op=ALU.add),
                                waits=[(ss, ss0 + 3)])
                            sop(v, vs, lambda: v.tensor_scalar(vband[:, :], vband[:, :], 10.0, 1e-6,
                                                               op0=ALU.min, op1=ALU.max))
                            sop(v, vs, lambda: v.tensor_scalar(t1[:, :], t1[:, :], 1.0, 1e-6,
                                                               op0=ALU.min, op1=ALU.max),
                                waits=[(ss, ss0 + 4)])
                            sop(v, vs, lambda: v.tensor_tensor_scan(
                                t2[:, :], t1[:, :], uf[sl][:, :], 0.0, op0=ALU.mult, op1=ALU.add),
                                waits=[(ld[sl], uld)])
                            sop(v, vs, lambda: v.scalar_tensor_tensor(
                                t2[:, :], uf[sl][:, :], DCOL(kt), t2[:, :], op0=ALU.mult, op1=ALU.add))
                            sop(v, vs, lambda: v.tensor_scalar(t2[:, :], t2[:, :], 1e4, -1e4,
                                                               op0=ALU.min, op1=ALU.max))
                            sop(v, vs, lambda: v.tensor_tensor(
                                y_sb[:, kt * T:(kt + 1) * T], t2[:, :], t1[:, :], op=ALU.mult),
                                waits=[(ss, ss0 + 5)])

                        @block.scalar
                        def _(sc):
                            sop(sc, ss, lambda: sc.activation(t1[:, :], vband[:, :], AF.Abs),
                                waits=[(vs, vs0 + NCF)])
                            sop(sc, ss, lambda: sc.activation(t1[:, :], t1[:, :], AF.Exp, scale=-1.0))
                            sop(sc, ss, lambda: sc.activation(t1[:, :], t1[:, :], AF.Ln, bias=1.0))
                            sop(sc, ss, lambda: sc.activation(t1[:, :], vband[:, :], AF.Exp,
                                                              scale=ACOL(kt)),
                                waits=[(vs, vs0 + NCF + 3)])
                            sop(sc, ss, lambda: sc.activation(t1[:, :], zf[sl][:, :], AF.Silu),
                                waits=[(vs, vs0 + NCF + 5), (ld[7], zld)])

                pO = psums("out")
                ts0, vs0 = ts.n, vs.n
                ld0s = [ld[0].n, ld[1].n]
                st0s = [st[0].n, st[1].n]
                with nc.Block() as block:
                    @block.tensor
                    def _(t):
                        for tt in range(NTB):
                            sl = tt % 2
                            if tt >= 2:
                                t.wait_ge(vs.h, vs0 + (tt - 2) + 1)
                            for kt in range(NDI):
                                for dh in range(2):
                                    mm = t.matmul(pO[tt % 2][:, dh * 512:(dh + 1) * 512],
                                                  y_sb[:, kt * T + tt * 128: kt * T + (tt + 1) * 128],
                                                  oproj[:, kt * D + dh * 512: kt * D + dh * 512 + 512],
                                                  start=(kt == 0), stop=(kt == NDI - 1))
                            mm.then_inc(ts.h, 1); ts.n += 1

                    @block.vector
                    def _(v):
                        for tt in range(NTB):
                            sl = tt % 2
                            sop(v, vs, lambda tt=tt, sl=sl: v.scalar_tensor_tensor(
                                xt[sl][:, :], pO[tt % 2][:, :], 1.0, xt[sl][:, :],
                                op0=ALU.mult, op1=ALU.add),
                                waits=[(ts, ts0 + tt + 1),
                                       (ld[sl], ld0s[sl] + 16 * (tt // 2 + 1))])

                    @block.sync
                    def _(sync):
                        for tt in range(NTB):
                            sl = tt % 2
                            if tt >= 2:
                                sync.wait_ge(vs.h, vs0 + (tt - 2) + 1)
                                sync.wait_ge(st[sl].h, st0s[sl] + 16 * (tt // 2))
                            sync.dma_start(
                                xt[sl][:, :],
                                rs[b * T + tt * 128: b * T + (tt + 1) * 128, :]).then_inc(ld[sl].h, 16)
                            ld[sl].n += 16
                            if tt >= 1:
                                psl = (tt - 1) % 2
                                sync.wait_ge(vs.h, vs0 + tt)
                                sync.dma_start(rd[b * T + (tt - 1) * 128: b * T + tt * 128, :],
                                               xt[psl][:, :]).then_inc(st[psl].h, 16)
                                st[psl].n += 16
                        sync.wait_ge(vs.h, vs0 + NTB)
                        lsl = (NTB - 1) % 2
                        sync.dma_start(rd[b * T + (NTB - 1) * 128: b * T + NTB * 128, :],
                                       xt[lsl][:, :]).then_inc(st[lsl].h, 16)
                        st[lsl].n += 16
            pstk[0].close()
            drain_dmas(p)


def build_moe(p, isrc, idst, final=False, tag=0):
    nc = p.nc
    NTOK, NTT = p.NTOK, p.NTT
    ND = D // 128
    NMT = DM // 128
    RC = min(1024, NTOK)
    NRC = NTOK // RC
    CAP = RC // 2
    NTB = CAP // 128
    TPC = RC // 128
    JUNK = float(NE * CAP)
    rs = p.resid[isrc]
    rd = p.y_out if final else p.resid[idst]
    ts, vs, ss, gs = p.ts, p.vs, p.ss, p.gs
    ld, st = p.ld, p.st
    gld = p.gld

    with ExitStack() as stage:
        sen = stage.enter_context
        c_all = sen(nc.sbuf_tensor(f"mo{tag}_call", [128, NTT * NE], F32))

        with ExitStack() as ctx:
            en = ctx.enter_context
            xt = [en(nc.sbuf_tensor(f"mo{tag}_xt{i}", [128, D], F32)) for i in range(2)]
            xlf = en(nc.sbuf_tensor(f"mo{tag}_xlf", [128, D], F32))
            xtm = [en(nc.sbuf_tensor(f"mo{tag}_xtm{i}", [128, 1032], BF16)) for i in range(2)]
            xlo = [en(nc.sbuf_tensor(f"mo{tag}_xlo{i}", [128, D], BF16)) for i in range(2)]
            xtbh = [en(nc.sbuf_tensor(f"mo{tag}_xtbh{i}", [128, D], BF16)) for i in range(2)]
            xtbl = [en(nc.sbuf_tensor(f"mo{tag}_xtbl{i}", [128, D], BF16)) for i in range(2)]
            junk = [en(nc.sbuf_tensor(f"mo{tag}_junk", [128, D], F32))] * 2
            stat = en(nc.sbuf_tensor(f"mo{tag}_stat", [128, 8], F32))
            tp2 = en(nc.sbuf_tensor(f"mo{tag}_tp2", [128, 16], F32))
            lsb = en(nc.sbuf_tensor(f"mo{tag}_lsb", [128, NE], F32))
            geb = en(nc.sbuf_tensor(f"mo{tag}_geb", [128, NE], F32))
            exb = en(nc.sbuf_tensor(f"mo{tag}_exb", [128, NE], F32))
            mskf = en(nc.sbuf_tensor(f"mo{tag}_mskf", [128, NE], F32))
            mskb = en(nc.sbuf_tensor(f"mo{tag}_mskb", [128, NE], BF16))
            preb = en(nc.sbuf_tensor(f"mo{tag}_preb", [128, NE], F32))
            rowb = en(nc.sbuf_tensor(f"mo{tag}_rowb", [1, NE], F32))
            offb = en(nc.sbuf_tensor(f"mo{tag}_offb", [1, NE], F32))
            posa = en(nc.sbuf_tensor(f"mo{tag}_posa", [128, NE], F32))
            posb = en(nc.sbuf_tensor(f"mo{tag}_posb", [128, NE], F32))
            pos12 = en(nc.sbuf_tensor(f"mo{tag}_pos12", [128, 4], F32))
            posi = en(nc.sbuf_tensor(f"mo{tag}_posi", [128, 4], I32))
            ptT = [en(nc.psum_tensor(f"mo{tag}_ptT{i}", [128, 128], BF16)) for i in range(2)]
            pl = [en(nc.psum_tensor(f"mo{tag}_pl{i}", [128, NE], F32)) for i in range(2)]
            psP = en(nc.psum_tensor(f"mo{tag}_psP", [128, NE], F32))
            psT = en(nc.psum_tensor(f"mo{tag}_psT", [1, NE], F32))
            psB = en(nc.psum_tensor(f"mo{tag}_psB", [128, NE], F32))
            SUM = stat[:, 0:1]; SQ = stat[:, 1:2]; MC = stat[:, 2:3]; VAR = stat[:, 3:4]; RSTD = stat[:, 4:5]
            M1 = tp2[:, 0:1]; M2 = tp2[:, 1:2]; DMM = tp2[:, 2:3]; EDM = tp2[:, 3:4]
            RDEN = tp2[:, 4:5]; NEGM1 = tp2[:, 5:6]
            PMIN = tp2[:, 6:7]; PEQ = tp2[:, 7:8]; PMIN2 = tp2[:, 8:9]

            NS = 21
            NT = 20
            NVu = [20 + 16 + (1 if (u % TPC == 0) else 0) for u in range(NTT)]
            ts_base = [0]; vs_base = [0]; ss_base = [0]
            for u in range(NTT):
                ts_base.append(ts_base[-1] + NT)
                vs_base.append(vs_base[-1] + NVu[u])
                ss_base.append(ss_base[-1] + NS)
            ts0, vs0, ss0 = ts.n, vs.n, ss.n
            ld0 = [ld[0].n, ld[1].n]
            st2_0 = st[2].n
            gf0 = gld[0].n
            gsc0 = gld[1].n
            gx0 = p.gidx.n

            def TSB(u): return ts0 + ts_base[u]
            def VSB(u): return vs0 + vs_base[u]
            def SSB(u): return ss0 + ss_base[u]

            with nc.Block() as block:
                @block.sync
                def _(sync):
                    for u in range(NTT):
                        sl = u % 2
                        if u >= 2:
                            sync.wait_ge(vs.h, VSB(u - 2) + 6)
                        sync.dma_start(xt[sl][:, :],
                                       rs[u * 128:(u + 1) * 128, :]).then_inc(ld[sl].h, 16)
                        ld[sl].n += 16
                        if u >= 1:
                            sync.wait_ge(vs.h, VSB(u - 1) + 21)
                            sync.dma_start(p.xtm_scr[(u - 1) * 128:u * 128, :],
                                           xtm[(u - 1) % 2][:, :]).then_inc(st[2].h, 16)
                            st[2].n += 16
                    sync.wait_ge(vs.h, VSB(NTT - 1) + 21)
                    sync.dma_start(p.xtm_scr[(NTT - 1) * 128:NTT * 128, :],
                                   xtm[(NTT - 1) % 2][:, :]).then_inc(st[2].h, 16)
                    st[2].n += 16

                @block.vector
                def _(v):
                    for u in range(NTT):
                        sl = u % 2
                        xld = ld0[sl] + 16 * (u // 2 + 1)
                        sop(v, vs, lambda sl=sl: v.tensor_reduce(
                            SUM, xt[sl][:, :], axis=AX.X, op=ALU.add), waits=[(ld[sl], xld)])
                        sop(v, vs, lambda: v.tensor_scalar(MC, SUM, 1.0 / D, None, op0=ALU.mult))
                        sop(v, vs, lambda: v.tensor_tensor(VAR, MC, MC, op=ALU.mult))
                        sop(v, vs, lambda u=u: v.scalar_tensor_tensor(
                            VAR, SQ, 1.0 / D, VAR, op0=ALU.mult, op1=ALU.subtract),
                            waits=[(ss, SSB(u) + 1)])
                        sop(v, vs, lambda: v.tensor_scalar(VAR, VAR, EPS, None, op0=ALU.add))
                        sop(v, vs, lambda sl=sl, u=u: v.tensor_scalar(
                            xlf[:, :], xt[sl][:, :], MC, RSTD, op0=ALU.subtract, op1=ALU.mult),
                            waits=[(ss, SSB(u) + 3)])
                        w7 = []
                        if u >= 2:
                            w7 = [(ts, TSB(u - 2) + 16), (st[2], st2_0 + 16 * (u - 1))]
                        sop(v, vs, lambda u=u, sl=sl: v.tensor_copy(xtm[sl][:, 0:D], xlf[:, :]),
                            waits=w7)
                        sop(v, vs, lambda sl=sl: v.tensor_tensor(
                            xlo[sl][:, :], xlf[:, :], xtm[sl][:, 0:D], op=ALU.subtract))
                        sop(v, vs, lambda u=u: v.tensor_copy(lsb[:, :], pl[u % 2][:, :]),
                            waits=[(ts, TSB(u) + 17)])
                        sop(v, vs, lambda: v.tensor_reduce(M1, lsb[:, :], axis=AX.X, op=ALU.max))
                        sop(v, vs, lambda: v.tensor_scalar(geb[:, :], lsb[:, :], M1, None,
                                                           op0=ALU.is_equal))
                        sop(v, vs, lambda: v.scalar_tensor_tensor(
                            geb[:, :], geb[:, :], -1e30, lsb[:, :], op0=ALU.mult, op1=ALU.add))
                        sop(v, vs, lambda: v.tensor_reduce(M2, geb[:, :], axis=AX.X, op=ALU.max))
                        sop(v, vs, lambda: v.tensor_tensor(DMM, M2, M1, op=ALU.subtract))
                        sop(v, vs, lambda u=u: v.tensor_scalar(EDM, EDM, 1.0, None, op0=ALU.add),
                            waits=[(ss, SSB(u) + 20)])
                        sop(v, vs, lambda: v.reciprocal(RDEN, EDM))
                        sop(v, vs, lambda: v.tensor_scalar(RDEN, RDEN, 0.5, None, op0=ALU.mult))
                        sop(v, vs, lambda: v.tensor_scalar(NEGM1, M1, -1.0, None, op0=ALU.mult))
                        sop(v, vs, lambda: v.tensor_scalar(geb[:, :], lsb[:, :], M2, None,
                                                           op0=ALU.is_ge))
                        sop(v, vs, lambda u=u: v.scalar_tensor_tensor(
                            c_all[:, u * NE:(u + 1) * NE], exb[:, :], RDEN, geb[:, :],
                            op0=ALU.mult, op1=ALU.mult),
                            waits=[(ss, SSB(u) + 21)])
                        sop(v, vs, lambda u=u, sl=sl: v.tensor_copy(
                            xtm[sl][:, D:D + NE], c_all[:, u * NE:(u + 1) * NE]))
                        sop(v, vs, lambda u=u: v.tensor_scalar(
                            mskf[:, :], c_all[:, u * NE:(u + 1) * NE], 0.0, None, op0=ALU.is_gt))
                        sop(v, vs, lambda: v.tensor_copy(mskb[:, :], mskf[:, :]))
                        sop(v, vs, lambda u=u: v.tensor_copy(preb[:, :], psP[:, :]),
                            waits=[(ts, TSB(u) + 18)])
                        if u % TPC == 0:
                            rcb = float((u // TPC) * 4224)
                            sop(v, vs, lambda rcb=rcb: v.tensor_scalar(
                                offb[0:1, :], p.ebase[0:1, :], 0.0, rcb,
                                op0=ALU.mult, op1=ALU.add))
                        sop(v, vs, lambda: v.tensor_tensor(rowb[0:1, :], offb[0:1, :],
                                                           p.ebase[0:1, :], op=ALU.add))
                        sop(v, vs, lambda u=u: v.tensor_tensor(posa[:, :], preb[:, :], psB[:, :],
                                                               op=ALU.add),
                            waits=[(ts, TSB(u) + 20)])
                        sop(v, vs, lambda: v.tensor_tensor(posa[:, :], posa[:, :], mskf[:, :],
                                                           op=ALU.mult))
                        jk = float((u // TPC) * 4224 + 4096)
                        sop(v, vs, lambda jk=jk: v.tensor_scalar(posb[:, :], mskf[:, :], -jk, jk,
                                                                 op0=ALU.mult, op1=ALU.add))
                        sop(v, vs, lambda: v.tensor_tensor(posa[:, :], posa[:, :], posb[:, :],
                                                           op=ALU.add))
                        sop(v, vs, lambda: v.tensor_reduce(PMIN, posa[:, :], axis=AX.X,
                                                           op=ALU.min))
                        sop(v, vs, lambda: v.tensor_scalar(posb[:, :], posa[:, :], PMIN, None,
                                                           op0=ALU.is_equal))
                        sop(v, vs, lambda: v.scalar_tensor_tensor(
                            posb[:, :], posb[:, :], 1.0e7, posa[:, :],
                            op0=ALU.mult, op1=ALU.add))
                        sop(v, vs, lambda: v.tensor_reduce(PMIN2, posb[:, :], axis=AX.X,
                                                           op=ALU.min))
                        wps = [(p.gidx, gx0 + 16 * ((u - 2) // TPC + 1) + 32 * (u - 1))] \
                            if u >= 2 else []
                        sl2 = (u % 2) * 2
                        sop(v, vs, lambda sl2=sl2: v.tensor_copy(posi[:, sl2:sl2 + 1], PMIN),
                            waits=wps)
                        sop(v, vs, lambda sl2=sl2: v.tensor_copy(posi[:, sl2 + 1:sl2 + 2],
                                                                 PMIN2))
                        sop(v, vs, lambda u=u: v.tensor_tensor(offb[0:1, :], offb[0:1, :],
                                                               psT[0:1, :], op=ALU.add),
                            waits=[(ts, TSB(u) + 19)])

                @block.scalar
                def _(sc):
                    for u in range(NTT):
                        sl = u % 2
                        xld = ld0[sl] + 16 * (u // 2 + 1)
                        sop(sc, ss, lambda sl=sl: sc.activation(
                            junk[sl][:, :], xt[sl][:, :], AF.Square, accum_out=SQ),
                            waits=[(ld[sl], xld), (vs, VSB(u - 1) + 6) if u >= 1 else (vs, 0)])
                        sop(sc, ss, lambda u=u: sc.activation(RSTD, VAR, AF.Ln),
                            waits=[(vs, VSB(u) + 5)])
                        sop(sc, ss, lambda: sc.activation(RSTD, RSTD, AF.Exp, scale=-0.5))
                        for j in range(16):
                            hi = j < 8
                            k = j % 8
                            dstb = xtbh if hi else xtbl
                            w = [(ts, TSB(u) + j + 1)]
                            if j == 0 and u >= 2:
                                w.append((ts, TSB(u - 2) + 17))
                            jj = NT * u + j
                            sop(sc, ss, lambda jj=jj, k=k, sl=sl, dstb=dstb: sc.copy(
                                dstb[sl][:, k * 128:(k + 1) * 128], ptT[jj % 2][:, :]), waits=w)
                        sop(sc, ss, lambda u=u: sc.activation(EDM, DMM, AF.Exp),
                            waits=[(vs, VSB(u) + 14)])
                        sop(sc, ss, lambda u=u: sc.activation(exb[:, :], lsb[:, :], AF.Exp,
                                                              bias=NEGM1),
                            waits=[(vs, VSB(u) + 18)])

                @block.tensor
                def _(t):
                    for u in range(NTT):
                        sl = u % 2
                        for j in range(16):
                            hi = j < 8
                            k = j % 8
                            srcb = xtm if hi else xlo
                            t.wait_ge(vs.h, VSB(u) + (7 if hi else 8))
                            jj = NT * u + j
                            if jj >= 2:
                                prev_u, prev_j = divmod(jj - 2, NT)
                                if prev_j < 16:
                                    t.wait_ge(ss.h, SSB(prev_u) + 3 + prev_j + 1)
                            t.transpose(ptT[jj % 2][:, :], srcb[sl][:, k * 128:(k + 1) * 128],
                                        p.ident[:, :]).then_inc(ts.h, 1)
                            ts.n += 1
                        t.wait_ge(ss.h, SSB(u) + 19)
                        if u >= 2:
                            t.wait_ge(vs.h, VSB(u - 2) + 9)
                        nmm = 0
                        for (xa, wa) in ((xtbh, p.wg_hi), (xtbh, p.wg_lo), (xtbl, p.wg_hi),
                                         (xtbl, p.wg_lo)):
                            for k in range(ND):
                                mm = t.matmul(pl[u % 2][:, :], xa[sl][:, k * 128:(k + 1) * 128],
                                              wa[:, k * NE:(k + 1) * NE],
                                              start=(nmm == 0), stop=(nmm == 4 * ND - 1))
                                nmm += 1
                        mm.then_inc(ts.h, 1); ts.n += 1
                        t.wait_ge(vs.h, VSB(u) + 23)
                        if u >= 1:
                            t.wait_ge(vs.h, VSB(u - 1) + 24)
                        mm = t.matmul(psP[:, :], p.tri[:, :], mskb[:, :], start=True, stop=True)
                        mm.then_inc(ts.h, 1); ts.n += 1
                        if u >= 1:
                            t.wait_ge(vs.h, VSB(u))
                        mm = t.matmul(psT[0:1, :], p.ones_bf[:, 0:1], mskb[:, :],
                                      start=True, stop=True)
                        mm.then_inc(ts.h, 1); ts.n += 1
                        t.wait_ge(vs.h, VSB(u) + 25 + (1 if u % TPC == 0 else 0))
                        if u >= 1:
                            t.wait_ge(vs.h, VSB(u - 1) + 26 + (1 if (u - 1) % TPC == 0 else 0))
                        mm = t.matmul(psB[:, :], p.ones_f32r[0:1, :], rowb[0:1, :],
                                      start=True, stop=True)
                        mm.then_inc(ts.h, 1); ts.n += 1

                @block.gpsimd
                def _(g):
                    gx = p.gidx
                    for u in range(NTT):
                        rc = u // TPC
                        if u % TPC == 0:
                            g.wait_ge(gx.h, gx.n)
                            g.dma_start(p.idx_scr[rc].rearrange("(q cc) -> q cc", cc=33),
                                        p.idx_fill.rearrange("(q cc) -> q cc", cc=33)
                                        ).then_inc(gx.h, 16)
                            gx.n += 16
                        g.wait_ge(vs.h, VSB(u) + 35 + (1 if u % TPC == 0 else 0))
                        for col in range(2):
                            g.wait_ge(gx.h, gx.n)
                            g.indirect_dma_start(
                                p.idx_scr.rearrange("a b -> (a b)")[:, None],
                                IndirectOffsetOnAxis(posi[:, (u % 2) * 2 + col:
                                                          (u % 2) * 2 + col + 1], 0),
                                p.tokid[:, u:u + 1], None).then_inc(gx.h, 16)
                            gx.n += 16

            assert ts.n == ts0 + ts_base[NTT]
            assert vs.n == vs0 + vs_base[NTT], (vs.n, vs0 + vs_base[NTT])
            assert ss.n == ss0 + ss_base[NTT]
            drain_dmas(p)

        with ExitStack() as ctx:
            en = ctx.enter_context
            idxs = en(nc.sbuf_tensor(f"mo{tag}_idxs", [128, 2 * NE * NTB], I32))
            xg = [en(nc.sbuf_tensor(f"mo{tag}_xg{i}", [128, NTB * 1032], BF16)) for i in range(2)]
            xgT = [en(nc.sbuf_tensor(f"mo{tag}_xgT", [128, ND * CAP], BF16))] * 2
            h_sb = en(nc.sbuf_tensor(f"mo{tag}_h", [128, NMT * CAP], BF16))
            y_sb = [en(nc.sbuf_tensor(f"mo{tag}_y", [128, NTB * D], F32))] * 2
            wq = [en(nc.sbuf_tensor(f"mo{tag}_wq{i}", [128, ND * D], BF16)) for i in range(2)]
            wo_sb = en(nc.sbuf_tensor(f"mo{tag}_wo", [128, NMT * D], BF16))
            ccol = en(nc.sbuf_tensor(f"mo{tag}_ccol", [128, 8], F32))
            ptT = [en(nc.psum_tensor(f"mo{tag}_qtT{i}", [128, 128], BF16)) for i in range(2)]
            pA = [en(nc.psum_tensor(f"mo{tag}_pA{i}", [128, 512], F32)) for i in range(2)]
            pB = [en(nc.psum_tensor(f"mo{tag}_pB{i}", [128, D], F32)) for i in range(2)]

            qtot = {}
            _q = [ld[2].n, ld[3].n]
            for _c in range(NRC):
                for _e in range(NE):
                    for _mq in range(4):
                        qg = _e * 4 + _mq
                        _q[qg % 2] += 16
                        qtot[(_c, _e, _mq)] = _q[qg % 2]
            idx0 = ld[5].n
            ini0 = st[3].n

            for c in range(NRC):
                ts0, vs0, ss0 = ts.n, vs.n, ss.n
                wotot = {}
                _w = ld[4].n
                for _e in range(NE):
                    _w += 16
                    wotot[_e] = _w
                NTR = NTB * 8
                EST = NTR + 32 + NTB
                SSE = NTR + 32

                def ts_tr(e, j):
                    return ts0 + EST * e + j + 1

                def ts_A(e, g):
                    return ts0 + EST * e + NTR + g + 1

                def ts_B(e, tb):
                    return ts0 + EST * e + NTR + 32 + tb + 1

                def ss_tr(e, j):
                    return ss0 + SSE * e + j + 1

                def ss_gelu(e, g):
                    return ss0 + SSE * e + NTR + g + 1

                def vs_y(e, tb):
                    return vs0 + 2 * NTB * e + 2 * tb + 2


                with nc.Block() as block:
                    @block.sync
                    def _(sync):
                        ts0c = ts.n
                        if c == 0:
                            sync.wait_ge(p.gidx.h, p.gidx.n)
                            for cc in range(NRC):
                                sync.dma_start(p.moe_out[cc * RC:(cc + 1) * RC, :],
                                               rs[cc * RC:(cc + 1) * RC, :]).then_inc(st[3].h, 16)
                                st[3].n += 16
                            for col in range(NE * NTB):
                                sync.dma_start(idxs[:, col:col + 1],
                                               p.idx_scr[0][col * 128:(col + 1) * 128][:, None]
                                               ).then_inc(ld[5].h, 16)
                                ld[5].n += 16
                            p.mo_idx_tot[(tag, 0)] = ld[5].n
                            sync.wait_ge(p.gc2.h, p.cast_moe_tot)
                        if c >= 1:
                            sync.wait_ge(ts.h, ts0c)
                        for mq in range(2):
                            sync.dma_start(
                                wq[mq % 2][:, :].rearrange("q (dt m) -> q dt m", dt=ND),
                                p.win_bf[0][:, mq * D:(mq + 1) * D].rearrange(
                                    "(dt q) m -> q dt m", q=128)).then_inc(ld[(mq % 2) + 2].h, 16)
                            ld[(mq % 2) + 2].n += 16
                        sync.dma_start(
                            wo_sb[:, :].rearrange("q (mt d) -> q mt d", mt=NMT),
                            p.wout_bf[0].rearrange("(mt q) d -> q mt d", q=128)
                            ).then_inc(ld[4].h, 16)
                        ld[4].n += 16
                        if c >= 1:
                            for _s in p.gsc:
                                sync.wait_ge(_s.h, _s.n)
                            sync.dma_start(rd[(c - 1) * RC:c * RC, :],
                                           p.moe_out[(c - 1) * RC:c * RC, :]).then_inc(st[3].h, 16)
                            st[3].n += 16
                        p.mo_ini_tot[(tag, c)] = st[3].n
                        for e in range(NE):
                            for mq in range(4):
                                if e == 0 and mq < 2:
                                    continue
                                qg = e * 4 + mq
                                if qg >= 2:
                                    pe, pq = divmod(qg - 2, 4)
                                    sync.wait_ge(ts.h, ts_A(pe, pq * 8 + 7))
                                sync.dma_start(
                                    wq[qg % 2][:, :].rearrange("q (dt m) -> q dt m", dt=ND),
                                    p.win_bf[e][:, mq * D:(mq + 1) * D].rearrange(
                                        "(dt q) m -> q dt m", q=128)).then_inc(ld[(qg % 2) + 2].h, 16)
                                ld[(qg % 2) + 2].n += 16
                            if e >= 1:
                                sync.wait_ge(ts.h, ts_B(e - 1, NTB - 1))
                                sync.dma_start(
                                    wo_sb[:, :].rearrange("q (mt d) -> q mt d", mt=NMT),
                                    p.wout_bf[e].rearrange("(mt q) d -> q mt d", q=128)
                                    ).then_inc(ld[4].h, 16)
                                ld[4].n += 16
                        if c + 1 < NRC:
                            for _s in p.gga:
                                sync.wait_ge(_s.h, _s.n)
                            so = ((c + 1) % 2) * NE * NTB
                            for col in range(NE * NTB):
                                sync.dma_start(idxs[:, so + col:so + col + 1],
                                               p.idx_scr[c + 1][col * 128:(col + 1) * 128][:, None]
                                               ).then_inc(ld[5].h, 16)
                                ld[5].n += 16
                            p.mo_idx_tot[(tag, c + 1)] = ld[5].n

                    @block.gpsimd
                    def _(g):
                        ssem = p.gsc[0]
                        if c == 0:
                            for _s in p.gga + p.gsc:
                                g.wait_ge(_s.h, _s.n)
                        g.wait_ge(ld[5].h, p.mo_idx_tot[(tag, c)])
                        iso = (c % 2) * NE * NTB
                        for e in range(NE + 2):
                            if e >= 2:
                                se = e - 2
                                if e == 2:
                                    g.wait_ge(st[3].h, p.mo_ini_tot[(tag, c)])
                                E = c * NE + se
                                for tb in range(NTB):
                                    g.wait_ge(vs.h, vs_y(se, tb))
                                    g.wait_ge(ssem.h, ssem.n)
                                    g.indirect_dma_start(
                                        p.moe_out[:, :],
                                        IndirectOffsetOnAxis(
                                            idxs[:, iso + se * NTB + tb: iso + se * NTB + tb + 1], 0),
                                        y_sb[se % 2][:, tb * D:(tb + 1) * D], None,
                                        compute_op=ALU.add).then_inc(ssem.h, 16)
                                    ssem.n += 16
                                p.mo_sc_tot[(tag, c, se)] = (0, ssem.n)
                            if e < NE:
                                E = c * NE + e
                                gsem = p.gga[E % 2]
                                if e >= 2:
                                    g.wait_ge(ts.h, ts_tr(e - 2, NTR - 1))
                                    g.wait_ge(vs.h, vs_y(e - 2, NTB - 1))
                                for tb in range(NTB):
                                    g.indirect_dma_start(
                                        xg[e % 2][:, tb * 1032:(tb + 1) * 1032], None,
                                        p.xtm_scr[:, :],
                                        IndirectOffsetOnAxis(
                                            idxs[:, iso + e * NTB + tb: iso + e * NTB + tb + 1], 0)
                                        ).then_inc(gsem.h, 16)
                                    gsem.n += 16
                                p.mo_ga_tot[(tag, c, e)] = (E % 2, gsem.n)

                    @block.tensor
                    def _(t):
                        for e in range(NE):
                            gsl, gtot = p.mo_ga_tot[(tag, c, e)]
                            t.wait_ge(p.gga[gsl].h, gtot)
                            for j in range(NTR):
                                tb, k = divmod(j, 8)
                                jj = EST * e + j
                                if jj >= 2:
                                    pe, pj = divmod(jj - 2, EST)
                                    if pj < NTR:
                                        t.wait_ge(ss.h, ss_tr(pe, pj))
                                t.transpose(ptT[jj % 2][:, :],
                                            xg[e % 2][:, tb * 1032 + k * 128: tb * 1032 + (k + 1) * 128],
                                            p.ident[:, :]).then_inc(ts.h, 1)
                                ts.n += 1
                            for mq in range(4):
                                t.wait_ge(ld[(e * 4 + mq) % 2 + 2].h, qtot[(c, e, mq)])
                                for mtl in range(8):
                                    g2 = mq * 8 + mtl
                                    if e * 32 + g2 >= 2:
                                        pe, pg = (e, g2 - 2) if g2 >= 2 else (e - 1, 30 + g2)
                                        t.wait_ge(ss.h, ss_gelu(pe, pg))
                                    if g2 == 0:
                                        t.wait_ge(ss.h, ss_tr(e, NTR - 1))
                                    for k in range(ND):
                                        mm = t.matmul(pA[g2 % 2][:, 0:CAP],
                                                      wq[(e * 4 + mq) % 2][:, k * D + mtl * 128:
                                                                           k * D + (mtl + 1) * 128],
                                                      xgT[e % 2][:, k * CAP:(k + 1) * CAP],
                                                      start=(k == 0), stop=(k == ND - 1))
                                    mm.then_inc(ts.h, 1); ts.n += 1
                            t.wait_ge(ld[4].h, wotot[e])
                            t.wait_ge(ss.h, ss_gelu(e, 31))
                            for tb in range(NTB):
                                if tb >= 2:
                                    t.wait_ge(vs.h, vs_y(e, tb - 2))
                                elif e >= 1:
                                    t.wait_ge(vs.h, vs_y(e - 1, NTB - 2 + tb) if NTB >= 2
                                              else vs_y(e - 1, tb))
                                for dh in range(2):
                                    mm = t.matmul(pB[tb % 2][:, dh * 512:(dh + 1) * 512],
                                                  p.ones_row[0:1, :],
                                                  p.bout_bf[0:1, e * D + dh * 512: e * D + dh * 512 + 512],
                                                  start=True, stop=False)
                                    for mt in range(NMT):
                                        mm = t.matmul(pB[tb % 2][:, dh * 512:(dh + 1) * 512],
                                                      h_sb[:, mt * CAP + tb * 128: mt * CAP + (tb + 1) * 128],
                                                      wo_sb[:, mt * D + dh * 512: mt * D + dh * 512 + 512],
                                                      start=False, stop=(mt == NMT - 1))
                                mm.then_inc(ts.h, 1); ts.n += 1

                    @block.scalar
                    def _(sc):
                        for e in range(NE):
                            for j in range(NTR):
                                tb, k = divmod(j, 8)
                                jj = EST * e + j
                                w = [(ts, ts_tr(e, j))]
                                if j == 0 and e >= 1:
                                    w.append((ts, ts_A(e - 1, 31)))
                                sop(sc, ss, lambda jj=jj, tb=tb, k=k, e=e: sc.copy(
                                    xgT[e % 2][:, k * CAP + tb * 128: k * CAP + (tb + 1) * 128],
                                    ptT[jj % 2][:, :]), waits=w)
                            for g2 in range(32):
                                sop(sc, ss, lambda e=e, g2=g2: sc.activation(
                                    h_sb[:, g2 * CAP:(g2 + 1) * CAP], pA[g2 % 2][:, 0:CAP], AF.Gelu,
                                    bias=p.bin_cols[:, e * 32 + g2: e * 32 + g2 + 1]),
                                    waits=[(ts, ts_A(e, g2))])

                    @block.vector
                    def _(v):
                        for e in range(NE):
                            gsl, gtot = p.mo_ga_tot[(tag, c, e)]
                            for tb in range(NTB):
                                sop(v, vs, lambda e=e, tb=tb: v.tensor_copy(
                                    ccol[:, tb:tb + 1],
                                    xg[e % 2][:, tb * 1032 + D + e: tb * 1032 + D + e + 1]),
                                    waits=[(p.gga[gsl], gtot)] if tb == 0 else [])
                                w = [(ts, ts_B(e, tb))]
                                E = c * NE + e
                                if E >= 1:
                                    pc, pe = divmod(E - 1, NE)
                                    ssl, stot = p.mo_sc_tot[(tag, pc, pe)]
                                    w.append((p.gsc[ssl], stot))
                                sop(v, vs, lambda e=e, tb=tb: v.tensor_scalar(
                                    y_sb[e % 2][:, tb * D:(tb + 1) * D], pB[tb % 2][:, :],
                                    ccol[:, tb:tb + 1], None,
                                    op0=ALU.mult), waits=w)

                assert ts.n == ts0 + EST * NE
                assert ss.n == ss0 + SSE * NE
                assert vs.n == vs0 + 2 * NTB * NE

            with nc.Block() as block:
                @block.sync
                def _(sync):
                    for _s in p.gsc:
                        sync.wait_ge(_s.h, _s.n)
                    sync.dma_start(rd[(NRC - 1) * RC:NRC * RC, :],
                                   p.moe_out[(NRC - 1) * RC:NRC * RC, :]).then_inc(st[3].h, 16)
                    st[3].n += 16
            drain_dmas(p)


def build_attn(p, isrc, idst):
    nc = p.nc
    B, T, NTOK = p.B, p.T, p.NTOK
    Tc = min(512, NTOK)
    ntt = Tc // 128
    ND = D // 128
    NTB = T // 128
    NH = 16
    HD = 64
    Wq = min(512, T)
    NQC = T // Wq
    nwt = Wq // 128
    rs, rd = p.resid[isrc], p.resid[idst]
    ts, vs, ss = p.ts, p.vs, p.ss
    ld, st = p.ld, p.st

    with ExitStack() as stage:
        sen = stage.enter_context
        v_sb = sen(nc.sbuf_tensor("at_v", [128, NTB * D], BF16))
        attnT = sen(nc.sbuf_tensor("at_attnT", [128, 8 * T], BF16))

        with ExitStack() as ctx:
            en = ctx.enter_context
            xch = [en(nc.sbuf_tensor(f"at_xch{i}", [128, ntt * D], F32)) for i in range(2)]
            xlnT = [en(nc.sbuf_tensor(f"at_xlnT{i}", [128, ND * Tc], BF16)) for i in range(2)]
            xtm = [en(nc.sbuf_tensor(f"at_xtm{i}", [128, D], BF16)) for i in range(2)]
            junk = [en(nc.sbuf_tensor("at_junk", [128, D], F32))] * 2
            stat = en(nc.sbuf_tensor("at_stat", [128, 5 * ntt], F32))
            qbuf = en(nc.sbuf_tensor("at_qbuf", [128, ND * Tc], BF16))
            kbuf = en(nc.sbuf_tensor("at_kbuf", [128, ND * Tc], BF16))
            wqk = en(nc.sbuf_tensor("at_wqk", [128, ND * 2 * D], BF16))
            wv = en(nc.sbuf_tensor("at_wv", [128, ND * D], BF16))
            ptT = [en(nc.psum_tensor(f"at_ptT{i}", [128, 128], BF16)) for i in range(2)]
            pQK = [en(nc.psum_tensor(f"at_pQK{i}", [128, 512], F32)) for i in range(2)]
            pV = [en(nc.psum_tensor(f"at_pV{i}", [128, D], F32)) for i in range(2)]
            NCHG = NTOK // Tc

            p.at_xch_ld = {}
            _lc = [ld[0].n, ld[1].n]
            for _c in range(NCHG):
                _lc[_c % 2] += 16
                p.at_xch_ld[_c] = _lc[_c % 2]

            R = {"xch": xch, "xlnT": xlnT, "xtm": xtm, "junk": junk, "stat": stat, "ptT": ptT,
                 "ntt": ntt, "xld": lambda c: (ld[c % 2], p.at_xch_ld[c])}

            def a1_load(sync, c):
                s2 = c % 2
                sync.wait_ge(st[s2].h, st[s2].n)
                sync.dma_start(xch[s2][:, :].rearrange("q (tt d) -> q tt d", tt=ntt),
                               rs[c * Tc:(c + 1) * Tc, :].rearrange("(tt q) d -> q tt d", q=128)
                               ).then_inc(ld[s2].h, 16)
                ld[s2].n += 16
                assert p.at_xch_ld[c] == ld[s2].n

            with nc.Block() as block:
                @block.sync
                def _(sync):
                    sync.wait_ge(p.gc3.h, p.cast_attn_tot)
                    sync.dma_start(wqk[:, :].rearrange("q (dt f) -> q dt f", dt=ND),
                                   p.qkv_bf[:, 0:2 * D].rearrange("(dt q) f -> q dt f", q=128)
                                   ).then_inc(ld[6].h, 16)
                    ld[6].n += 16
                    sync.dma_start(wv[:, :].rearrange("q (dt f) -> q dt f", dt=ND),
                                   p.qkv_bf[:, 2 * D:3 * D].rearrange("(dt q) f -> q dt f", q=128)
                                   ).then_inc(ld[6].h, 16)
                    ld[6].n += 16
                    a1_load(sync, 0)

            for c_glob in range(NCHG):
                b = (c_glob * Tc) // T
                t0 = c_glob * Tc - b * T
                s = c_glob % 2
                with nc.Block() as block:
                    ln0_ss_end = None
                    if c_glob == 0:
                        emit_ln(p, block, R, 0)
                        ln0_ss_end = ss.n

                    ts0, vs0, ss0 = ts.n, vs.n, ss.n
                    TS_QK0 = ts0
                    TS_V0 = TS_QK0 + 2 * ND
                    VS_E0 = vs0

                    @block.vector
                    def _(v):
                        for fq in range(ND):
                            sop(v, vs, lambda fq=fq: v.tensor_copy(
                                qbuf[:, fq * Tc:(fq + 1) * Tc], pQK[0][:, 0:Tc]),
                                waits=[(ts, TS_QK0 + 2 * fq + 1),
                                       (st[3], p.at_qst.get(c_glob - 1, 0))])
                            sop(v, vs, lambda fq=fq: v.tensor_copy(
                                kbuf[:, fq * Tc:(fq + 1) * Tc], pQK[1][:, 0:Tc]),
                                waits=[(ts, TS_QK0 + 2 * fq + 2)])
                        for tb in range(ntt):
                            sop(v, vs, lambda tb=tb: v.tensor_copy(
                                v_sb[:, (t0 // 128 + tb) * D:(t0 // 128 + tb + 1) * D],
                                pV[tb % 2][:, :]),
                                waits=[(ts, TS_V0 + tb + 1)])

                    @block.tensor
                    def _(t):
                        t.wait_ge(ld[6].h, ld[6].n)
                        if ln0_ss_end is not None:
                            t.wait_ge(ss.h, ln0_ss_end)
                        for fq in range(ND):
                            for qk in range(2):
                                g = 2 * fq + qk
                                if g >= 2:
                                    t.wait_ge(vs.h, VS_E0 + g - 1)
                                for k in range(ND):
                                    mm = t.matmul(pQK[g % 2][:, 0:Tc],
                                                  wqk[:, k * 2 * D + qk * D + fq * 128:
                                                      k * 2 * D + qk * D + (fq + 1) * 128],
                                                  xlnT[s][:, k * Tc:(k + 1) * Tc],
                                                  start=(k == 0), stop=(k == ND - 1))
                                mm.then_inc(ts.h, 1); ts.n += 1
                        for tb in range(ntt):
                            if tb >= 2:
                                t.wait_ge(vs.h, VS_E0 + 2 * ND + (tb - 2) + 1)
                            for k in range(ND):
                                for dh in range(2):
                                    mm = t.matmul(pV[tb % 2][:, dh * 512:(dh + 1) * 512],
                                                  xlnT[s][:, k * Tc + tb * 128: k * Tc + (tb + 1) * 128],
                                                  wv[:, k * D + dh * 512: k * D + dh * 512 + 512],
                                                  start=(k == 0), stop=(k == ND - 1))
                            mm.then_inc(ts.h, 1); ts.n += 1

                    assert ts.n == TS_V0 + ntt
                    assert vs.n == VS_E0 + 2 * ND + ntt

                    if c_glob + 1 < NCHG:
                        emit_ln(p, block, R, c_glob + 1, guard=(c_glob == 0))

                    @block.sync
                    def _(sync):
                        sync.wait_ge(vs.h, VS_E0 + 2 * ND)
                        sync.dma_start(
                            p.qT_scr[b][:, t0:t0 + Tc].rearrange("(ft q) t -> q ft t", q=128),
                            qbuf[:, :].rearrange("q (ft t) -> q ft t", ft=ND)).then_inc(st[3].h, 16)
                        st[3].n += 16
                        sync.dma_start(
                            p.kT_scr[b][:, t0:t0 + Tc].rearrange("(ft q) t -> q ft t", q=128),
                            kbuf[:, :].rearrange("q (ft t) -> q ft t", ft=ND)).then_inc(st[3].h, 16)
                        st[3].n += 16
                        if c_glob + 1 < NCHG:
                            a1_load(sync, c_glob + 1)
                p.at_qst[c_glob] = st[3].n
            drain_dmas(p)

        for b in range(B):
            with ExitStack() as ctx:
                en = ctx.enter_context
                qh = [en(nc.sbuf_tensor(f"at_qh{b}{i}", [128, Wq], BF16)) for i in range(2)]
                kh = [en(nc.sbuf_tensor(f"at_kh{b}{i}", [128, T], BF16)) for i in range(2)]
                esb = [en(nc.sbuf_tensor(f"at_esb{b}{i}", [128, 2 * Wq], BF16)) for i in range(4)]
                rsb = en(nc.sbuf_tensor(f"at_rsb{b}", [64, Wq], F32))
                pS = [en(nc.psum_tensor(f"at_pS{b}{i}", [128, 2 * Wq], F32)) for i in range(3)]
                pSum = en(nc.psum_tensor(f"at_pSum{b}", [64, Wq], F32))
                pPV = en(nc.psum_tensor(f"at_pPV{b}", [64, Wq], F32))

                ts0, vs0, ss0 = ts.n, vs.n, ss.n
                ld2_0 = [ld[2].n, ld[3].n]
                nk_of = [(qc + 1) * nwt for qc in range(NQC)]
                SEGS = [(h, qc) for h in range(NH) for qc in range(NQC)]
                PAIRS = []
                seg_first_pg = {}
                for (h, qc) in SEGS:
                    seg_first_pg[(h, qc)] = len(PAIRS)
                    for prl in range(nk_of[qc] // 2):
                        PAIRS.append((h, qc, [2 * prl, 2 * prl + 1]))
                NP = len(PAIRS)
                vs_seg = {}
                _v = 0
                for si, (h, qc) in enumerate(SEGS):
                    vs_seg[(h, qc)] = _v
                    _v += nwt + 2
                VS_TOT = _v
                tpos_S = {}
                tpos_PV = {}
                LAG = 2
                send_ts = {}
                for si, seg in enumerate(SEGS):
                    nxt = seg_first_pg[SEGS[si + 1]] if si + 1 < len(SEGS) else NP
                    ilast = nxt - 1
                    send_ts[seg] = ts0 + 2 * (ilast + 1) + 2 * max(0, ilast - LAG)

                with nc.Block() as block:
                    @block.sync
                    def _(sync):
                        gqc = 0
                        for h in range(NH):
                            pb = (h % 2) * 64
                            if h == 0:
                                sync.wait_ge(st[3].h, st[3].n)
                                sync.dma_start(kh[0][pb:pb + 64, :],
                                               p.kT_scr[b][0:64, :]).then_inc(ld[4].h, 16)
                                ld[4].n += 16
                                p.at_khld[(b, 0)] = (4, ld[4].n)
                            if h + 1 < NH:
                                pb2 = ((h + 1) % 2) * 64
                                sm = ld[4 + (h + 1) % 2]
                                if h >= 1:
                                    sync.wait_ge(ts.h, send_ts[(h - 1, NQC - 1)])
                                sync.dma_start(kh[(h + 1) % 2][pb2:pb2 + 64, :],
                                               p.kT_scr[b][(h + 1) * 64:(h + 2) * 64, :]
                                               ).then_inc(sm.h, 16)
                                sm.n += 16
                                p.at_khld[(b, h + 1)] = (4 + (h + 1) % 2, sm.n)
                            for qc in range(NQC):
                                sl = gqc % 2
                                if gqc >= 2:
                                    sync.wait_ge(ts.h, send_ts[SEGS[gqc - 2]])
                                sync.dma_start(
                                    qh[sl][pb:pb + 64, :],
                                    p.qT_scr[b][h * 64:(h + 1) * 64, qc * Wq:(qc + 1) * Wq]
                                    ).then_inc(ld[2 + sl].h, 16)
                                ld[2 + sl].n += 16
                                gqc += 1

                    @block.tensor
                    def _(t):
                        for i in range(NP + LAG):
                            if i < NP:
                                pg = i
                                h, qc, kts = PAIRS[pg]
                                pb = (h % 2) * 64
                                gqc = SEGS.index((h, qc))
                                if pg == seg_first_pg[(h, qc)]:
                                    t.wait_ge(ld[2 + gqc % 2].h,
                                              ld2_0[gqc % 2] + 16 * (gqc // 2 + 1))
                                    if qc == 0:
                                        khs, khn = p.at_khld[(b, h)]
                                        t.wait_ge(ld[khs].h, khn)
                                if pg >= 3:
                                    t.wait_ge(ss.h, ss0 + pg - 2)
                                for half in range(2):
                                    kt = kts[half]
                                    mm = t.matmul(pS[pg % 3][:, half * Wq:(half + 1) * Wq],
                                                  kh[h % 2][pb:pb + 64, kt * 128:(kt + 1) * 128],
                                                  qh[(gqc % 2)][pb:pb + 64, :],
                                                  start=True, stop=True)
                                    mm.then_inc(ts.h, 1); ts.n += 1
                                    tpos_S[(h, qc, kt)] = ts.n
                            if i >= LAG:
                                pg2 = i - LAG
                                h2, qc2, kts2 = PAIRS[pg2]
                                nk2 = nk_of[qc2]
                                if pg2 == seg_first_pg[(h2, qc2)]:
                                    si2 = SEGS.index((h2, qc2))
                                    if si2 >= 1:
                                        t.wait_ge(vs.h, vs0 + vs_seg[(h2, qc2)])
                                t.wait_ge(ss.h, ss0 + pg2 + 1)
                                for half in range(2):
                                    kt = kts2[half]
                                    t.matmul(pSum[0:64, :], p.ones_bf[:, 0:64],
                                             esb[pg2 % 4][:, half * Wq:(half + 1) * Wq],
                                             start=(kt == 0), stop=(kt == nk2 - 1))
                                    mm = t.matmul(pPV[0:64, :],
                                                  v_sb[:, kt * D + h2 * 64: kt * D + h2 * 64 + 64],
                                                  esb[pg2 % 4][:, half * Wq:(half + 1) * Wq],
                                                  start=(kt == 0), stop=(kt == nk2 - 1))
                                    mm.then_inc(ts.h, 1); ts.n += 1
                                    tpos_PV[(h2, qc2, kt)] = ts.n

                    @block.scalar
                    def _(sc):
                        for pg in range(NP):
                            h, qc, kts = PAIRS[pg]
                            w = [(ts, tpos_S[(h, qc, kts[1])])]
                            for kt in kts:
                                mp = kt - qc * nwt
                                if mp >= 0:
                                    w.append((vs, vs0 + vs_seg[(h, qc)] + mp + 1))
                            if pg >= 4:
                                h4, qc4, kts4 = PAIRS[pg - 4]
                                w.append((ts, tpos_PV[(h4, qc4, kts4[1])]))
                            sop(sc, ss, lambda pg=pg: sc.activation(
                                esb[pg % 4][:, :], pS[pg % 3][:, :], AF.Exp, scale=0.125),
                                waits=w)

                    @block.vector
                    def _(v):
                        for (h, qc) in SEGS:
                            pb = (h % 2) * 64
                            fg = h // 2
                            for mp in range(nwt):
                                kt = qc * nwt + mp
                                pg = seg_first_pg[(h, qc)] + kt // 2
                                sop(v, vs, lambda pg=pg, kt=kt, mp=mp: v.tensor_tensor(
                                    pS[pg % 3][:, (kt % 2) * Wq:(kt % 2 + 1) * Wq],
                                    pS[pg % 3][:, (kt % 2) * Wq:(kt % 2 + 1) * Wq],
                                    p.masks[:, mp * 512: mp * 512 + Wq], op=ALU.add),
                                    waits=[(ts, tpos_S[(h, qc, kt)])])
                            sop(v, vs, lambda: v.reciprocal(rsb[0:64, :], pSum[0:64, :]),
                                waits=[(ts, tpos_PV[(h, qc, nk_of[qc] - 1)])])
                            sop(v, vs, lambda h=h, qc=qc, pb=pb, fg=fg: v.tensor_tensor(
                                attnT[pb:pb + 64, fg * T + qc * Wq: fg * T + (qc + 1) * Wq],
                                pPV[0:64, :], rsb[0:64, :], op=ALU.mult))

                assert ss.n == ss0 + NP
                assert vs.n == vs0 + VS_TOT


            with ExitStack() as ctx:
                en = ctx.enter_context
                xt = [en(nc.sbuf_tensor(f"at_xt{b}{i}", [128, D], F32)) for i in range(2)]
                wo2 = en(nc.sbuf_tensor(f"at_wo2{b}", [128, ND * D], BF16))
                pO = [en(nc.psum_tensor(f"at_pO{b}{i}", [128, D], F32)) for i in range(2)]
                ts0, vs0 = ts.n, vs.n
                ld0s = [ld[0].n, ld[1].n]
                st0s = [st[0].n, st[1].n]
                with nc.Block() as block:
                    @block.sync
                    def _(sync):
                        sync.dma_start(wo2[:, :].rearrange("q (dt f) -> q dt f", dt=ND),
                                       p.wo_bf.rearrange("(dt q) f -> q dt f", q=128)
                                       ).then_inc(ld[6].h, 16)
                        ld[6].n += 16

                    @block.tensor
                    def _(t):
                        t.wait_ge(ld[6].h, ld[6].n)
                        for tt in range(NTB):
                            if tt >= 2:
                                t.wait_ge(vs.h, vs0 + (tt - 2) + 1)
                            for ft in range(ND):
                                for dh in range(2):
                                    mm = t.matmul(pO[tt % 2][:, dh * 512:(dh + 1) * 512],
                                                  attnT[:, ft * T + tt * 128: ft * T + (tt + 1) * 128],
                                                  wo2[:, ft * D + dh * 512: ft * D + dh * 512 + 512],
                                                  start=(ft == 0), stop=(ft == ND - 1))
                            mm.then_inc(ts.h, 1); ts.n += 1

                    @block.vector
                    def _(v):
                        for tt in range(NTB):
                            sl = tt % 2
                            sop(v, vs, lambda tt=tt, sl=sl: v.scalar_tensor_tensor(
                                xt[sl][:, :], pO[tt % 2][:, :], 1.0, xt[sl][:, :],
                                op0=ALU.mult, op1=ALU.add),
                                waits=[(ts, ts0 + tt + 1),
                                       (ld[sl], ld0s[sl] + 16 * (tt // 2 + 1))])

                    @block.sync
                    def _(sync):
                        for tt in range(NTB):
                            sl = tt % 2
                            if tt >= 2:
                                sync.wait_ge(vs.h, vs0 + (tt - 2) + 1)
                                sync.wait_ge(st[sl].h, st0s[sl] + 16 * (tt // 2))
                            sync.dma_start(
                                xt[sl][:, :],
                                rs[b * T + tt * 128: b * T + (tt + 1) * 128, :]).then_inc(ld[sl].h, 16)
                            ld[sl].n += 16
                            if tt >= 1:
                                psl = (tt - 1) % 2
                                sync.wait_ge(vs.h, vs0 + tt)
                                sync.dma_start(rd[b * T + (tt - 1) * 128: b * T + tt * 128, :],
                                               xt[psl][:, :]).then_inc(st[psl].h, 16)
                                st[psl].n += 16
                        sync.wait_ge(vs.h, vs0 + NTB)
                        lsl = (NTB - 1) % 2
                        sync.dma_start(rd[b * T + (NTB - 1) * 128: b * T + NTB * 128, :],
                                       xt[lsl][:, :]).then_inc(st[lsl].h, 16)
                        st[lsl].n += 16
        drain_dmas(p)



def kernel(**inputs):
    cfg = cfg_full()
    nc = build_program(cfg)
    x = np.ascontiguousarray(np.asarray(inputs["L0_attn_x"]).reshape(16 * 2048, 1024))
    conv_w = np.ascontiguousarray(np.asarray(inputs["L0_attn_conv_w"]).reshape(DI, KCONV).T)
    names = {
        "L0_attn_in_proj_w": inputs["L0_attn_in_proj_w"],
        "L0_attn_conv_w": conv_w,
        "L0_attn_conv_b": inputs["L0_attn_conv_b"],
        "L0_attn_x_proj_w": inputs["L0_attn_x_proj_w"],
        "L0_attn_dt_proj_w": inputs["L0_attn_dt_proj_w"],
        "L0_attn_dt_proj_b": inputs["L0_attn_dt_proj_b"],
        "L0_attn_A_log": inputs["L0_attn_A_log"],
        "L0_attn_D": inputs["L0_attn_D"],
        "L0_attn_out_proj_w": inputs["L0_attn_out_proj_w"],
        "L0_ffn_w_gate": inputs["L0_ffn_w_gate"],
        "L0_ffn_w_up": inputs["L0_ffn_w_up"],
        "L0_ffn_w_down": inputs["L0_ffn_w_down"],
        "L1_attn_w_qkv": inputs["L1_attn_w_qkv"],
        "L1_attn_w_out": inputs["L1_attn_w_out"],
        "L1_ffn_W_gate": inputs["L1_ffn_W_gate"],
        "L1_ffn_W_in": inputs["L1_ffn_W_in"],
        "L1_ffn_b_in": inputs["L1_ffn_b_in"],
        "L1_ffn_W_out": inputs["L1_ffn_W_out"],
        "L1_ffn_b_out": inputs["L1_ffn_b_out"],
    }
    names = {k: np.ascontiguousarray(np.asarray(v)) for k, v in names.items()}
    in_maps = []
    ntok = cfg["B"] * cfg["T"]
    for c in range(8):
        m = dict(names)
        m["x0"] = x[c * ntok:(c + 1) * ntok]
        in_maps.append(m)
    res = run_bass_kernel_spmd(nc, in_maps, core_ids=list(range(8)))
    out = np.concatenate([np.asarray(r["y"]) for r in res.results], axis=0)
    return out.reshape(16, 2048, 1024).astype(np.float32)


# version 12
# speedup vs baseline: 1.0424x; 1.0117x over previous
import numpy as np
import ml_dtypes
import concourse.bass as bass
import concourse.mybir as mybir
from concourse.bass_utils import run_bass_kernel_spmd
from concourse.bass import IndirectOffsetOnAxis
from contextlib import ExitStack

F32 = mybir.dt.float32
BF16 = mybir.dt.bfloat16
I32 = mybir.dt.int32
AF = mybir.ActivationFunctionType
ALU = mybir.AluOpType
AX = mybir.AxisListType

D = 1024
DI = 2048
DT = 64
KCONV = 4
H3 = 3072
DM = 4096
NE = 8
QSCALE = 1.0 / 127.0
EPS = 1e-5
NEG = -1e9


def cfg_full():
    return {"B": 2, "T": 2048}


class Sem:

    def __init__(self, nc, name):
        self.h = nc.alloc_semaphore(name=name)
        self.n = 0

    def inc(self, instr, k=1):
        instr.then_inc(self.h, k)
        self.n += k
        return self.n


class P:
    pass


def build_program(cfg, dbg=False, nstages=6):
    B, T = cfg["B"], cfg["T"]
    NTOK = B * T
    NTT = NTOK // 128
    assert T % 128 == 0

    nc = bass.Bass("TRN2", target_bir_lowering=False)
    p = P()
    p.nc = nc
    p.B, p.T, p.NTOK, p.NTT = B, T, NTOK, NTT

    ap = lambda t: t.ap()
    p.x_in = ap(nc.dram_tensor("x0", [NTOK, D], F32, kind="ExternalInput"))
    p.w_inproj = ap(nc.dram_tensor("L0_attn_in_proj_w", [2 * DI, D], F32, kind="ExternalInput"))
    p.w_conv = ap(nc.dram_tensor("L0_attn_conv_w", [KCONV, DI], F32, kind="ExternalInput"))
    p.b_conv = ap(nc.dram_tensor("L0_attn_conv_b", [DI], F32, kind="ExternalInput"))
    p.w_xproj = ap(nc.dram_tensor("L0_attn_x_proj_w", [DT, DI], F32, kind="ExternalInput"))
    p.w_dtproj = ap(nc.dram_tensor("L0_attn_dt_proj_w", [DI, DT], F32, kind="ExternalInput"))
    p.b_dtproj = ap(nc.dram_tensor("L0_attn_dt_proj_b", [DI], F32, kind="ExternalInput"))
    p.a_log = ap(nc.dram_tensor("L0_attn_A_log", [DI], F32, kind="ExternalInput"))
    p.d_vec = ap(nc.dram_tensor("L0_attn_D", [DI], F32, kind="ExternalInput"))
    p.w_outproj = ap(nc.dram_tensor("L0_attn_out_proj_w", [D, DI], F32, kind="ExternalInput"))
    p.w_gate_i = ap(nc.dram_tensor("L0_ffn_w_gate", [H3, D], I32, kind="ExternalInput"))
    p.w_up_i = ap(nc.dram_tensor("L0_ffn_w_up", [H3, D], I32, kind="ExternalInput"))
    p.w_down_i = ap(nc.dram_tensor("L0_ffn_w_down", [D, H3], I32, kind="ExternalInput"))
    p.w_qkv = ap(nc.dram_tensor("L1_attn_w_qkv", [D, 3 * D], F32, kind="ExternalInput"))
    p.w_wo = ap(nc.dram_tensor("L1_attn_w_out", [D, D], F32, kind="ExternalInput"))
    p.w_moe_gate = ap(nc.dram_tensor("L1_ffn_W_gate", [D, NE], F32, kind="ExternalInput"))
    p.w_moe_in = ap(nc.dram_tensor("L1_ffn_W_in", [NE, D, DM], F32, kind="ExternalInput"))
    p.b_moe_in = ap(nc.dram_tensor("L1_ffn_b_in", [NE, DM], F32, kind="ExternalInput"))
    p.w_moe_out = ap(nc.dram_tensor("L1_ffn_W_out", [NE, DM, D], F32, kind="ExternalInput"))
    p.b_moe_out = ap(nc.dram_tensor("L1_ffn_b_out", [NE, D], F32, kind="ExternalInput"))
    p.y_out = ap(nc.dram_tensor("y", [NTOK, D], F32, kind="ExternalOutput"))

    if dbg:
        dr = lambda name, shape, dt=BF16: nc.dram_tensor(name, shape, dt, kind="ExternalOutput").ap()
    else:
        dr = lambda name, shape, dt=BF16: nc.dram_tensor(name, shape, dt).ap()
    p.resid = [dr("resid_a", [NTOK, D], F32), dr("resid_b", [NTOK, D], F32)]
    p.l0_gu = dr("l0_gu", [H3 // 128, 128, 2 * D])
    p.l0_dT = dr("l0_dT", [H3, D])
    p.inprojT = dr("inprojT", [D, 2 * DI])
    p.outprojT = dr("outprojT", [DI, D])
    p.xprojT = dr("xprojT", [DI, DT])
    p.dtprojT = dr("dtprojT", [DT, DI])
    p.qkv_bf = dr("qkv_bf", [D, 3 * D])
    p.wo_bf = dr("wo_bf", [D, D])
    p.wgate_bf = dr("wgate_bf", [D, NE])
    p.win_bf = dr("win_bf", [NE, D, DM])
    p.wout_bf = dr("wout_bf", [NE, DM, D])
    p.xz_scr = dr("xz_scr", [B, 2 * DI, T], F32)
    p.xiact_f = dr("xiact_f", [B, DI, T], F32)
    p.xiact_b = dr("xiact_b", [B, DI, T])
    p.qT_scr = dr("qT_scr", [B, D, T])
    p.kT_scr = dr("kT_scr", [B, D, T])
    p.xlnT_scr = dr("xlnT_scr", [D, NTOK])
    p.xtm_scr = dr("xtm_scr", [NTOK + 1, 1032])
    p.moe_out = dr("moe_out", [NTOK + 1, D], F32)
    NRC_ = max(1, NTOK // 1024)
    p.idx_scr = dr("idx_scr", [NRC_, 4224], I32)
    p.idx_fill = dr("idx_fill", [4224], I32)
    if dbg:
        p.dbg_call = [dr(f"dbg_call{i}", [128, NTT * NE], F32) for i in range(2)]
        p.dbg_xln = dr("dbg_xln", [128, 8 * 512])
        p.dbg_h = dr("dbg_h", [128, 24 * 512])
        p.dbg_stat = dr("dbg_stat", [128, 20], F32)
        p.dbg_xch = dr("dbg_xch", [128, 4 * 1024], F32)
    p.dbg = dbg

    p.ts = Sem(nc, "ts")
    p.vs = Sem(nc, "vs")
    p.ss = Sem(nc, "ss")
    p.gs = Sem(nc, "gs")
    p.ld = [Sem(nc, f"ld{i}") for i in range(8)]
    p.st = [Sem(nc, f"st{i}") for i in range(4)]
    p.gld = [Sem(nc, f"gld{i}") for i in range(2)]
    p.gcast = Sem(nc, "gcast")
    p.gga = [Sem(nc, f"gga{i}") for i in range(2)]
    p.gidx = Sem(nc, "gidx")
    p.gc2 = Sem(nc, "gc2")
    p.gc3 = Sem(nc, "gc3")
    p.gsc = [Sem(nc, f"gsc{i}") for i in range(2)]

    p.stack = ExitStack()
    en = p.stack.enter_context
    p.ident = en(nc.sbuf_tensor("ident", [128, 128], BF16))
    p.masks = en(nc.sbuf_tensor("masks", [128, 4 * 512], BF16))
    p.ones_bf = en(nc.sbuf_tensor("ones_bf", [128, 64], BF16))
    p.ones_f1 = en(nc.sbuf_tensor("ones_f1", [1, 8], F32))
    p.bcols = en(nc.sbuf_tensor("bcols", [128, 128], F32))
    p.bin_cols = en(nc.sbuf_tensor("bin_cols", [128, 256], F32))
    p.bout_bf = en(nc.sbuf_tensor("bout_bf", [1, NE * D], BF16))
    p.wg_hi = en(nc.sbuf_tensor("wg_hi", [128, (D // 128) * NE], BF16))
    p.wg_lo = en(nc.sbuf_tensor("wg_lo", [128, (D // 128) * NE], BF16))
    p.ones_row = en(nc.sbuf_tensor("ones_row", [1, 128], BF16))
    p.tri = en(nc.sbuf_tensor("tri", [128, 128], BF16))
    p.ones_f32r = en(nc.sbuf_tensor("ones_f32r", [1, 128], F32))
    p.ebase = en(nc.sbuf_tensor("ebase", [1, NE], F32))
    p.tokid = en(nc.sbuf_tensor("tokid", [128, 64], I32))

    p.mb_dtmm = {}
    p.mo_ga_tot = {}
    p.mo_ini_tot = {}
    p.mo_idx_tot = {}
    p.mo_sc_tot = {}
    p.at_qst = {}
    p.at_khld = {}
    p.mb_strip_ld = {}
    p.mb_kt_done = {}
    build_prepass(p)
    stages = [
        lambda: build_ffn0(p, 0, 1, tag=0),
        lambda: build_mamba(p, 1, 0),
        lambda: build_ffn0(p, 0, 1, tag=1),
        lambda: build_moe(p, 1, 0, tag=0),
        lambda: build_attn(p, 0, 1),
        lambda: build_moe(p, 1, 0, final=True, tag=1),
    ]
    for f in stages[:nstages]:
        f()
    build_drain(p)
    p.stack.close()
    return nc


def build_drain(p):
    nc = p.nc
    with nc.Block() as block:
        @block.sync
        def _(sync):
            for s in p.ld + p.st + p.gld + p.gga + p.gsc + [p.gcast, p.gidx,
                                                            p.gc2, p.gc3]:
                sync.wait_ge(s.h, s.n)


def build_prepass(p):
    nc = p.nc
    ld, st = p.ld, p.st
    ts, vs, ss, gs = p.ts, p.vs, p.ss, p.gs
    gld = p.gld
    gcast = p.gcast

    NHT = H3 // 128
    ND = D // 128
    NDI = DI // 128

    with ExitStack() as ctx:
        en = ctx.enter_context
        wrow = [en(nc.sbuf_tensor(f"pp_wrow{i}", [128, H3], BF16)) for i in range(2)]
        strip = [en(nc.sbuf_tensor(f"pp_strip{i}", [128, 2 * D], BF16)) for i in range(2)]
        dstrips = en(nc.sbuf_tensor("pp_dstrips", [128, (NHT // 2) * D], BF16))
        istrips = en(nc.sbuf_tensor("pp_istrips", [128, ND * DI], BF16))
        ostrips = en(nc.sbuf_tensor("pp_ostrips", [128, NDI * D], BF16))
        xstrip = en(nc.sbuf_tensor("pp_xstrip", [128, NDI * DT], BF16))
        dtstrip = en(nc.sbuf_tensor("pp_dtstrip", [64, DI], BF16))
        rowbuf = en(nc.sbuf_tensor("pp_rowbuf", [1, DM // 2], F32))
        bout_rows = en(nc.sbuf_tensor("pp_bout_rows", [1, NE * D], F32))
        ptr = [en(nc.psum_tensor(f"pp_ptr{i}", [128, 128], BF16)) for i in range(2)]
        pb = en(nc.psum_tensor("pp_pb", [128, 512], F32))

        with nc.Block() as block:
            @block.gpsimd
            def _(g):
                g.dma_start(p.wgate_bf, p.w_moe_gate).then_inc(gcast.h, 16); gcast.n += 16
                p.cast_gate_tot = gcast.n
                g.dma_start(p.win_bf, p.w_moe_in).then_inc(p.gc2.h, 16); p.gc2.n += 16
                g.dma_start(p.wout_bf, p.w_moe_out).then_inc(p.gc2.h, 16); p.gc2.n += 16
                p.cast_moe_tot = p.gc2.n
                g.dma_start(p.qkv_bf, p.w_qkv).then_inc(p.gc3.h, 16); p.gc3.n += 16
                g.dma_start(p.wo_bf, p.w_wo).then_inc(p.gc3.h, 16); p.gc3.n += 16
                p.cast_attn_tot = p.gc3.n
                g.memset(p.ident[:], 0.0).then_inc(gs.h, 1); gs.n += 1
                g.memset(p.ones_bf[:], 1.0).then_inc(gs.h, 1); gs.n += 1
                g.memset(p.ones_row[:], 1.0).then_inc(gs.h, 1); gs.n += 1
                g.memset(p.ones_f1[:], 1.0).then_inc(gs.h, 1); gs.n += 1
                g.memset(p.masks[:], 0.0).then_inc(gs.h, 1); gs.n += 1
                g.memset(p.tri[:], 1.0).then_inc(gs.h, 1); gs.n += 1
                g.memset(p.ones_f32r[:], 1.0).then_inc(gs.h, 1); gs.n += 1
                g.iota(p.ebase[0:1, :], [[min(1024, p.NTOK) // 2, NE]], base=-1,
                       channel_multiplier=0,
                       allow_small_or_imprecise_dtypes=True).then_inc(gs.h, 1); gs.n += 1
                g.iota(p.tokid[:, 0:p.NTT], [[128, p.NTT]], base=0,
                       channel_multiplier=1).then_inc(gs.h, 1); gs.n += 1
                g.wait_ge(gs.h, gs.n)
                g.affine_select(out=p.ident[:], in_=p.ident[:], compare_op=ALU.not_equal,
                                fill=1.0, base=0, pattern=[[-1, 128]], channel_multiplier=1).then_inc(gs.h, 1)
                gs.n += 1
                g.affine_select(out=p.tri[:], in_=p.tri[:], compare_op=ALU.is_ge, fill=0.0,
                                base=0, pattern=[[1, 128]], channel_multiplier=-1).then_inc(gs.h, 1)
                gs.n += 1
                for mp in range(4):
                    g.affine_select(out=p.masks[:, mp * 512:(mp + 1) * 512],
                                    in_=p.masks[:, mp * 512:(mp + 1) * 512],
                                    compare_op=ALU.is_ge, fill=NEG, base=-mp * 128,
                                    pattern=[[1, 512]], channel_multiplier=-1).then_inc(gs.h, 1)
                    gs.n += 1

            @block.sync
            def _(sync):
                sync.dma_start(p.resid[0], p.x_in).then_inc(st[0].h, 16); st[0].n += 16
                sync.dma_start(bout_rows[0:1, :],
                               p.b_moe_out.rearrange("e d -> (e d)")[None, :]).then_inc(ld[1].h, 16)
                ld[1].n += 16

        rows = ([("bin", e * 2 + hh) for e in range(NE) for hh in range(2)]
                + [("vec", i) for i in range(4)] + [("cw", k) for k in range(KCONV)])
        vec_src = [p.b_conv, p.b_dtproj, p.a_log, p.d_vec]
        ts0, ld0 = ts.n, ld[0].n
        with nc.Block() as block:
            @block.sync
            def _(sync):
                for i, (kind, idx) in enumerate(rows):
                    if i >= 1:
                        sync.wait_ge(ts.h, ts0 + i)
                    if kind == "bin":
                        e2, hh = divmod(idx, 2)
                        sync.dma_start(rowbuf[0:1, 0:DM // 2],
                                       p.b_moe_in[e2, hh * (DM // 2):(hh + 1) * (DM // 2)][None, :]
                                       ).then_inc(ld[0].h, 16)
                    elif kind == "vec":
                        sync.dma_start(rowbuf[0:1, 0:DI], vec_src[idx][None, :]).then_inc(ld[0].h, 16)
                    else:
                        sync.dma_start(rowbuf[0:1, 0:DI], p.w_conv[idx, :][None, :]).then_inc(ld[0].h, 16)
                    ld[0].n += 16

            @block.tensor
            def _(t):
                for i, (kind, idx) in enumerate(rows):
                    t.wait_ge(ld[0].h, ld0 + 16 * (i + 1))
                    ncols = 16
                    for c in range(ncols):
                        if kind == "bin":
                            col = idx * 16 + c
                        elif kind == "vec":
                            col = 256 + idx * 16 + c
                        else:
                            col = 320 + idx * 16 + c
                        mm = t.matmul(pb[:, col:col + 1], rowbuf[0:1, c * 128:(c + 1) * 128],
                                      p.ones_f1[0:1, 0:1], start=True, stop=True)
                    mm.then_inc(ts.h, 1); ts.n += 1

            @block.vector
            def _(v):
                v.wait_ge(ts.h, ts.n)
                v.tensor_copy(p.bin_cols[:], pb[:, 0:256]).then_inc(vs.h, 1); vs.n += 1
                v.tensor_copy(p.bcols[:], pb[:, 256:384]).then_inc(vs.h, 1); vs.n += 1
                v.wait_ge(ss.h, ss.n + 1)
                v.tensor_scalar(p.bcols[:, 32:48], p.bcols[:, 32:48], 10.0, 1e-6,
                                op0=ALU.min, op1=ALU.max).then_inc(vs.h, 1); vs.n += 1
                v.wait_ge(vs.h, vs.n)
                v.tensor_scalar(p.bcols[:, 32:48], p.bcols[:, 32:48], -1.0, None,
                                op0=ALU.mult).then_inc(vs.h, 1); vs.n += 1

            vsnap = vs.n - 2

            @block.scalar
            def _(sc):
                sc.wait_ge(vs.h, vsnap)
                sc.activation(p.bcols[:, 32:48], p.bcols[:, 32:48], AF.Exp).then_inc(ss.h, 1); ss.n += 1

        with ExitStack() as ctx2:
            wgf = ctx2.enter_context(nc.sbuf_tensor("pp_wgf", [128, (D // 128) * NE], F32))
            fillb = ctx2.enter_context(nc.sbuf_tensor("pp_fillb", [128, 33], I32))
            zrow = ctx2.enter_context(nc.sbuf_tensor("pp_zrow", [1, 1032], BF16))
            with nc.Block() as block:
                @block.gpsimd
                def _(g):
                    g.iota(fillb[:, :], [[0, 33]], base=p.NTOK,
                           channel_multiplier=0).then_inc(gs.h, 1); gs.n += 1
                    g.memset(zrow[0:1, :], 0.0).then_inc(gs.h, 1); gs.n += 1
                    g.wait_ge(gs.h, gs.n)
                    g.dma_start(p.idx_fill.rearrange("(q cc) -> q cc", cc=33),
                                fillb[:, :]).then_inc(p.gld[0].h, 16)
                    p.gld[0].n += 16
                    g.dma_start(p.xtm_scr[p.NTOK:p.NTOK + 1, :], zrow[0:1, :]).then_inc(p.gld[0].h, 16)
                    p.gld[0].n += 16
                @block.sync
                def _(sync):
                    sync.dma_start(wgf[:, :].rearrange("q (dt e) -> q dt e", dt=D // 128),
                                   p.w_moe_gate.rearrange("(dt q) e -> q dt e", q=128)
                                   ).then_inc(ld[2].h, 16)
                    ld[2].n += 16
                    sync.wait_ge(gcast.h, p.cast_gate_tot)
                    sync.dma_start(p.wg_hi[:, :].rearrange("q (dt e) -> q dt e", dt=D // 128),
                                   p.wgate_bf.rearrange("(dt q) e -> q dt e", q=128)
                                   ).then_inc(ld[3].h, 16)
                    ld[3].n += 16

                @block.vector
                def _(v):
                    sop(v, vs, lambda: v.tensor_tensor(p.wg_lo[:, :], wgf[:, :], p.wg_hi[:, :],
                                                       op=ALU.subtract),
                        waits=[(ld[2], ld[2].n), (ld[3], ld[3].n)])
                    sop(v, vs, lambda: v.tensor_copy(p.bout_bf[0:1, :], bout_rows[0:1, :]),
                        waits=[(ld[1], ld[1].n)])
            drain_dmas(p)

        units = []
        for ht in range(NHT):
            units.append(("gu", ht, 2, 2 * ND))
        for half in range(2):
            for dt_i in range(ND):
                units.append(("wd", half * ND + dt_i, 1, NHT // 2))
        for half in range(2):
            for ft in range(2 * ND):
                units.append(("ip", half * 2 * ND + ft, 1, ND))
        for dt_i in range(ND):
            units.append(("op", dt_i, 1, NDI))
        units.append(("xp", 0, 1, NDI))
        for kt in range(NDI):
            units.append(("dp", kt, 1, 1))

        ts_base, vs_base = ts.n, vs.n
        gld_tot = [gld[0].n, gld[1].n]
        unit_gld = []
        unit_tcum = []
        tc = 0
        for i, (kind, idx, nl, nt) in enumerate(units):
            slot = i % 2
            gld_tot[slot] += 16 * nl
            unit_gld.append((slot, gld_tot[slot]))
            tc += nt
            unit_tcum.append(tc)
        NIP = 2 * ND

        def stores_after(i):
            kind, idx, _, _ = units[i]
            out = []
            if kind == "gu":
                out.append((p.l0_gu[idx, :, :], strip[idx % 2][:, :]))
            elif kind == "wd" and idx % ND == ND - 1:
                half = idx // ND
                out.append((p.l0_dT[half * (NHT // 2) * 128:(half + 1) * (NHT // 2) * 128, :]
                            .rearrange("(ht q) d -> q ht d", q=128),
                            dstrips[:, :].rearrange("q (ht d) -> q ht d", ht=NHT // 2)))
            elif kind == "ip" and idx == NIP - 1:
                out.append((p.inprojT[:, 0:DI].rearrange("(dt q) f -> q dt f", q=128),
                            istrips[:, :].rearrange("q (dt f) -> q dt f", dt=ND)))
            elif kind == "ip" and idx == 2 * NIP - 1:
                out.append((p.inprojT[:, DI:2 * DI].rearrange("(dt q) f -> q dt f", q=128),
                            istrips[:, :].rearrange("q (dt f) -> q dt f", dt=ND)))
            elif kind == "op" and idx == ND - 1:
                out.append((p.outprojT.rearrange("(kt q) d -> q kt d", q=128),
                            ostrips[:, :].rearrange("q (kt d) -> q kt d", kt=NDI)))
            elif kind == "xp":
                out.append((p.xprojT.rearrange("(kt q) r -> q kt r", q=128),
                            xstrip[:, :].rearrange("q (kt r) -> q kt r", kt=NDI)))
            elif kind == "dp" and idx == NDI - 1:
                out.append((p.dtprojT[:, :], dtstrip[0:64, :]))
            return out

        st_plan = {}
        st_tot = {j: p.st[j].n for j in range(4)}
        sidx = 0
        for i in range(len(units)):
            for _ in stores_after(i):
                sl = sidx % 4
                st_tot[sl] += 16
                st_plan.setdefault(i, []).append((sl, st_tot[sl]))
                sidx += 1

        with nc.Block() as block:
            @block.gpsimd
            def _(g):
                for i, (kind, idx, nl, nt) in enumerate(units):
                    slot, tot = unit_gld[i]
                    buf = wrow[slot]
                    if i >= 2:
                        g.wait_ge(ts.h, ts_base + unit_tcum[i - 2])
                    if kind == "gu":
                        g.dma_start(buf[:, 0:D], p.w_gate_i[idx * 128:(idx + 1) * 128, :]).then_inc(gld[slot].h, 16)
                        g.dma_start(buf[:, D:2 * D], p.w_up_i[idx * 128:(idx + 1) * 128, :]).then_inc(gld[slot].h, 16)
                    elif kind == "wd":
                        g.dma_start(buf[:, 0:H3], p.w_down_i[(idx % ND) * 128:(idx % ND + 1) * 128, :]).then_inc(gld[slot].h, 16)
                    elif kind == "ip":
                        g.dma_start(buf[:, 0:D], p.w_inproj[idx * 128:(idx + 1) * 128, :]).then_inc(gld[slot].h, 16)
                    elif kind == "op":
                        g.dma_start(buf[:, 0:DI], p.w_outproj[idx * 128:(idx + 1) * 128, :]).then_inc(gld[slot].h, 16)
                    elif kind == "xp":
                        g.dma_start(buf[0:64, 0:DI], p.w_xproj[:, :]).then_inc(gld[slot].h, 16)
                    elif kind == "dp":
                        g.dma_start(buf[:, 0:DT], p.w_dtproj[idx * 128:(idx + 1) * 128, :]).then_inc(gld[slot].h, 16)
                    gld[slot].n += 16

            @block.tensor
            def _(t):
                j = 0

                def tr(in_ap, out_ap):
                    nonlocal j
                    if j >= 2:
                        t.wait_ge(vs.h, vs_base + j - 1)
                    t.transpose(out_ap, in_ap, p.ident[0:in_ap.shape[0], 0:in_ap.shape[0]]).then_inc(ts.h, 1)
                    ts.n += 1
                    j += 1

                for i, (kind, idx, nl, nt) in enumerate(units):
                    slot, tot = unit_gld[i]
                    buf = wrow[slot]
                    t.wait_ge(gld[slot].h, tot)
                    if kind == "gu":
                        for gu in range(2):
                            for dt_i in range(ND):
                                tr(buf[:, gu * D + dt_i * 128: gu * D + (dt_i + 1) * 128], ptr[j % 2][:, :])
                    elif kind == "wd":
                        half, dt2 = divmod(idx, ND)
                        for ht in range(NHT // 2):
                            tr(buf[:, (half * (NHT // 2) + ht) * 128:(half * (NHT // 2) + ht + 1) * 128],
                               ptr[j % 2][:, :])
                    elif kind == "ip":
                        for dt_i in range(ND):
                            tr(buf[:, dt_i * 128:(dt_i + 1) * 128], ptr[j % 2][:, :])
                    elif kind == "op":
                        for kt in range(NDI):
                            tr(buf[:, kt * 128:(kt + 1) * 128], ptr[j % 2][:, :])
                    elif kind == "xp":
                        for kt in range(NDI):
                            tr(buf[0:64, kt * 128:(kt + 1) * 128], ptr[j % 2][0:128, 0:64])
                    elif kind == "dp":
                        tr(buf[:, 0:DT], ptr[j % 2][0:64, 0:128])

            @block.vector
            def _(v):
                j = 0

                def ev(dst_ap, src_ap, i):
                    nonlocal j
                    v.wait_ge(ts.h, ts_base + j + 1)
                    v.tensor_copy(dst_ap, src_ap).then_inc(vs.h, 1); vs.n += 1
                    j += 1

                for i, (kind, idx, nl, nt) in enumerate(units):
                    if kind == "gu":
                        if idx >= 2 and (idx - 2) in st_plan:
                            for sl, tot in st_plan[idx - 2]:
                                v.wait_ge(p.st[sl].h, tot)
                        for gu in range(2):
                            for dt_i in range(ND):
                                ev(strip[idx % 2][:, gu * D + dt_i * 128: gu * D + (dt_i + 1) * 128],
                                   ptr[j % 2][:, :], i)
                    elif kind == "wd":
                        half, dt2 = divmod(idx, ND)
                        if dt2 == 0 and half == 1:
                            for ui, sts in st_plan.items():
                                if units[ui][0] == "wd" and units[ui][1] == ND - 1:
                                    for slx, tot in sts:
                                        v.wait_ge(p.st[slx].h, tot)
                        for ht in range(NHT // 2):
                            ev(dstrips[:, ht * D + dt2 * 128: ht * D + (dt2 + 1) * 128], ptr[j % 2][:, :], i)
                    elif kind == "ip":
                        lidx = idx % NIP
                        if lidx == 0 and idx >= NIP:
                            for ui, sts in st_plan.items():
                                if units[ui][0] == "ip" and units[ui][1] == NIP - 1:
                                    for sl, tot in sts:
                                        v.wait_ge(p.st[sl].h, tot)
                        for dt_i in range(ND):
                            ev(istrips[:, dt_i * DI + lidx * 128: dt_i * DI + (lidx + 1) * 128],
                               ptr[j % 2][:, :], i)
                    elif kind == "op":
                        for kt in range(NDI):
                            ev(ostrips[:, kt * D + idx * 128: kt * D + (idx + 1) * 128], ptr[j % 2][:, :], i)
                    elif kind == "xp":
                        for kt in range(NDI):
                            ev(xstrip[:, kt * DT:(kt + 1) * DT], ptr[j % 2][0:128, 0:64], i)
                    elif kind == "dp":
                        ev(dtstrip[0:64, idx * 128:(idx + 1) * 128], ptr[j % 2][0:64, 0:128], i)

            @block.sync
            def _(sync):
                for i in range(len(units)):
                    if i in st_plan:
                        sync.wait_ge(vs.h, vs_base + unit_tcum[i])
                        for (sl, tot), (dram_ap, sb_ap) in zip(st_plan[i], stores_after(i)):
                            sync.dma_start(dram_ap, sb_ap).then_inc(p.st[sl].h, 16)
                            p.st[sl].n += 16
                sync.wait_ge(gcast.h, gcast.n)
                for s in p.st:
                    sync.wait_ge(s.h, s.n)
                for s in (p.ld[0], p.ld[1]):
                    sync.wait_ge(s.h, s.n)


def drain_dmas(p):
    nc = p.nc
    with nc.Block() as block:
        @block.sync
        def _(sync):
            for s in p.ld + p.st + p.gld + p.gga + p.gsc + [p.gidx]:
                sync.wait_ge(s.h, s.n)


def sop(eng, sem, emit_fn, waits=()):
    for (s, n) in waits:
        eng.wait_ge(s.h, n)
    eng.wait_ge(sem.h, sem.n)
    instr = emit_fn()
    sem.inc(instr)
    return sem.n


def emit_ln(p, block, R, c, guard=False):
    nc = p.nc
    ts, vs, ss = p.ts, p.vs, p.ss
    ntt = R["ntt"]
    ND = D // 128
    Tc = ntt * 128
    s = c % 2
    xch, xlnT, xtm, junk, stat, ptT = (R[k] for k in ("xch", "xlnT", "xtm", "junk", "stat", "ptT"))
    SUM, SQ, MC, VAR, RSTD = (stat[:, i * ntt:(i + 1) * ntt] for i in range(5))
    xsem, xtot = R["xld"](c)
    ts0, vs0, ss0 = ts.n, vs.n, ss.n
    TS_TR0 = ts0
    SS_TR0 = ss0 + ntt + 2
    VS_N0 = vs0 + ntt + 4

    @block.scalar
    def _(sc):
        for tt in range(ntt):
            w = [(xsem, xtot)]
            if guard and tt == 0:
                w.append((vs, vs0))
            sop(sc, ss, lambda tt=tt: sc.activation(
                junk[tt % 2][:, :], xch[s][:, tt * D:(tt + 1) * D], AF.Square,
                accum_out=SQ[:, tt:tt + 1]), waits=w)
        sop(sc, ss, lambda: sc.activation(RSTD[:, :], VAR[:, :], AF.Ln),
            waits=[(vs, vs0 + ntt + 4)])
        sop(sc, ss, lambda: sc.activation(RSTD[:, :], RSTD[:, :], AF.Exp, scale=-0.5))
        for j in range(8 * ntt):
            tt, k = divmod(j, ND)
            sop(sc, ss, lambda j=j, tt=tt, k=k: sc.copy(
                xlnT[s][:, k * Tc + tt * 128: k * Tc + (tt + 1) * 128], ptT[j % 2][:, :]),
                waits=[(ts, ts0 + j + 1)])

    @block.vector
    def _(v):
        for tt in range(ntt):
            w = [(xsem, xtot)]
            if guard and tt == 0:
                w.append((ss, ss0))
            sop(v, vs, lambda tt=tt: v.tensor_reduce(
                SUM[:, tt:tt + 1], xch[s][:, tt * D:(tt + 1) * D], axis=AX.X, op=ALU.add), waits=w)
        sop(v, vs, lambda: v.tensor_scalar(MC[:, :], SUM[:, :], 1.0 / D, None, op0=ALU.mult))
        sop(v, vs, lambda: v.tensor_tensor(VAR[:, :], MC[:, :], MC[:, :], op=ALU.mult))
        sop(v, vs, lambda: v.scalar_tensor_tensor(
            VAR[:, :], SQ[:, :], 1.0 / D, VAR[:, :], op0=ALU.mult, op1=ALU.subtract),
            waits=[(ss, ss0 + ntt)])
        sop(v, vs, lambda: v.tensor_scalar(VAR[:, :], VAR[:, :], EPS, None, op0=ALU.add))
        for tt in range(ntt):
            w = [(ss, ss0 + ntt + 2)]
            if tt >= 2:
                w.append((ts, ts0 + 8 * (tt - 1)))
            elif guard:
                w.append((ts, ts0))
            sop(v, vs, lambda tt=tt: v.tensor_scalar(
                xtm[tt % 2][:, :], xch[s][:, tt * D:(tt + 1) * D],
                MC[:, tt:tt + 1], RSTD[:, tt:tt + 1], op0=ALU.subtract, op1=ALU.mult),
                waits=w)

    @block.tensor
    def _(t):
        for j in range(8 * ntt):
            tt, k = divmod(j, ND)
            t.wait_ge(vs.h, VS_N0 + tt + 1)
            if j >= 2:
                t.wait_ge(ss.h, SS_TR0 + (j - 2) + 1)
            t.transpose(ptT[j % 2][:, :], xtm[tt % 2][:, k * 128:(k + 1) * 128],
                        p.ident[:, :]).then_inc(ts.h, 1)
            ts.n += 1

    assert ts.n == ts0 + 8 * ntt
    assert ss.n == SS_TR0 + 8 * ntt
    assert vs.n == VS_N0 + ntt
    return ts0, vs0, ss0


def build_ffn0(p, isrc, idst, tag=0):
    nc = p.nc
    NTOK, NTT = p.NTOK, p.NTT
    Tc = min(512, NTOK)
    ntt = Tc // 128
    NCH = NTOK // Tc
    NHT = H3 // 128
    ND = D // 128
    rs, rd = p.resid[isrc], p.resid[idst]
    ts, vs, ss = p.ts, p.vs, p.ss
    ld, st = p.ld, p.st

    with ExitStack() as ctx:
        en = ctx.enter_context
        xch = [en(nc.sbuf_tensor(f"f0{tag}_xch{i}", [128, ntt * D], F32)) for i in range(2)]
        wdT = en(nc.sbuf_tensor(f"f0{tag}_wdT", [128, NHT * D], BF16))
        slab = [en(nc.sbuf_tensor(f"f0{tag}_slab{i}", [128, 2 * D], BF16)) for i in range(4)]
        xlnT = [en(nc.sbuf_tensor(f"f0{tag}_xlnT{i}", [128, ND * Tc], BF16)) for i in range(2)]
        h_sb = en(nc.sbuf_tensor(f"f0{tag}_h", [128, NHT * Tc], BF16))
        xtm = [en(nc.sbuf_tensor(f"f0{tag}_xtm{i}", [128, D], BF16)) for i in range(2)]
        junk = [en(nc.sbuf_tensor(f"f0{tag}_junk", [128, D], F32))] * 2
        sg = [en(nc.sbuf_tensor(f"f0{tag}_sg{i}", [128, Tc], BF16)) for i in range(2)]
        su = [en(nc.sbuf_tensor(f"f0{tag}_su{i}", [128, Tc], BF16)) for i in range(2)]
        stat = en(nc.sbuf_tensor(f"f0{tag}_stat", [128, 5 * ntt], F32))
        ptT = [en(nc.psum_tensor(f"f0{tag}_ptT{i}", [128, 128], BF16)) for i in range(2)]
        ptA = [en(nc.psum_tensor(f"f0{tag}_ptA{i}", [128, 512], F32)) for i in range(2)]
        ptB = [en(nc.psum_tensor(f"f0{tag}_ptB{i}", [128, D], F32)) for i in range(2)]

        p.f0_xch_ld = getattr(p, "f0_xch_ld", {})
        _lc = [ld[0].n, ld[1].n]
        for _c in range(NCH):
            _lc[_c % 2] += 16
            p.f0_xch_ld[(tag, _c)] = _lc[_c % 2]
        p.f0_ts_after_A = {}
        p.f0_slab_tot = {}
        _cnt = [ld[2 + i].n for i in range(4)]
        for _c in range(NCH):
            for _ht in range(NHT):
                _sl = _ht % 4
                _cnt[_sl] += 16
                p.f0_slab_tot[(_c, _ht)] = _cnt[_sl]

        R = {"xch": xch, "xlnT": xlnT, "xtm": xtm, "junk": junk, "stat": stat, "ptT": ptT,
             "ntt": ntt, "xld": lambda c: (ld[c % 2], p.f0_xch_ld[(tag, c)])}

        def slab_load(sync, c, ht, in_block):
            sl = ht % 4
            if in_block:
                sync.wait_ge(ts.h, p.f0_ts_after_A[(c, ht - 4)])
            elif c > 0:
                sync.wait_ge(ts.h, p.f0_ts_after_A[(c - 1, NHT - 4 + sl)])
            sync.dma_start(slab[sl][:, :], p.l0_gu[ht, :, :]).then_inc(ld[2 + sl].h, 16)
            ld[2 + sl].n += 16
            assert p.f0_slab_tot[(c, ht)] == ld[2 + sl].n

        def emit_loads(sync, c):
            s = c % 2
            sync.wait_ge(st[s].h, st[s].n)
            sync.dma_start(xch[s][:, :].rearrange("q (tt d) -> q tt d", tt=ntt),
                           rs[c * Tc:(c + 1) * Tc, :].rearrange("(tt q) d -> q tt d", q=128)
                           ).then_inc(ld[s].h, 16)
            ld[s].n += 16
            assert p.f0_xch_ld[(tag, c)] == ld[s].n
            for ht in range(4):
                slab_load(sync, c, ht, in_block=False)

        with nc.Block() as block:
            @block.sync
            def _(sync):
                sync.dma_start(wdT[:, :].rearrange("q (ht d) -> q ht d", ht=NHT),
                               p.l0_dT.rearrange("(ht q) d -> q ht d", q=128)).then_inc(ld[6].h, 16)
                ld[6].n += 16
                emit_loads(sync, 0)

        for c in range(NCH):
            s = c % 2
            with nc.Block() as block:
                ln0_ss_end = None
                if c == 0:
                    emit_ln(p, block, R, 0)
                    ln0_ss_end = ss.n

                ts0, vs0, ss0 = ts.n, vs.n, ss.n
                TS_A0 = ts0
                TS_B0 = TS_A0 + 2 * NHT
                SS_A0 = ss0
                VS_A0 = vs0
                VS_B0 = VS_A0 + 2 * NHT
                for ht in range(NHT):
                    p.f0_ts_after_A[(c, ht)] = TS_A0 + 2 * ht + 2

                @block.scalar
                def _(sc):
                    for ht in range(NHT):
                        w = [(ts, TS_A0 + 2 * ht + 1)]
                        if ht >= 2:
                            w.append((vs, VS_A0 + 2 * (ht - 2) + 2))
                        sop(sc, ss, lambda ht=ht: sc.activation(
                            sg[ht % 2][:, 0:Tc], ptA[0][:, 0:Tc], AF.Silu, scale=QSCALE), waits=w)

                @block.vector
                def _(v):
                    for ht in range(NHT):
                        sop(v, vs, lambda ht=ht: v.tensor_copy(su[ht % 2][:, 0:Tc], ptA[1][:, 0:Tc]),
                            waits=[(ts, TS_A0 + 2 * ht + 2)])
                        sop(v, vs, lambda ht=ht: v.tensor_tensor(
                            h_sb[:, ht * Tc:(ht + 1) * Tc], sg[ht % 2][:, 0:Tc], su[ht % 2][:, 0:Tc],
                            op=ALU.mult),
                            waits=[(ss, SS_A0 + ht + 1)])
                    for tb in range(ntt):
                        sop(v, vs, lambda tb=tb: v.scalar_tensor_tensor(
                            xch[s][:, tb * D:(tb + 1) * D], ptB[tb % 2][:, :],
                            0.5 * QSCALE * QSCALE, xch[s][:, tb * D:(tb + 1) * D],
                            op0=ALU.mult, op1=ALU.add),
                            waits=[(ts, TS_B0 + tb + 1)])

                @block.tensor
                def _(t):
                    if ln0_ss_end is not None:
                        t.wait_ge(ss.h, ln0_ss_end)
                    for ht in range(NHT):
                        sl = ht % 4
                        t.wait_ge(ld[2 + sl].h, p.f0_slab_tot[(c, ht)])
                        for gu in range(2):
                            if ht >= 1:
                                if gu == 0:
                                    t.wait_ge(ss.h, SS_A0 + (ht - 1) + 1)
                                else:
                                    t.wait_ge(vs.h, VS_A0 + 2 * (ht - 1) + 1)
                            for k in range(ND):
                                mm = t.matmul(ptA[gu][:, 0:Tc],
                                              slab[sl][:, gu * D + k * 128: gu * D + (k + 1) * 128],
                                              xlnT[s][:, k * Tc:(k + 1) * Tc],
                                              start=(k == 0), stop=(k == ND - 1))
                            mm.then_inc(ts.h, 1); ts.n += 1
                    for tb in range(ntt):
                        t.wait_ge(vs.h, VS_A0 + 2 * NHT)
                        if tb >= 2:
                            t.wait_ge(vs.h, VS_B0 + (tb - 2) + 1)
                        t.wait_ge(ld[6].h, ld[6].n)
                        for ht in range(NHT):
                            for dh in range(2):
                                mm = t.matmul(ptB[tb % 2][:, dh * 512:(dh + 1) * 512],
                                              h_sb[:, ht * Tc + tb * 128: ht * Tc + (tb + 1) * 128],
                                              wdT[:, ht * D + dh * 512: ht * D + dh * 512 + 512],
                                              start=(ht == 0), stop=(ht == NHT - 1))
                        mm.then_inc(ts.h, 1); ts.n += 1

                assert ts.n == TS_B0 + ntt
                assert ss.n == SS_A0 + NHT
                assert vs.n == VS_B0 + ntt

                if c + 1 < NCH:
                    emit_ln(p, block, R, c + 1, guard=(c == 0))

                @block.sync
                def _(sync):
                    for ht in range(4, NHT):
                        slab_load(sync, c, ht, in_block=True)
                    sync.wait_ge(vs.h, VS_B0 + ntt)
                    sync.dma_start(rd[c * Tc:(c + 1) * Tc, :].rearrange("(tt q) d -> q tt d", q=128),
                                   xch[s][:, :].rearrange("q (tt d) -> q tt d", tt=ntt)
                                   ).then_inc(st[s].h, 16)
                    st[s].n += 16
                    if c + 1 < NCH:
                        emit_loads(sync, c + 1)
        drain_dmas(p)



def build_mamba(p, isrc, idst):
    nc = p.nc
    B, T, NTOK = p.B, p.T, p.NTOK
    Tc = min(512, NTOK)
    ntt = Tc // 128
    NCH = NTOK // Tc
    ND = D // 128
    NDI = DI // 128
    NF = 2 * DI // 128
    NTB = T // 128
    rs, rd = p.resid[isrc], p.resid[idst]
    ts, vs, ss = p.ts, p.vs, p.ss
    ld, st = p.ld, p.st

    CW = lambda k, kt: p.bcols[:, 64 + k * 16 + kt: 64 + k * 16 + kt + 1]
    CB = lambda kt: p.bcols[:, kt:kt + 1]
    DTB = lambda kt: p.bcols[:, 16 + kt:16 + kt + 1]
    ACOL = lambda kt: p.bcols[:, 32 + kt:32 + kt + 1]
    DCOL = lambda kt: p.bcols[:, 48 + kt:48 + kt + 1]

    with ExitStack() as stage:

        with ExitStack() as ctx:
            en = ctx.enter_context
            iproj = en(nc.sbuf_tensor("mb_iproj", [128, ND * 2 * DI], BF16))
            xch = [en(nc.sbuf_tensor(f"mb_xch{i}", [128, ntt * D], F32)) for i in range(2)]
            xlnT = [en(nc.sbuf_tensor(f"mb_xlnT{i}", [128, ND * Tc], BF16)) for i in range(2)]
            xtm = [en(nc.sbuf_tensor(f"mb_xtm{i}", [128, D], BF16)) for i in range(2)]
            junk = [en(nc.sbuf_tensor("mb_junk", [128, D], F32))] * 2
            stat = en(nc.sbuf_tensor("mb_stat", [128, 5 * ntt], F32))
            xz_sb = en(nc.sbuf_tensor("mb_xz", [128, (NF // 2) * Tc], F32))
            ptT = [en(nc.psum_tensor(f"mb_ptT{i}", [128, 128], BF16)) for i in range(2)]
            ptM = [en(nc.psum_tensor(f"mb_ptM{i}", [128, 512], F32)) for i in range(2)]

            p.mb_xz_st = {}
            _s2 = p.st[2].n
            for _c in range(NCH):
                for _hf in range(2):
                    _s2 += 16
                    p.mb_xz_st[(_c, _hf)] = _s2
            p.mb_xch_ld = {}
            _lc = [ld[0].n, ld[1].n]
            for _c in range(NCH):
                _lc[_c % 2] += 16
                p.mb_xch_ld[_c] = _lc[_c % 2]

            R = {"xch": xch, "xlnT": xlnT, "xtm": xtm, "junk": junk, "stat": stat, "ptT": ptT,
                 "ntt": ntt, "xld": lambda c: (ld[c % 2], p.mb_xch_ld[c])}

            def m1_load(sync, c):
                s2 = c % 2
                sync.wait_ge(st[s2].h, st[s2].n)
                sync.dma_start(xch[s2][:, :].rearrange("q (tt d) -> q tt d", tt=ntt),
                               rs[c * Tc:(c + 1) * Tc, :].rearrange("(tt q) d -> q tt d", q=128)
                               ).then_inc(ld[s2].h, 16)
                ld[s2].n += 16
                assert p.mb_xch_ld[c] == ld[s2].n

            with nc.Block() as block:
                @block.sync
                def _(sync):
                    sync.dma_start(iproj[:, :].rearrange("q (dt f) -> q dt f", dt=ND),
                                   p.inprojT.rearrange("(dt q) f -> q dt f", q=128)).then_inc(ld[6].h, 16)
                    ld[6].n += 16
                    m1_load(sync, 0)

            for c in range(NCH):
                s = c % 2
                b = (c * Tc) // T
                t0 = c * Tc - b * T
                with nc.Block() as block:
                    ln0_ss_end = None
                    if c == 0:
                        emit_ln(p, block, R, 0)
                        ln0_ss_end = ss.n

                    ts0, vs0, ss0 = ts.n, vs.n, ss.n

                    @block.vector
                    def _(v):
                        for ft in range(NF):
                            w = [(ts, ts0 + ft + 1)]
                            if ft == 0 and c > 0:
                                w.append((p.st[2], p.mb_xz_st[(c - 1, 1)]))
                            if ft == NF // 2:
                                w.append((p.st[2], p.mb_xz_st[(c, 0)]))
                            sop(v, vs, lambda ft=ft: v.tensor_copy(
                                xz_sb[:, (ft % (NF // 2)) * Tc:((ft % (NF // 2)) + 1) * Tc],
                                ptM[ft % 2][:, 0:Tc]), waits=w)

                    @block.tensor
                    def _(t):
                        t.wait_ge(ld[6].h, ld[6].n)
                        if ln0_ss_end is not None:
                            t.wait_ge(ss.h, ln0_ss_end)
                        for ft in range(NF):
                            if ft >= 2:
                                t.wait_ge(vs.h, vs0 + (ft - 2) + 1)
                            for k in range(ND):
                                mm = t.matmul(ptM[ft % 2][:, 0:Tc],
                                              iproj[:, k * 2 * DI + ft * 128: k * 2 * DI + (ft + 1) * 128],
                                              xlnT[s][:, k * Tc:(k + 1) * Tc],
                                              start=(k == 0), stop=(k == ND - 1))
                            mm.then_inc(ts.h, 1); ts.n += 1

                    assert ts.n == ts0 + NF and vs.n == vs0 + NF

                    if c + 1 < NCH:
                        emit_ln(p, block, R, c + 1, guard=(c == 0))

                    @block.sync
                    def _(sync):
                        for hf in range(2):
                            sync.wait_ge(vs.h, vs0 + (NF // 2) * (hf + 1))
                            sync.dma_start(
                                p.xz_scr[b, hf * DI:(hf + 1) * DI, t0:t0 + Tc]
                                .rearrange("(ft q) t -> q ft t", q=128),
                                xz_sb[:, :].rearrange("q (ft t) -> q ft t", ft=NF // 2)
                                ).then_inc(st[2].h, 16)
                            st[2].n += 16
                            assert p.mb_xz_st[(c, hf)] == st[2].n
                        if c + 1 < NCH:
                            m1_load(sync, c + 1)
            drain_dmas(p)

        with ExitStack() as ctx:
            en = ctx.enter_context
            xpad = [en(nc.sbuf_tensor(f"mb_xpad{i}", [128, T + 3], F32)) for i in range(2)]
            acc = [en(nc.sbuf_tensor(f"mb_acc{i}", [128, T], F32)) for i in range(2)]
            xaf = [en(nc.sbuf_tensor(f"mb_xaf{i}", [128, T], F32)) for i in range(2)]
            xab = [en(nc.sbuf_tensor(f"mb_xab{i}", [128, T], BF16)) for i in range(2)]
            units = [(b, kt) for b in range(B) for kt in range(NDI)]
            vs0, ss0 = vs.n, ss.n
            ld0 = [ld[0].n, ld[1].n]
            st0 = [p.st[0].n, p.st[1].n]

            with nc.Block() as block:
                @block.gpsimd
                def _(g):
                    g.memset(xpad[0][:, 0:3], 0.0).then_inc(p.gs.h, 1); p.gs.n += 1
                    g.memset(xpad[1][:, 0:3], 0.0).then_inc(p.gs.h, 1); p.gs.n += 1

                @block.sync
                def _(sync):
                    def store(u2):
                        b2, kt2 = units[u2]
                        sl2 = u2 % 2
                        sync.wait_ge(vs.h, vs0 + 5 * u2 + 5)
                        sync.dma_start(p.xiact_f[b2, kt2 * 128:(kt2 + 1) * 128, :],
                                       xaf[sl2][:, :]).then_inc(p.st[sl2].h, 16)
                        sync.dma_start(p.xiact_b[b2, kt2 * 128:(kt2 + 1) * 128, :],
                                       xab[sl2][:, :]).then_inc(p.st[sl2].h, 16)
                        p.st[sl2].n += 32
                    for u, (b, kt) in enumerate(units):
                        sl = u % 2
                        if u >= 2:
                            sync.wait_ge(vs.h, vs0 + 5 * (u - 2) + 4)
                        sync.dma_start(xpad[sl][:, 3:3 + T],
                                       p.xz_scr[b, kt * 128:(kt + 1) * 128, :]).then_inc(ld[sl].h, 16)
                        ld[sl].n += 16
                        if u >= 1:
                            store(u - 1)
                    store(len(units) - 1)

                @block.vector
                def _(v):
                    for u, (b, kt) in enumerate(units):
                        sl = u % 2
                        w = [(ld[sl], ld0[sl] + 16 * (u // 2 + 1)), (p.gs, p.gs.n)]
                        sop(v, vs, lambda sl=sl, kt=kt: v.tensor_scalar(
                            acc[sl][:, :], xpad[sl][:, 0:T], CW(0, kt), None, op0=ALU.mult), waits=w)
                        for k in range(1, KCONV):
                            sop(v, vs, lambda sl=sl, kt=kt, k=k: v.scalar_tensor_tensor(
                                acc[sl][:, :], xpad[sl][:, k:k + T], CW(k, kt), acc[sl][:, :],
                                op0=ALU.mult, op1=ALU.add))
                        sop(v, vs, lambda sl=sl: v.tensor_copy(xab[sl][:, :], xaf[sl][:, :]),
                            waits=[(ss, ss0 + u + 1)])

                @block.scalar
                def _(sc):
                    for u, (b, kt) in enumerate(units):
                        sl = u % 2
                        w = [(vs, vs0 + 5 * u + 4)]
                        if u >= 2:
                            w.append((p.st[sl], st0[sl] + 32 * (u // 2)))
                        sop(sc, ss, lambda sl=sl, kt=kt: sc.activation(
                            xaf[sl][:, :], acc[sl][:, :], AF.Silu, bias=CB(kt)), waits=w)

            drain_dmas(p)

        with ExitStack() as ctx:
            en = ctx.enter_context
            y_sb = en(nc.sbuf_tensor("mb_y", [128, NDI * T], BF16))
            xpr = en(nc.sbuf_tensor("mb_xpr", [128, NDI * DT], BF16))
            dpr = en(nc.sbuf_tensor("mb_dpr", [64, DI], BF16))
            oproj = en(nc.sbuf_tensor("mb_oproj", [128, NDI * D], BF16))
            strip = [en(nc.sbuf_tensor(f"mb_strip{i}", [128, T], BF16)) for i in range(2)]
            dtlow = en(nc.sbuf_tensor("mb_dtlow", [64, T], BF16))
            vband = en(nc.sbuf_tensor("mb_vband", [128, T], F32))
            t1 = en(nc.sbuf_tensor("mb_t1", [128, T], F32))
            t2 = en(nc.sbuf_tensor("mb_t2", [128, T], F32))
            uf = [en(nc.sbuf_tensor("mb_uf", [128, T], F32))] * 2
            zf = [en(nc.sbuf_tensor("mb_zf", [128, T], F32))] * 2
            xt = [en(nc.sbuf_tensor(f"mb_xt{i}", [128, D], F32)) for i in range(2)]
            W5 = min(512, T)
            NCF = T // W5

            with nc.Block() as block:
                @block.sync
                def _(sync):
                    sync.dma_start(xpr[:, :].rearrange("q (kt r) -> q kt r", kt=NDI),
                                   p.xprojT.rearrange("(kt q) r -> q kt r", q=128)).then_inc(ld[6].h, 16)
                    ld[6].n += 16
                    sync.dma_start(dpr[0:64, :], p.dtprojT[:, :]).then_inc(ld[6].h, 16)
                    ld[6].n += 16
                    sync.dma_start(oproj[:, :].rearrange("q (kt d) -> q kt d", kt=NDI),
                                   p.outprojT.rearrange("(kt q) d -> q kt d", q=128)).then_inc(ld[6].h, 16)
                    ld[6].n += 16

            pstk = [ExitStack()]

            def psums(phase):
                pstk[0].close()
                pstk[0] = ExitStack()
                if phase == "dt":
                    return pstk[0].enter_context(nc.psum_tensor(f"mb_pDL{b}", [64, T], F32))
                if phase == "delta":
                    return [pstk[0].enter_context(nc.psum_tensor(f"mb_pD{b}{i}", [128, 512], F32))
                            for i in range(2)]
                return [pstk[0].enter_context(nc.psum_tensor(f"mb_pO{b}{i}", [128, D], F32))
                        for i in range(2)]

            for b in range(B):
                pDL = psums("dt")
                ts0 = ts.n
                with nc.Block() as block:
                    @block.sync
                    def _(sync):
                        for kt in range(NDI):
                            sl = kt % 2
                            if kt >= 2:
                                sync.wait_ge(ts.h, ts0 + (kt - 2) + 1)
                            sync.dma_start(strip[sl][:, :],
                                           p.xiact_b[b, kt * 128:(kt + 1) * 128, :]).then_inc(ld[sl].h, 16)
                            ld[sl].n += 16
                            p.mb_strip_ld[(b, kt)] = ld[sl].n

                    @block.tensor
                    def _(t):
                        t.wait_ge(ld[6].h, ld[6].n)
                        for kt in range(NDI):
                            sl = kt % 2
                            t.wait_ge(ld[sl].h, p.mb_strip_ld[(b, kt)])
                            for ncf in range(NCF):
                                mm = t.matmul(pDL[0:64, ncf * W5:(ncf + 1) * W5],
                                              xpr[:, kt * DT:(kt + 1) * DT],
                                              strip[sl][:, ncf * W5:(ncf + 1) * W5],
                                              start=(kt == 0), stop=(kt == NDI - 1))
                            mm.then_inc(ts.h, 1); ts.n += 1

                    @block.vector
                    def _(v):
                        sop(v, vs, lambda: v.tensor_copy(dtlow[0:64, :], pDL[0:64, :]),
                            waits=[(ts, ts0 + NDI)])

                pD = psums("delta")
                for kt in range(NDI):
                    sl = kt % 2
                    ts0, vs0, ss0 = ts.n, vs.n, ss.n
                    uld, zld = ld[sl].n + 16, ld[7].n + 16
                    with nc.Block() as block:
                        @block.sync
                        def _(sync):
                            sync.dma_start(uf[sl][:, :],
                                           p.xiact_f[b, kt * 128:(kt + 1) * 128, :]).then_inc(ld[sl].h, 16)
                            ld[sl].n += 16
                            sync.dma_start(zf[sl][:, :],
                                           p.xz_scr[b, DI + kt * 128:DI + (kt + 1) * 128, :]
                                           ).then_inc(ld[7].h, 16)
                            ld[7].n += 16

                        @block.tensor
                        def _(t):
                            for ncf in range(NCF):
                                if ncf >= 2:
                                    t.wait_ge(vs.h, vs0 + (ncf - 2) + 1)
                                mm = t.matmul(pD[ncf % 2][:, 0:W5],
                                              dpr[0:64, kt * 128:(kt + 1) * 128],
                                              dtlow[0:64, ncf * W5:(ncf + 1) * W5],
                                              start=True, stop=True)
                                mm.then_inc(ts.h, 1); ts.n += 1

                        @block.vector
                        def _(v):
                            for ncf in range(NCF):
                                sop(v, vs, lambda ncf=ncf: v.tensor_scalar(
                                    vband[:, ncf * W5:(ncf + 1) * W5], pD[ncf % 2][:, 0:W5],
                                    DTB(kt), None, op0=ALU.add),
                                    waits=[(ts, ts0 + ncf + 1)])
                            sop(v, vs, lambda: v.tensor_scalar(vband[:, :], vband[:, :], 0.0, None,
                                                               op0=ALU.max),
                                waits=[(ss, ss0 + 1)])
                            sop(v, vs, lambda: v.tensor_tensor(vband[:, :], vband[:, :], t1[:, :],
                                                               op=ALU.add),
                                waits=[(ss, ss0 + 3)])
                            sop(v, vs, lambda: v.tensor_scalar(vband[:, :], vband[:, :], 10.0, 1e-6,
                                                               op0=ALU.min, op1=ALU.max))
                            sop(v, vs, lambda: v.tensor_scalar(t1[:, :], t1[:, :], 1.0, 1e-6,
                                                               op0=ALU.min, op1=ALU.max),
                                waits=[(ss, ss0 + 4)])
                            sop(v, vs, lambda: v.tensor_tensor_scan(
                                t2[:, :], t1[:, :], uf[sl][:, :], 0.0, op0=ALU.mult, op1=ALU.add),
                                waits=[(ld[sl], uld)])
                            sop(v, vs, lambda: v.scalar_tensor_tensor(
                                t2[:, :], uf[sl][:, :], DCOL(kt), t2[:, :], op0=ALU.mult, op1=ALU.add))
                            sop(v, vs, lambda: v.tensor_scalar(t2[:, :], t2[:, :], 1e4, -1e4,
                                                               op0=ALU.min, op1=ALU.max))
                            sop(v, vs, lambda: v.tensor_tensor(
                                y_sb[:, kt * T:(kt + 1) * T], t2[:, :], t1[:, :], op=ALU.mult),
                                waits=[(ss, ss0 + 5)])

                        @block.scalar
                        def _(sc):
                            sop(sc, ss, lambda: sc.activation(t1[:, :], vband[:, :], AF.Abs),
                                waits=[(vs, vs0 + NCF)])
                            sop(sc, ss, lambda: sc.activation(t1[:, :], t1[:, :], AF.Exp, scale=-1.0))
                            sop(sc, ss, lambda: sc.activation(t1[:, :], t1[:, :], AF.Ln, bias=1.0))
                            sop(sc, ss, lambda: sc.activation(t1[:, :], vband[:, :], AF.Exp,
                                                              scale=ACOL(kt)),
                                waits=[(vs, vs0 + NCF + 3)])
                            sop(sc, ss, lambda: sc.activation(t1[:, :], zf[sl][:, :], AF.Silu),
                                waits=[(vs, vs0 + NCF + 5), (ld[7], zld)])

                pO = psums("out")
                ts0, vs0 = ts.n, vs.n
                ld0s = [ld[0].n, ld[1].n]
                st0s = [st[0].n, st[1].n]
                with nc.Block() as block:
                    @block.tensor
                    def _(t):
                        for tt in range(NTB):
                            sl = tt % 2
                            if tt >= 2:
                                t.wait_ge(vs.h, vs0 + (tt - 2) + 1)
                            for kt in range(NDI):
                                for dh in range(2):
                                    mm = t.matmul(pO[tt % 2][:, dh * 512:(dh + 1) * 512],
                                                  y_sb[:, kt * T + tt * 128: kt * T + (tt + 1) * 128],
                                                  oproj[:, kt * D + dh * 512: kt * D + dh * 512 + 512],
                                                  start=(kt == 0), stop=(kt == NDI - 1))
                            mm.then_inc(ts.h, 1); ts.n += 1

                    @block.vector
                    def _(v):
                        for tt in range(NTB):
                            sl = tt % 2
                            sop(v, vs, lambda tt=tt, sl=sl: v.scalar_tensor_tensor(
                                xt[sl][:, :], pO[tt % 2][:, :], 1.0, xt[sl][:, :],
                                op0=ALU.mult, op1=ALU.add),
                                waits=[(ts, ts0 + tt + 1),
                                       (ld[sl], ld0s[sl] + 16 * (tt // 2 + 1))])

                    @block.sync
                    def _(sync):
                        for tt in range(NTB):
                            sl = tt % 2
                            if tt >= 2:
                                sync.wait_ge(vs.h, vs0 + (tt - 2) + 1)
                                sync.wait_ge(st[sl].h, st0s[sl] + 16 * (tt // 2))
                            sync.dma_start(
                                xt[sl][:, :],
                                rs[b * T + tt * 128: b * T + (tt + 1) * 128, :]).then_inc(ld[sl].h, 16)
                            ld[sl].n += 16
                            if tt >= 1:
                                psl = (tt - 1) % 2
                                sync.wait_ge(vs.h, vs0 + tt)
                                sync.dma_start(rd[b * T + (tt - 1) * 128: b * T + tt * 128, :],
                                               xt[psl][:, :]).then_inc(st[psl].h, 16)
                                st[psl].n += 16
                        sync.wait_ge(vs.h, vs0 + NTB)
                        lsl = (NTB - 1) % 2
                        sync.dma_start(rd[b * T + (NTB - 1) * 128: b * T + NTB * 128, :],
                                       xt[lsl][:, :]).then_inc(st[lsl].h, 16)
                        st[lsl].n += 16
            pstk[0].close()
            drain_dmas(p)


def build_moe(p, isrc, idst, final=False, tag=0):
    nc = p.nc
    NTOK, NTT = p.NTOK, p.NTT
    ND = D // 128
    NMT = DM // 128
    RC = min(1024, NTOK)
    NRC = NTOK // RC
    CAP = RC // 2
    NTB = CAP // 128
    TPC = RC // 128
    JUNK = float(NE * CAP)
    rs = p.resid[isrc]
    rd = p.y_out if final else p.resid[idst]
    ts, vs, ss, gs = p.ts, p.vs, p.ss, p.gs
    ld, st = p.ld, p.st
    gld = p.gld

    with ExitStack() as stage:
        sen = stage.enter_context
        c_all = sen(nc.sbuf_tensor(f"mo{tag}_call", [128, NTT * NE], F32))
        idxs = sen(nc.sbuf_tensor(f"mo{tag}_idxs", [128, 2 * NE * NTB], I32))

        with ExitStack() as ctx:
            en = ctx.enter_context
            xt = [en(nc.sbuf_tensor(f"mo{tag}_xt{i}", [128, D], F32)) for i in range(2)]
            xlf = en(nc.sbuf_tensor(f"mo{tag}_xlf", [128, D], F32))
            xtm = [en(nc.sbuf_tensor(f"mo{tag}_xtm{i}", [128, 1032], BF16)) for i in range(2)]
            xlo = [en(nc.sbuf_tensor(f"mo{tag}_xlo{i}", [128, D], BF16)) for i in range(2)]
            xtbh = [en(nc.sbuf_tensor(f"mo{tag}_xtbh{i}", [128, D], BF16)) for i in range(2)]
            xtbl = [en(nc.sbuf_tensor(f"mo{tag}_xtbl{i}", [128, D], BF16)) for i in range(2)]
            junk = [en(nc.sbuf_tensor(f"mo{tag}_junk", [128, D], F32))] * 2
            stat = en(nc.sbuf_tensor(f"mo{tag}_stat", [128, 8], F32))
            tp2 = en(nc.sbuf_tensor(f"mo{tag}_tp2", [128, 16], F32))
            lsb = en(nc.sbuf_tensor(f"mo{tag}_lsb", [128, NE], F32))
            geb = en(nc.sbuf_tensor(f"mo{tag}_geb", [128, NE], F32))
            exb = en(nc.sbuf_tensor(f"mo{tag}_exb", [128, NE], F32))
            mskf = en(nc.sbuf_tensor(f"mo{tag}_mskf", [128, NE], F32))
            mskb = en(nc.sbuf_tensor(f"mo{tag}_mskb", [128, NE], BF16))
            preb = en(nc.sbuf_tensor(f"mo{tag}_preb", [128, NE], F32))
            rowb = en(nc.sbuf_tensor(f"mo{tag}_rowb", [1, NE], F32))
            offb = en(nc.sbuf_tensor(f"mo{tag}_offb", [1, NE], F32))
            posa = en(nc.sbuf_tensor(f"mo{tag}_posa", [128, NE], F32))
            posb = en(nc.sbuf_tensor(f"mo{tag}_posb", [128, NE], F32))
            pos12 = en(nc.sbuf_tensor(f"mo{tag}_pos12", [128, 4], F32))
            posi = en(nc.sbuf_tensor(f"mo{tag}_posi", [128, 4], I32))
            ptT = [en(nc.psum_tensor(f"mo{tag}_ptT{i}", [128, 128], BF16)) for i in range(2)]
            pl = [en(nc.psum_tensor(f"mo{tag}_pl{i}", [128, NE], F32)) for i in range(2)]
            psP = en(nc.psum_tensor(f"mo{tag}_psP", [128, NE], F32))
            psT = en(nc.psum_tensor(f"mo{tag}_psT", [1, NE], F32))
            psB = en(nc.psum_tensor(f"mo{tag}_psB", [128, NE], F32))
            SUM = stat[:, 0:1]; SQ = stat[:, 1:2]; MC = stat[:, 2:3]; VAR = stat[:, 3:4]; RSTD = stat[:, 4:5]
            M1 = tp2[:, 0:1]; M2 = tp2[:, 1:2]; DMM = tp2[:, 2:3]; EDM = tp2[:, 3:4]
            RDEN = tp2[:, 4:5]; NEGM1 = tp2[:, 5:6]
            PMIN = tp2[:, 6:7]; PEQ = tp2[:, 7:8]; PMIN2 = tp2[:, 8:9]

            NS = 21
            NT = 20
            NVu = [20 + 16 + (1 if (u % TPC == 0) else 0) for u in range(NTT)]
            ts_base = [0]; vs_base = [0]; ss_base = [0]
            for u in range(NTT):
                ts_base.append(ts_base[-1] + NT)
                vs_base.append(vs_base[-1] + NVu[u])
                ss_base.append(ss_base[-1] + NS)
            ts0, vs0, ss0 = ts.n, vs.n, ss.n
            ld0 = [ld[0].n, ld[1].n]
            st2_0 = st[2].n
            gf0 = gld[0].n
            gsc0 = gld[1].n
            gx0 = p.gidx.n

            def TSB(u): return ts0 + ts_base[u]
            def VSB(u): return vs0 + vs_base[u]
            def SSB(u): return ss0 + ss_base[u]

            with nc.Block() as block:
                @block.sync
                def _(sync):
                    for u in range(NTT):
                        sl = u % 2
                        if u >= 2:
                            sync.wait_ge(vs.h, VSB(u - 2) + 6)
                        sync.dma_start(xt[sl][:, :],
                                       rs[u * 128:(u + 1) * 128, :]).then_inc(ld[sl].h, 16)
                        ld[sl].n += 16
                        if u >= 1:
                            sync.wait_ge(vs.h, VSB(u - 1) + 21)
                            sync.dma_start(p.xtm_scr[(u - 1) * 128:u * 128, :],
                                           xtm[(u - 1) % 2][:, :]).then_inc(st[2].h, 16)
                            st[2].n += 16
                    sync.wait_ge(vs.h, VSB(NTT - 1) + 21)
                    sync.dma_start(p.xtm_scr[(NTT - 1) * 128:NTT * 128, :],
                                   xtm[(NTT - 1) % 2][:, :]).then_inc(st[2].h, 16)
                    st[2].n += 16

                @block.vector
                def _(v):
                    for u in range(NTT):
                        sl = u % 2
                        xld = ld0[sl] + 16 * (u // 2 + 1)
                        sop(v, vs, lambda sl=sl: v.tensor_reduce(
                            SUM, xt[sl][:, :], axis=AX.X, op=ALU.add), waits=[(ld[sl], xld)])
                        sop(v, vs, lambda: v.tensor_scalar(MC, SUM, 1.0 / D, None, op0=ALU.mult))
                        sop(v, vs, lambda: v.tensor_tensor(VAR, MC, MC, op=ALU.mult))
                        sop(v, vs, lambda u=u: v.scalar_tensor_tensor(
                            VAR, SQ, 1.0 / D, VAR, op0=ALU.mult, op1=ALU.subtract),
                            waits=[(ss, SSB(u) + 1)])
                        sop(v, vs, lambda: v.tensor_scalar(VAR, VAR, EPS, None, op0=ALU.add))
                        sop(v, vs, lambda sl=sl, u=u: v.tensor_scalar(
                            xlf[:, :], xt[sl][:, :], MC, RSTD, op0=ALU.subtract, op1=ALU.mult),
                            waits=[(ss, SSB(u) + 3)])
                        w7 = []
                        if u >= 2:
                            w7 = [(ts, TSB(u - 2) + 16), (st[2], st2_0 + 16 * (u - 1))]
                        sop(v, vs, lambda u=u, sl=sl: v.tensor_copy(xtm[sl][:, 0:D], xlf[:, :]),
                            waits=w7)
                        sop(v, vs, lambda sl=sl: v.tensor_tensor(
                            xlo[sl][:, :], xlf[:, :], xtm[sl][:, 0:D], op=ALU.subtract))
                        sop(v, vs, lambda u=u: v.tensor_copy(lsb[:, :], pl[u % 2][:, :]),
                            waits=[(ts, TSB(u) + 17)])
                        sop(v, vs, lambda: v.tensor_reduce(M1, lsb[:, :], axis=AX.X, op=ALU.max))
                        sop(v, vs, lambda: v.tensor_scalar(geb[:, :], lsb[:, :], M1, None,
                                                           op0=ALU.is_equal))
                        sop(v, vs, lambda: v.scalar_tensor_tensor(
                            geb[:, :], geb[:, :], -1e30, lsb[:, :], op0=ALU.mult, op1=ALU.add))
                        sop(v, vs, lambda: v.tensor_reduce(M2, geb[:, :], axis=AX.X, op=ALU.max))
                        sop(v, vs, lambda: v.tensor_tensor(DMM, M2, M1, op=ALU.subtract))
                        sop(v, vs, lambda u=u: v.tensor_scalar(EDM, EDM, 1.0, None, op0=ALU.add),
                            waits=[(ss, SSB(u) + 20)])
                        sop(v, vs, lambda: v.reciprocal(RDEN, EDM))
                        sop(v, vs, lambda: v.tensor_scalar(RDEN, RDEN, 0.5, None, op0=ALU.mult))
                        sop(v, vs, lambda: v.tensor_scalar(NEGM1, M1, -1.0, None, op0=ALU.mult))
                        sop(v, vs, lambda: v.tensor_scalar(geb[:, :], lsb[:, :], M2, None,
                                                           op0=ALU.is_ge))
                        sop(v, vs, lambda u=u: v.scalar_tensor_tensor(
                            c_all[:, u * NE:(u + 1) * NE], exb[:, :], RDEN, geb[:, :],
                            op0=ALU.mult, op1=ALU.mult),
                            waits=[(ss, SSB(u) + 21)])
                        sop(v, vs, lambda u=u, sl=sl: v.tensor_copy(
                            xtm[sl][:, D:D + NE], c_all[:, u * NE:(u + 1) * NE]))
                        sop(v, vs, lambda u=u: v.tensor_scalar(
                            mskf[:, :], c_all[:, u * NE:(u + 1) * NE], 0.0, None, op0=ALU.is_gt))
                        sop(v, vs, lambda: v.tensor_copy(mskb[:, :], mskf[:, :]))
                        sop(v, vs, lambda u=u: v.tensor_copy(preb[:, :], psP[:, :]),
                            waits=[(ts, TSB(u) + 18)])
                        if u % TPC == 0:
                            rcb = float((u // TPC) * 4224)
                            sop(v, vs, lambda rcb=rcb: v.tensor_scalar(
                                offb[0:1, :], p.ebase[0:1, :], 0.0, rcb,
                                op0=ALU.mult, op1=ALU.add))
                        sop(v, vs, lambda: v.tensor_tensor(rowb[0:1, :], offb[0:1, :],
                                                           p.ebase[0:1, :], op=ALU.add))
                        sop(v, vs, lambda u=u: v.tensor_tensor(posa[:, :], preb[:, :], psB[:, :],
                                                               op=ALU.add),
                            waits=[(ts, TSB(u) + 20)])
                        sop(v, vs, lambda: v.tensor_tensor(posa[:, :], posa[:, :], mskf[:, :],
                                                           op=ALU.mult))
                        jk = float((u // TPC) * 4224 + 4096)
                        sop(v, vs, lambda jk=jk: v.tensor_scalar(posb[:, :], mskf[:, :], -jk, jk,
                                                                 op0=ALU.mult, op1=ALU.add))
                        sop(v, vs, lambda: v.tensor_tensor(posa[:, :], posa[:, :], posb[:, :],
                                                           op=ALU.add))
                        sop(v, vs, lambda: v.tensor_reduce(PMIN, posa[:, :], axis=AX.X,
                                                           op=ALU.min))
                        sop(v, vs, lambda: v.tensor_scalar(posb[:, :], posa[:, :], PMIN, None,
                                                           op0=ALU.is_equal))
                        sop(v, vs, lambda: v.scalar_tensor_tensor(
                            posb[:, :], posb[:, :], 1.0e7, posa[:, :],
                            op0=ALU.mult, op1=ALU.add))
                        sop(v, vs, lambda: v.tensor_reduce(PMIN2, posb[:, :], axis=AX.X,
                                                           op=ALU.min))
                        wps = [(p.gidx, gx0 + 16 * ((u - 2) // TPC + 1) + 32 * (u - 1))] \
                            if u >= 2 else []
                        sl2 = (u % 2) * 2
                        sop(v, vs, lambda sl2=sl2: v.tensor_copy(posi[:, sl2:sl2 + 1], PMIN),
                            waits=wps)
                        sop(v, vs, lambda sl2=sl2: v.tensor_copy(posi[:, sl2 + 1:sl2 + 2],
                                                                 PMIN2))
                        sop(v, vs, lambda u=u: v.tensor_tensor(offb[0:1, :], offb[0:1, :],
                                                               psT[0:1, :], op=ALU.add),
                            waits=[(ts, TSB(u) + 19)])

                @block.scalar
                def _(sc):
                    for u in range(NTT):
                        sl = u % 2
                        xld = ld0[sl] + 16 * (u // 2 + 1)
                        sop(sc, ss, lambda sl=sl: sc.activation(
                            junk[sl][:, :], xt[sl][:, :], AF.Square, accum_out=SQ),
                            waits=[(ld[sl], xld), (vs, VSB(u - 1) + 6) if u >= 1 else (vs, 0)])
                        sop(sc, ss, lambda u=u: sc.activation(RSTD, VAR, AF.Ln),
                            waits=[(vs, VSB(u) + 5)])
                        sop(sc, ss, lambda: sc.activation(RSTD, RSTD, AF.Exp, scale=-0.5))
                        for j in range(16):
                            hi = j < 8
                            k = j % 8
                            dstb = xtbh if hi else xtbl
                            w = [(ts, TSB(u) + j + 1)]
                            if j == 0 and u >= 2:
                                w.append((ts, TSB(u - 2) + 17))
                            jj = NT * u + j
                            sop(sc, ss, lambda jj=jj, k=k, sl=sl, dstb=dstb: sc.copy(
                                dstb[sl][:, k * 128:(k + 1) * 128], ptT[jj % 2][:, :]), waits=w)
                        sop(sc, ss, lambda u=u: sc.activation(EDM, DMM, AF.Exp),
                            waits=[(vs, VSB(u) + 14)])
                        sop(sc, ss, lambda u=u: sc.activation(exb[:, :], lsb[:, :], AF.Exp,
                                                              bias=NEGM1),
                            waits=[(vs, VSB(u) + 18)])

                @block.tensor
                def _(t):
                    for u in range(NTT):
                        sl = u % 2
                        for j in range(16):
                            hi = j < 8
                            k = j % 8
                            srcb = xtm if hi else xlo
                            t.wait_ge(vs.h, VSB(u) + (7 if hi else 8))
                            jj = NT * u + j
                            if jj >= 2:
                                prev_u, prev_j = divmod(jj - 2, NT)
                                if prev_j < 16:
                                    t.wait_ge(ss.h, SSB(prev_u) + 3 + prev_j + 1)
                            t.transpose(ptT[jj % 2][:, :], srcb[sl][:, k * 128:(k + 1) * 128],
                                        p.ident[:, :]).then_inc(ts.h, 1)
                            ts.n += 1
                        t.wait_ge(ss.h, SSB(u) + 19)
                        if u >= 2:
                            t.wait_ge(vs.h, VSB(u - 2) + 9)
                        nmm = 0
                        for (xa, wa) in ((xtbh, p.wg_hi), (xtbh, p.wg_lo), (xtbl, p.wg_hi),
                                         (xtbl, p.wg_lo)):
                            for k in range(ND):
                                mm = t.matmul(pl[u % 2][:, :], xa[sl][:, k * 128:(k + 1) * 128],
                                              wa[:, k * NE:(k + 1) * NE],
                                              start=(nmm == 0), stop=(nmm == 4 * ND - 1))
                                nmm += 1
                        mm.then_inc(ts.h, 1); ts.n += 1
                        t.wait_ge(vs.h, VSB(u) + 23)
                        if u >= 1:
                            t.wait_ge(vs.h, VSB(u - 1) + 24)
                        mm = t.matmul(psP[:, :], p.tri[:, :], mskb[:, :], start=True, stop=True)
                        mm.then_inc(ts.h, 1); ts.n += 1
                        if u >= 1:
                            t.wait_ge(vs.h, VSB(u))
                        mm = t.matmul(psT[0:1, :], p.ones_bf[:, 0:1], mskb[:, :],
                                      start=True, stop=True)
                        mm.then_inc(ts.h, 1); ts.n += 1
                        t.wait_ge(vs.h, VSB(u) + 25 + (1 if u % TPC == 0 else 0))
                        if u >= 1:
                            t.wait_ge(vs.h, VSB(u - 1) + 26 + (1 if (u - 1) % TPC == 0 else 0))
                        mm = t.matmul(psB[:, :], p.ones_f32r[0:1, :], rowb[0:1, :],
                                      start=True, stop=True)
                        mm.then_inc(ts.h, 1); ts.n += 1

                @block.sync
                def _(sync):
                    sync.wait_ge(p.gidx.h, gx0 + 16 + 32 * TPC)
                    for col in range(NE * NTB):
                        sync.dma_start(idxs[:, col:col + 1],
                                       p.idx_scr[0][col * 128:(col + 1) * 128][:, None]
                                       ).then_inc(ld[5].h, 16)
                        ld[5].n += 16
                    p.mo_idx_tot[(tag, 0)] = ld[5].n

                @block.gpsimd
                def _(g):
                    gx = p.gidx
                    for u in range(NTT):
                        rc = u // TPC
                        if u % TPC == 0:
                            g.wait_ge(gx.h, gx.n)
                            g.dma_start(p.idx_scr[rc].rearrange("(q cc) -> q cc", cc=33),
                                        p.idx_fill.rearrange("(q cc) -> q cc", cc=33)
                                        ).then_inc(gx.h, 16)
                            gx.n += 16
                        g.wait_ge(vs.h, VSB(u) + 35 + (1 if u % TPC == 0 else 0))
                        for col in range(2):
                            g.wait_ge(gx.h, gx.n)
                            g.indirect_dma_start(
                                p.idx_scr.rearrange("a b -> (a b)")[:, None],
                                IndirectOffsetOnAxis(posi[:, (u % 2) * 2 + col:
                                                          (u % 2) * 2 + col + 1], 0),
                                p.tokid[:, u:u + 1], None).then_inc(gx.h, 16)
                            gx.n += 16

            assert ts.n == ts0 + ts_base[NTT]
            assert vs.n == vs0 + vs_base[NTT], (vs.n, vs0 + vs_base[NTT])
            assert ss.n == ss0 + ss_base[NTT]
            drain_dmas(p)

        with ExitStack() as ctx:
            en = ctx.enter_context
            xg = [en(nc.sbuf_tensor(f"mo{tag}_xg{i}", [128, NTB * 1032], BF16)) for i in range(2)]
            xgT = [en(nc.sbuf_tensor(f"mo{tag}_xgT", [128, ND * CAP], BF16))] * 2
            h_sb = en(nc.sbuf_tensor(f"mo{tag}_h", [128, NMT * CAP], BF16))
            y_sb = [en(nc.sbuf_tensor(f"mo{tag}_y", [128, NTB * D], F32))] * 2
            wq = [en(nc.sbuf_tensor(f"mo{tag}_wq{i}", [128, ND * D], BF16)) for i in range(2)]
            wo_sb = en(nc.sbuf_tensor(f"mo{tag}_wo", [128, NMT * D], BF16))
            ccol = en(nc.sbuf_tensor(f"mo{tag}_ccol", [128, 8], F32))
            ptT = [en(nc.psum_tensor(f"mo{tag}_qtT{i}", [128, 128], BF16)) for i in range(2)]
            pA = [en(nc.psum_tensor(f"mo{tag}_pA{i}", [128, 512], F32)) for i in range(2)]
            pB = [en(nc.psum_tensor(f"mo{tag}_pB{i}", [128, D], F32)) for i in range(2)]

            qtot = {}
            _q = [ld[2].n, ld[3].n]
            for _c in range(NRC):
                for _e in range(NE):
                    for _mq in range(4):
                        qg = _e * 4 + _mq
                        _q[qg % 2] += 16
                        qtot[(_c, _e, _mq)] = _q[qg % 2]
            idx0 = ld[5].n
            ini0 = st[3].n

            for c in range(NRC):
                ts0, vs0, ss0 = ts.n, vs.n, ss.n
                wotot = {}
                _w = ld[4].n
                for _e in range(NE):
                    _w += 16
                    wotot[_e] = _w
                NTR = NTB * 8
                EST = NTR + 32 + NTB
                SSE = NTR + 32

                def ts_tr(e, j):
                    return ts0 + EST * e + j + 1

                def ts_A(e, g):
                    return ts0 + EST * e + NTR + g + 1

                def ts_B(e, tb):
                    return ts0 + EST * e + NTR + 32 + tb + 1

                def ss_tr(e, j):
                    return ss0 + SSE * e + j + 1

                def ss_gelu(e, g):
                    return ss0 + SSE * e + NTR + g + 1

                def vs_y(e, tb):
                    return vs0 + 2 * NTB * e + 2 * tb + 2


                with nc.Block() as block:
                    @block.sync
                    def _(sync):
                        ts0c = ts.n
                        if c == 0:
                            for cc in range(NRC):
                                sync.dma_start(p.moe_out[cc * RC:(cc + 1) * RC, :],
                                               rs[cc * RC:(cc + 1) * RC, :]).then_inc(st[3].h, 16)
                                st[3].n += 16
                            sync.wait_ge(p.gc2.h, p.cast_moe_tot)
                        if c >= 1:
                            sync.wait_ge(ts.h, ts0c)
                        for mq in range(2):
                            sync.dma_start(
                                wq[mq % 2][:, :].rearrange("q (dt m) -> q dt m", dt=ND),
                                p.win_bf[0][:, mq * D:(mq + 1) * D].rearrange(
                                    "(dt q) m -> q dt m", q=128)).then_inc(ld[(mq % 2) + 2].h, 16)
                            ld[(mq % 2) + 2].n += 16
                        sync.dma_start(
                            wo_sb[:, :].rearrange("q (mt d) -> q mt d", mt=NMT),
                            p.wout_bf[0].rearrange("(mt q) d -> q mt d", q=128)
                            ).then_inc(ld[4].h, 16)
                        ld[4].n += 16
                        if c >= 1:
                            for _s in p.gsc:
                                sync.wait_ge(_s.h, _s.n)
                            sync.dma_start(rd[(c - 1) * RC:c * RC, :],
                                           p.moe_out[(c - 1) * RC:c * RC, :]).then_inc(st[3].h, 16)
                            st[3].n += 16
                        p.mo_ini_tot[(tag, c)] = st[3].n
                        for e in range(NE):
                            for mq in range(4):
                                if e == 0 and mq < 2:
                                    continue
                                qg = e * 4 + mq
                                if qg >= 2:
                                    pe, pq = divmod(qg - 2, 4)
                                    sync.wait_ge(ts.h, ts_A(pe, pq * 8 + 7))
                                sync.dma_start(
                                    wq[qg % 2][:, :].rearrange("q (dt m) -> q dt m", dt=ND),
                                    p.win_bf[e][:, mq * D:(mq + 1) * D].rearrange(
                                        "(dt q) m -> q dt m", q=128)).then_inc(ld[(qg % 2) + 2].h, 16)
                                ld[(qg % 2) + 2].n += 16
                            if e >= 1:
                                sync.wait_ge(ts.h, ts_B(e - 1, NTB - 1))
                                sync.dma_start(
                                    wo_sb[:, :].rearrange("q (mt d) -> q mt d", mt=NMT),
                                    p.wout_bf[e].rearrange("(mt q) d -> q mt d", q=128)
                                    ).then_inc(ld[4].h, 16)
                                ld[4].n += 16
                        if c + 1 < NRC:
                            if c == 0:
                                sync.wait_ge(p.gidx.h, p.gidx.n)
                            for _s in p.gga:
                                sync.wait_ge(_s.h, _s.n)
                            so = ((c + 1) % 2) * NE * NTB
                            for col in range(NE * NTB):
                                sync.dma_start(idxs[:, so + col:so + col + 1],
                                               p.idx_scr[c + 1][col * 128:(col + 1) * 128][:, None]
                                               ).then_inc(ld[5].h, 16)
                                ld[5].n += 16
                            p.mo_idx_tot[(tag, c + 1)] = ld[5].n

                    @block.gpsimd
                    def _(g):
                        ssem = p.gsc[0]
                        if c == 0:
                            for _s in p.gga + p.gsc:
                                g.wait_ge(_s.h, _s.n)
                        g.wait_ge(ld[5].h, p.mo_idx_tot[(tag, c)])
                        iso = (c % 2) * NE * NTB
                        for e in range(NE + 2):
                            if e >= 2:
                                se = e - 2
                                if e == 2:
                                    g.wait_ge(st[3].h, p.mo_ini_tot[(tag, c)])
                                E = c * NE + se
                                for tb in range(NTB):
                                    g.wait_ge(vs.h, vs_y(se, tb))
                                    g.wait_ge(ssem.h, ssem.n)
                                    g.indirect_dma_start(
                                        p.moe_out[:, :],
                                        IndirectOffsetOnAxis(
                                            idxs[:, iso + se * NTB + tb: iso + se * NTB + tb + 1], 0),
                                        y_sb[se % 2][:, tb * D:(tb + 1) * D], None,
                                        compute_op=ALU.add).then_inc(ssem.h, 16)
                                    ssem.n += 16
                                p.mo_sc_tot[(tag, c, se)] = (0, ssem.n)
                            if e < NE:
                                E = c * NE + e
                                gsem = p.gga[E % 2]
                                if e >= 2:
                                    g.wait_ge(ts.h, ts_tr(e - 2, NTR - 1))
                                    g.wait_ge(vs.h, vs_y(e - 2, NTB - 1))
                                for tb in range(NTB):
                                    g.indirect_dma_start(
                                        xg[e % 2][:, tb * 1032:(tb + 1) * 1032], None,
                                        p.xtm_scr[:, :],
                                        IndirectOffsetOnAxis(
                                            idxs[:, iso + e * NTB + tb: iso + e * NTB + tb + 1], 0)
                                        ).then_inc(gsem.h, 16)
                                    gsem.n += 16
                                p.mo_ga_tot[(tag, c, e)] = (E % 2, gsem.n)

                    @block.tensor
                    def _(t):
                        for e in range(NE):
                            gsl, gtot = p.mo_ga_tot[(tag, c, e)]
                            t.wait_ge(p.gga[gsl].h, gtot)
                            for j in range(NTR):
                                tb, k = divmod(j, 8)
                                jj = EST * e + j
                                if jj >= 2:
                                    pe, pj = divmod(jj - 2, EST)
                                    if pj < NTR:
                                        t.wait_ge(ss.h, ss_tr(pe, pj))
                                t.transpose(ptT[jj % 2][:, :],
                                            xg[e % 2][:, tb * 1032 + k * 128: tb * 1032 + (k + 1) * 128],
                                            p.ident[:, :]).then_inc(ts.h, 1)
                                ts.n += 1
                            for mq in range(4):
                                t.wait_ge(ld[(e * 4 + mq) % 2 + 2].h, qtot[(c, e, mq)])
                                for mtl in range(8):
                                    g2 = mq * 8 + mtl
                                    if e * 32 + g2 >= 2:
                                        pe, pg = (e, g2 - 2) if g2 >= 2 else (e - 1, 30 + g2)
                                        t.wait_ge(ss.h, ss_gelu(pe, pg))
                                    if g2 == 0:
                                        t.wait_ge(ss.h, ss_tr(e, NTR - 1))
                                    for k in range(ND):
                                        mm = t.matmul(pA[g2 % 2][:, 0:CAP],
                                                      wq[(e * 4 + mq) % 2][:, k * D + mtl * 128:
                                                                           k * D + (mtl + 1) * 128],
                                                      xgT[e % 2][:, k * CAP:(k + 1) * CAP],
                                                      start=(k == 0), stop=(k == ND - 1))
                                    mm.then_inc(ts.h, 1); ts.n += 1
                            t.wait_ge(ld[4].h, wotot[e])
                            t.wait_ge(ss.h, ss_gelu(e, 31))
                            for tb in range(NTB):
                                if tb >= 2:
                                    t.wait_ge(vs.h, vs_y(e, tb - 2))
                                elif e >= 1:
                                    t.wait_ge(vs.h, vs_y(e - 1, NTB - 2 + tb) if NTB >= 2
                                              else vs_y(e - 1, tb))
                                for dh in range(2):
                                    mm = t.matmul(pB[tb % 2][:, dh * 512:(dh + 1) * 512],
                                                  p.ones_row[0:1, :],
                                                  p.bout_bf[0:1, e * D + dh * 512: e * D + dh * 512 + 512],
                                                  start=True, stop=False)
                                    for mt in range(NMT):
                                        mm = t.matmul(pB[tb % 2][:, dh * 512:(dh + 1) * 512],
                                                      h_sb[:, mt * CAP + tb * 128: mt * CAP + (tb + 1) * 128],
                                                      wo_sb[:, mt * D + dh * 512: mt * D + dh * 512 + 512],
                                                      start=False, stop=(mt == NMT - 1))
                                mm.then_inc(ts.h, 1); ts.n += 1

                    @block.scalar
                    def _(sc):
                        for e in range(NE):
                            for j in range(NTR):
                                tb, k = divmod(j, 8)
                                jj = EST * e + j
                                w = [(ts, ts_tr(e, j))]
                                if j == 0 and e >= 1:
                                    w.append((ts, ts_A(e - 1, 31)))
                                sop(sc, ss, lambda jj=jj, tb=tb, k=k, e=e: sc.copy(
                                    xgT[e % 2][:, k * CAP + tb * 128: k * CAP + (tb + 1) * 128],
                                    ptT[jj % 2][:, :]), waits=w)
                            for g2 in range(32):
                                sop(sc, ss, lambda e=e, g2=g2: sc.activation(
                                    h_sb[:, g2 * CAP:(g2 + 1) * CAP], pA[g2 % 2][:, 0:CAP], AF.Gelu,
                                    bias=p.bin_cols[:, e * 32 + g2: e * 32 + g2 + 1]),
                                    waits=[(ts, ts_A(e, g2))])

                    @block.vector
                    def _(v):
                        for e in range(NE):
                            gsl, gtot = p.mo_ga_tot[(tag, c, e)]
                            for tb in range(NTB):
                                sop(v, vs, lambda e=e, tb=tb: v.tensor_copy(
                                    ccol[:, tb:tb + 1],
                                    xg[e % 2][:, tb * 1032 + D + e: tb * 1032 + D + e + 1]),
                                    waits=[(p.gga[gsl], gtot)] if tb == 0 else [])
                                w = [(ts, ts_B(e, tb))]
                                E = c * NE + e
                                if E >= 1:
                                    pc, pe = divmod(E - 1, NE)
                                    ssl, stot = p.mo_sc_tot[(tag, pc, pe)]
                                    w.append((p.gsc[ssl], stot))
                                sop(v, vs, lambda e=e, tb=tb: v.tensor_scalar(
                                    y_sb[e % 2][:, tb * D:(tb + 1) * D], pB[tb % 2][:, :],
                                    ccol[:, tb:tb + 1], None,
                                    op0=ALU.mult), waits=w)

                assert ts.n == ts0 + EST * NE
                assert ss.n == ss0 + SSE * NE
                assert vs.n == vs0 + 2 * NTB * NE

            with nc.Block() as block:
                @block.sync
                def _(sync):
                    for _s in p.gsc:
                        sync.wait_ge(_s.h, _s.n)
                    sync.dma_start(rd[(NRC - 1) * RC:NRC * RC, :],
                                   p.moe_out[(NRC - 1) * RC:NRC * RC, :]).then_inc(st[3].h, 16)
                    st[3].n += 16
            drain_dmas(p)


def build_attn(p, isrc, idst):
    nc = p.nc
    B, T, NTOK = p.B, p.T, p.NTOK
    Tc = min(512, NTOK)
    ntt = Tc // 128
    ND = D // 128
    NTB = T // 128
    NH = 16
    HD = 64
    Wq = min(512, T)
    NQC = T // Wq
    nwt = Wq // 128
    rs, rd = p.resid[isrc], p.resid[idst]
    ts, vs, ss = p.ts, p.vs, p.ss
    ld, st = p.ld, p.st

    with ExitStack() as stage:
        sen = stage.enter_context
        v_sb = sen(nc.sbuf_tensor("at_v", [128, NTB * D], BF16))
        attnT = sen(nc.sbuf_tensor("at_attnT", [128, 8 * T], BF16))

        with ExitStack() as ctx:
            en = ctx.enter_context
            xch = [en(nc.sbuf_tensor(f"at_xch{i}", [128, ntt * D], F32)) for i in range(2)]
            xlnT = [en(nc.sbuf_tensor(f"at_xlnT{i}", [128, ND * Tc], BF16)) for i in range(2)]
            xtm = [en(nc.sbuf_tensor(f"at_xtm{i}", [128, D], BF16)) for i in range(2)]
            junk = [en(nc.sbuf_tensor("at_junk", [128, D], F32))] * 2
            stat = en(nc.sbuf_tensor("at_stat", [128, 5 * ntt], F32))
            qbuf = en(nc.sbuf_tensor("at_qbuf", [128, ND * Tc], BF16))
            kbuf = en(nc.sbuf_tensor("at_kbuf", [128, ND * Tc], BF16))
            wqk = en(nc.sbuf_tensor("at_wqk", [128, ND * 2 * D], BF16))
            wv = en(nc.sbuf_tensor("at_wv", [128, ND * D], BF16))
            ptT = [en(nc.psum_tensor(f"at_ptT{i}", [128, 128], BF16)) for i in range(2)]
            pQK = [en(nc.psum_tensor(f"at_pQK{i}", [128, 512], F32)) for i in range(2)]
            pV = [en(nc.psum_tensor(f"at_pV{i}", [128, D], F32)) for i in range(2)]
            NCHG = NTOK // Tc

            p.at_xch_ld = {}
            _lc = [ld[0].n, ld[1].n]
            for _c in range(NCHG):
                _lc[_c % 2] += 16
                p.at_xch_ld[_c] = _lc[_c % 2]

            R = {"xch": xch, "xlnT": xlnT, "xtm": xtm, "junk": junk, "stat": stat, "ptT": ptT,
                 "ntt": ntt, "xld": lambda c: (ld[c % 2], p.at_xch_ld[c])}

            def a1_load(sync, c):
                s2 = c % 2
                sync.wait_ge(st[s2].h, st[s2].n)
                sync.dma_start(xch[s2][:, :].rearrange("q (tt d) -> q tt d", tt=ntt),
                               rs[c * Tc:(c + 1) * Tc, :].rearrange("(tt q) d -> q tt d", q=128)
                               ).then_inc(ld[s2].h, 16)
                ld[s2].n += 16
                assert p.at_xch_ld[c] == ld[s2].n

            with nc.Block() as block:
                @block.sync
                def _(sync):
                    sync.wait_ge(p.gc3.h, p.cast_attn_tot)
                    sync.dma_start(wqk[:, :].rearrange("q (dt f) -> q dt f", dt=ND),
                                   p.qkv_bf[:, 0:2 * D].rearrange("(dt q) f -> q dt f", q=128)
                                   ).then_inc(ld[6].h, 16)
                    ld[6].n += 16
                    sync.dma_start(wv[:, :].rearrange("q (dt f) -> q dt f", dt=ND),
                                   p.qkv_bf[:, 2 * D:3 * D].rearrange("(dt q) f -> q dt f", q=128)
                                   ).then_inc(ld[6].h, 16)
                    ld[6].n += 16
                    a1_load(sync, 0)

            for c_glob in range(NCHG):
                b = (c_glob * Tc) // T
                t0 = c_glob * Tc - b * T
                s = c_glob % 2
                with nc.Block() as block:
                    ln0_ss_end = None
                    if c_glob == 0:
                        emit_ln(p, block, R, 0)
                        ln0_ss_end = ss.n

                    ts0, vs0, ss0 = ts.n, vs.n, ss.n
                    TS_QK0 = ts0
                    TS_V0 = TS_QK0 + 2 * ND
                    VS_E0 = vs0

                    @block.vector
                    def _(v):
                        for fq in range(ND):
                            sop(v, vs, lambda fq=fq: v.tensor_copy(
                                qbuf[:, fq * Tc:(fq + 1) * Tc], pQK[0][:, 0:Tc]),
                                waits=[(ts, TS_QK0 + 2 * fq + 1),
                                       (st[3], p.at_qst.get(c_glob - 1, 0))])
                            sop(v, vs, lambda fq=fq: v.tensor_copy(
                                kbuf[:, fq * Tc:(fq + 1) * Tc], pQK[1][:, 0:Tc]),
                                waits=[(ts, TS_QK0 + 2 * fq + 2)])
                        for tb in range(ntt):
                            sop(v, vs, lambda tb=tb: v.tensor_copy(
                                v_sb[:, (t0 // 128 + tb) * D:(t0 // 128 + tb + 1) * D],
                                pV[tb % 2][:, :]),
                                waits=[(ts, TS_V0 + tb + 1)])

                    @block.tensor
                    def _(t):
                        t.wait_ge(ld[6].h, ld[6].n)
                        if ln0_ss_end is not None:
                            t.wait_ge(ss.h, ln0_ss_end)
                        for fq in range(ND):
                            for qk in range(2):
                                g = 2 * fq + qk
                                if g >= 2:
                                    t.wait_ge(vs.h, VS_E0 + g - 1)
                                for k in range(ND):
                                    mm = t.matmul(pQK[g % 2][:, 0:Tc],
                                                  wqk[:, k * 2 * D + qk * D + fq * 128:
                                                      k * 2 * D + qk * D + (fq + 1) * 128],
                                                  xlnT[s][:, k * Tc:(k + 1) * Tc],
                                                  start=(k == 0), stop=(k == ND - 1))
                                mm.then_inc(ts.h, 1); ts.n += 1
                        for tb in range(ntt):
                            if tb >= 2:
                                t.wait_ge(vs.h, VS_E0 + 2 * ND + (tb - 2) + 1)
                            for k in range(ND):
                                for dh in range(2):
                                    mm = t.matmul(pV[tb % 2][:, dh * 512:(dh + 1) * 512],
                                                  xlnT[s][:, k * Tc + tb * 128: k * Tc + (tb + 1) * 128],
                                                  wv[:, k * D + dh * 512: k * D + dh * 512 + 512],
                                                  start=(k == 0), stop=(k == ND - 1))
                            mm.then_inc(ts.h, 1); ts.n += 1

                    assert ts.n == TS_V0 + ntt
                    assert vs.n == VS_E0 + 2 * ND + ntt

                    if c_glob + 1 < NCHG:
                        emit_ln(p, block, R, c_glob + 1, guard=(c_glob == 0))

                    @block.sync
                    def _(sync):
                        sync.wait_ge(vs.h, VS_E0 + 2 * ND)
                        sync.dma_start(
                            p.qT_scr[b][:, t0:t0 + Tc].rearrange("(ft q) t -> q ft t", q=128),
                            qbuf[:, :].rearrange("q (ft t) -> q ft t", ft=ND)).then_inc(st[3].h, 16)
                        st[3].n += 16
                        sync.dma_start(
                            p.kT_scr[b][:, t0:t0 + Tc].rearrange("(ft q) t -> q ft t", q=128),
                            kbuf[:, :].rearrange("q (ft t) -> q ft t", ft=ND)).then_inc(st[3].h, 16)
                        st[3].n += 16
                        if c_glob + 1 < NCHG:
                            a1_load(sync, c_glob + 1)
                p.at_qst[c_glob] = st[3].n
            drain_dmas(p)

        for b in range(B):
            with ExitStack() as ctx:
                en = ctx.enter_context
                qh = [en(nc.sbuf_tensor(f"at_qh{b}{i}", [128, Wq], BF16)) for i in range(2)]
                kh = [en(nc.sbuf_tensor(f"at_kh{b}{i}", [128, T], BF16)) for i in range(2)]
                esb = [en(nc.sbuf_tensor(f"at_esb{b}{i}", [128, 2 * Wq], BF16)) for i in range(4)]
                rsb = en(nc.sbuf_tensor(f"at_rsb{b}", [64, Wq], F32))
                pS = [en(nc.psum_tensor(f"at_pS{b}{i}", [128, 2 * Wq], F32)) for i in range(3)]
                pSum = en(nc.psum_tensor(f"at_pSum{b}", [64, Wq], F32))
                pPV = en(nc.psum_tensor(f"at_pPV{b}", [64, Wq], F32))

                ts0, vs0, ss0 = ts.n, vs.n, ss.n
                ld2_0 = [ld[2].n, ld[3].n]
                nk_of = [(qc + 1) * nwt for qc in range(NQC)]
                SEGS = [(h, qc) for h in range(NH) for qc in range(NQC)]
                PAIRS = []
                seg_first_pg = {}
                for (h, qc) in SEGS:
                    seg_first_pg[(h, qc)] = len(PAIRS)
                    for prl in range(nk_of[qc] // 2):
                        PAIRS.append((h, qc, [2 * prl, 2 * prl + 1]))
                NP = len(PAIRS)
                vs_seg = {}
                _v = 0
                for si, (h, qc) in enumerate(SEGS):
                    vs_seg[(h, qc)] = _v
                    _v += nwt + 2
                VS_TOT = _v
                tpos_S = {}
                tpos_PV = {}
                LAG = 2
                send_ts = {}
                for si, seg in enumerate(SEGS):
                    nxt = seg_first_pg[SEGS[si + 1]] if si + 1 < len(SEGS) else NP
                    ilast = nxt - 1
                    send_ts[seg] = ts0 + 2 * (ilast + 1) + 2 * max(0, ilast - LAG)

                with nc.Block() as block:
                    @block.sync
                    def _(sync):
                        gqc = 0
                        for h in range(NH):
                            pb = (h % 2) * 64
                            if h == 0:
                                sync.wait_ge(st[3].h, st[3].n)
                                sync.dma_start(kh[0][pb:pb + 64, :],
                                               p.kT_scr[b][0:64, :]).then_inc(ld[4].h, 16)
                                ld[4].n += 16
                                p.at_khld[(b, 0)] = (4, ld[4].n)
                            if h + 1 < NH:
                                pb2 = ((h + 1) % 2) * 64
                                sm = ld[4 + (h + 1) % 2]
                                if h >= 1:
                                    sync.wait_ge(ts.h, send_ts[(h - 1, NQC - 1)])
                                sync.dma_start(kh[(h + 1) % 2][pb2:pb2 + 64, :],
                                               p.kT_scr[b][(h + 1) * 64:(h + 2) * 64, :]
                                               ).then_inc(sm.h, 16)
                                sm.n += 16
                                p.at_khld[(b, h + 1)] = (4 + (h + 1) % 2, sm.n)
                            for qc in range(NQC):
                                sl = gqc % 2
                                if gqc >= 2:
                                    sync.wait_ge(ts.h, send_ts[SEGS[gqc - 2]])
                                sync.dma_start(
                                    qh[sl][pb:pb + 64, :],
                                    p.qT_scr[b][h * 64:(h + 1) * 64, qc * Wq:(qc + 1) * Wq]
                                    ).then_inc(ld[2 + sl].h, 16)
                                ld[2 + sl].n += 16
                                gqc += 1

                    @block.tensor
                    def _(t):
                        for i in range(NP + LAG):
                            if i < NP:
                                pg = i
                                h, qc, kts = PAIRS[pg]
                                pb = (h % 2) * 64
                                gqc = SEGS.index((h, qc))
                                if pg == seg_first_pg[(h, qc)]:
                                    t.wait_ge(ld[2 + gqc % 2].h,
                                              ld2_0[gqc % 2] + 16 * (gqc // 2 + 1))
                                    if qc == 0:
                                        khs, khn = p.at_khld[(b, h)]
                                        t.wait_ge(ld[khs].h, khn)
                                if pg >= 3:
                                    t.wait_ge(ss.h, ss0 + pg - 2)
                                for half in range(2):
                                    kt = kts[half]
                                    mm = t.matmul(pS[pg % 3][:, half * Wq:(half + 1) * Wq],
                                                  kh[h % 2][pb:pb + 64, kt * 128:(kt + 1) * 128],
                                                  qh[(gqc % 2)][pb:pb + 64, :],
                                                  start=True, stop=True)
                                    mm.then_inc(ts.h, 1); ts.n += 1
                                    tpos_S[(h, qc, kt)] = ts.n
                            if i >= LAG:
                                pg2 = i - LAG
                                h2, qc2, kts2 = PAIRS[pg2]
                                nk2 = nk_of[qc2]
                                if pg2 == seg_first_pg[(h2, qc2)]:
                                    si2 = SEGS.index((h2, qc2))
                                    if si2 >= 1:
                                        t.wait_ge(vs.h, vs0 + vs_seg[(h2, qc2)])
                                t.wait_ge(ss.h, ss0 + pg2 + 1)
                                for half in range(2):
                                    kt = kts2[half]
                                    t.matmul(pSum[0:64, :], p.ones_bf[:, 0:64],
                                             esb[pg2 % 4][:, half * Wq:(half + 1) * Wq],
                                             start=(kt == 0), stop=(kt == nk2 - 1))
                                    mm = t.matmul(pPV[0:64, :],
                                                  v_sb[:, kt * D + h2 * 64: kt * D + h2 * 64 + 64],
                                                  esb[pg2 % 4][:, half * Wq:(half + 1) * Wq],
                                                  start=(kt == 0), stop=(kt == nk2 - 1))
                                    mm.then_inc(ts.h, 1); ts.n += 1
                                    tpos_PV[(h2, qc2, kt)] = ts.n

                    @block.scalar
                    def _(sc):
                        for pg in range(NP):
                            h, qc, kts = PAIRS[pg]
                            w = [(ts, tpos_S[(h, qc, kts[1])])]
                            for kt in kts:
                                mp = kt - qc * nwt
                                if mp >= 0:
                                    w.append((vs, vs0 + vs_seg[(h, qc)] + mp + 1))
                            if pg >= 4:
                                h4, qc4, kts4 = PAIRS[pg - 4]
                                w.append((ts, tpos_PV[(h4, qc4, kts4[1])]))
                            sop(sc, ss, lambda pg=pg: sc.activation(
                                esb[pg % 4][:, :], pS[pg % 3][:, :], AF.Exp, scale=0.125),
                                waits=w)

                    @block.vector
                    def _(v):
                        for (h, qc) in SEGS:
                            pb = (h % 2) * 64
                            fg = h // 2
                            for mp in range(nwt):
                                kt = qc * nwt + mp
                                pg = seg_first_pg[(h, qc)] + kt // 2
                                sop(v, vs, lambda pg=pg, kt=kt, mp=mp: v.tensor_tensor(
                                    pS[pg % 3][:, (kt % 2) * Wq:(kt % 2 + 1) * Wq],
                                    pS[pg % 3][:, (kt % 2) * Wq:(kt % 2 + 1) * Wq],
                                    p.masks[:, mp * 512: mp * 512 + Wq], op=ALU.add),
                                    waits=[(ts, tpos_S[(h, qc, kt)])])
                            sop(v, vs, lambda: v.reciprocal(rsb[0:64, :], pSum[0:64, :]),
                                waits=[(ts, tpos_PV[(h, qc, nk_of[qc] - 1)])])
                            sop(v, vs, lambda h=h, qc=qc, pb=pb, fg=fg: v.tensor_tensor(
                                attnT[pb:pb + 64, fg * T + qc * Wq: fg * T + (qc + 1) * Wq],
                                pPV[0:64, :], rsb[0:64, :], op=ALU.mult))

                assert ss.n == ss0 + NP
                assert vs.n == vs0 + VS_TOT


            with ExitStack() as ctx:
                en = ctx.enter_context
                xt = [en(nc.sbuf_tensor(f"at_xt{b}{i}", [128, D], F32)) for i in range(2)]
                wo2 = en(nc.sbuf_tensor(f"at_wo2{b}", [128, ND * D], BF16))
                pO = [en(nc.psum_tensor(f"at_pO{b}{i}", [128, D], F32)) for i in range(2)]
                ts0, vs0 = ts.n, vs.n
                ld0s = [ld[0].n, ld[1].n]
                st0s = [st[0].n, st[1].n]
                with nc.Block() as block:
                    @block.sync
                    def _(sync):
                        sync.dma_start(wo2[:, :].rearrange("q (dt f) -> q dt f", dt=ND),
                                       p.wo_bf.rearrange("(dt q) f -> q dt f", q=128)
                                       ).then_inc(ld[6].h, 16)
                        ld[6].n += 16

                    @block.tensor
                    def _(t):
                        t.wait_ge(ld[6].h, ld[6].n)
                        for tt in range(NTB):
                            if tt >= 2:
                                t.wait_ge(vs.h, vs0 + (tt - 2) + 1)
                            for ft in range(ND):
                                for dh in range(2):
                                    mm = t.matmul(pO[tt % 2][:, dh * 512:(dh + 1) * 512],
                                                  attnT[:, ft * T + tt * 128: ft * T + (tt + 1) * 128],
                                                  wo2[:, ft * D + dh * 512: ft * D + dh * 512 + 512],
                                                  start=(ft == 0), stop=(ft == ND - 1))
                            mm.then_inc(ts.h, 1); ts.n += 1

                    @block.vector
                    def _(v):
                        for tt in range(NTB):
                            sl = tt % 2
                            sop(v, vs, lambda tt=tt, sl=sl: v.scalar_tensor_tensor(
                                xt[sl][:, :], pO[tt % 2][:, :], 1.0, xt[sl][:, :],
                                op0=ALU.mult, op1=ALU.add),
                                waits=[(ts, ts0 + tt + 1),
                                       (ld[sl], ld0s[sl] + 16 * (tt // 2 + 1))])

                    @block.sync
                    def _(sync):
                        for tt in range(NTB):
                            sl = tt % 2
                            if tt >= 2:
                                sync.wait_ge(vs.h, vs0 + (tt - 2) + 1)
                                sync.wait_ge(st[sl].h, st0s[sl] + 16 * (tt // 2))
                            sync.dma_start(
                                xt[sl][:, :],
                                rs[b * T + tt * 128: b * T + (tt + 1) * 128, :]).then_inc(ld[sl].h, 16)
                            ld[sl].n += 16
                            if tt >= 1:
                                psl = (tt - 1) % 2
                                sync.wait_ge(vs.h, vs0 + tt)
                                sync.dma_start(rd[b * T + (tt - 1) * 128: b * T + tt * 128, :],
                                               xt[psl][:, :]).then_inc(st[psl].h, 16)
                                st[psl].n += 16
                        sync.wait_ge(vs.h, vs0 + NTB)
                        lsl = (NTB - 1) % 2
                        sync.dma_start(rd[b * T + (NTB - 1) * 128: b * T + NTB * 128, :],
                                       xt[lsl][:, :]).then_inc(st[lsl].h, 16)
                        st[lsl].n += 16
        drain_dmas(p)



def kernel(**inputs):
    cfg = cfg_full()
    nc = build_program(cfg)
    x = np.ascontiguousarray(np.asarray(inputs["L0_attn_x"]).reshape(16 * 2048, 1024))
    conv_w = np.ascontiguousarray(np.asarray(inputs["L0_attn_conv_w"]).reshape(DI, KCONV).T)
    names = {
        "L0_attn_in_proj_w": inputs["L0_attn_in_proj_w"],
        "L0_attn_conv_w": conv_w,
        "L0_attn_conv_b": inputs["L0_attn_conv_b"],
        "L0_attn_x_proj_w": inputs["L0_attn_x_proj_w"],
        "L0_attn_dt_proj_w": inputs["L0_attn_dt_proj_w"],
        "L0_attn_dt_proj_b": inputs["L0_attn_dt_proj_b"],
        "L0_attn_A_log": inputs["L0_attn_A_log"],
        "L0_attn_D": inputs["L0_attn_D"],
        "L0_attn_out_proj_w": inputs["L0_attn_out_proj_w"],
        "L0_ffn_w_gate": inputs["L0_ffn_w_gate"],
        "L0_ffn_w_up": inputs["L0_ffn_w_up"],
        "L0_ffn_w_down": inputs["L0_ffn_w_down"],
        "L1_attn_w_qkv": inputs["L1_attn_w_qkv"],
        "L1_attn_w_out": inputs["L1_attn_w_out"],
        "L1_ffn_W_gate": inputs["L1_ffn_W_gate"],
        "L1_ffn_W_in": inputs["L1_ffn_W_in"],
        "L1_ffn_b_in": inputs["L1_ffn_b_in"],
        "L1_ffn_W_out": inputs["L1_ffn_W_out"],
        "L1_ffn_b_out": inputs["L1_ffn_b_out"],
    }
    names = {k: np.ascontiguousarray(np.asarray(v)) for k, v in names.items()}
    in_maps = []
    ntok = cfg["B"] * cfg["T"]
    for c in range(8):
        m = dict(names)
        m["x0"] = x[c * ntok:(c + 1) * ntok]
        in_maps.append(m)
    res = run_bass_kernel_spmd(nc, in_maps, core_ids=list(range(8)))
    out = np.concatenate([np.asarray(r["y"]) for r in res.results], axis=0)
    return out.reshape(16, 2048, 1024).astype(np.float32)
